# version 04
# baseline (speedup 1.0000x reference)
import numpy as np
import concourse.bass as bass
import concourse.mybir as mybir
from concourse.bass_utils import run_bass_kernel_spmd

F32 = mybir.dt.float32
BF16 = mybir.dt.bfloat16
I32 = mybir.dt.int32
AF = mybir.ActivationFunctionType
ALU = mybir.AluOpType

ENGINES = ("sync", "tensor", "vector", "scalar", "gpsimd")

P = 128
D = 1024
DC = D // P
TK = 2048
NT4 = TK // 512
H = 4
SC = TK // P



class Op:
    __slots__ = ("eng", "emit", "reads", "writes", "kind", "idx", "deps",
                 "published", "tag")

    def __init__(self, eng, emit, reads, writes, kind, idx, tag):
        self.eng, self.emit = eng, emit
        self.reads, self.writes = reads, writes
        self.kind = kind
        self.idx = idx
        self.deps = []
        self.published = kind in ("dma", "cc")
        self.tag = tag


class Prog:
    def __init__(self):
        self.streams = {e: [] for e in ENGINES}
        self.res = {}

    def op(self, eng, emit, reads=(), writes=(), kind="cmp", tag=""):
        o = Op(eng, emit, list(reads), list(writes), kind,
               len(self.streams[eng]), tag)
        wdeps = []
        rdeps = []
        for (name, key) in o.reads:
            ent = self.res.get(name)
            if ent:
                ks = list(ent.keys()) if key == "*" else [key, "*"]
                for k in ks:
                    if k in ent:
                        wdeps.append(ent[k][0])
        for (name, key) in o.writes:
            ent = self.res.get(name)
            if ent:
                ks = list(ent.keys()) if key == "*" else [key, "*"]
                for k in ks:
                    if k in ent:
                        w, readers = ent[k]
                        wdeps.append(w)
                        rdeps.extend(readers)
        seen = set()
        for p in wdeps:
            if p is None or p is o or id(p) in seen:
                continue
            seen.add(id(p))
            if (p.eng == o.eng and p.kind == "cmp" and o.eng == "tensor"):
                continue
            o.deps.append(p)
            p.published = True
        for p in rdeps:
            if p is None or p is o or id(p) in seen:
                continue
            seen.add(id(p))
            if (p.eng == o.eng and p.kind == "cmp" and o.eng == "tensor"):
                continue
            o.deps.append(p)
            p.published = True
        for (name, key) in o.reads:
            ent = self.res.setdefault(name, {})
            ks = (set(ent.keys()) | {"*"}) if key == "*" else {key}
            for k in ks:
                if k not in ent:
                    ent[k] = (None, [])
                ent[k][1].append(o)
        for (name, key) in o.writes:
            ent = self.res.setdefault(name, {})
            if key == "*":
                ent.clear()
            ent[key] = (o, [])
        self.streams[eng].append(o)
        return o

    def finalize(self):
        self.pubnum = {}
        for e in ENGINES:
            counts = {"cmp": 0, "dma": 0, "cc": 0}
            for o in self.streams[e]:
                if o.published:
                    counts[o.kind] += 1
                    self.pubnum[id(o)] = counts[o.kind]

    def waits_of(self, o, clock):
        out = []
        for p in o.deps:
            num = self.pubnum[id(p)]
            val = num * (16 if p.kind == "dma" else 1)
            key = (p.eng, p.kind)
            if clock.get(key, 0) >= val:
                continue
            clock[key] = val
            out.append((p.eng, p.kind, val))
        return out




class KB:
    def __init__(self, debug_taps=()):
        self.nc = bass.Bass("TRN2", target_bir_lowering=False, debug=False)
        self.prog = Prog()
        self.dram = {}
        self.dram_h = {}
        self.sb_handles = []
        self.tensors = {}
        self.debug_taps = list(debug_taps)
        self._cmap = {}

    def din(self, name, shape, dtype=F32):
        ap = self.nc.dram_tensor(name, list(shape), dtype,
                                 kind="ExternalInput").ap()
        self.dram[name] = ap
        return ap

    def dout(self, name, shape, dtype=F32):
        ap = self.nc.dram_tensor(name, list(shape), dtype,
                                 kind="ExternalOutput").ap()
        self.dram[name] = ap
        return ap

    def dscratch(self, name, shape, dtype=F32):
        h = self.nc.dram_tensor(name, list(shape), dtype)
        self.dram_h[name] = h
        self.dram[name] = h.ap()
        return h.ap()

    def sb(self, name, shape, dtype=F32):
        cm = self.nc.sbuf_tensor(name, list(shape), dtype)
        h = cm.__enter__()
        self.sb_handles.append(cm)
        self.tensors[name] = h
        return h

    def ps(self, name, shape, dtype=F32):
        cm = self.nc.psum_tensor(name, list(shape), dtype)
        h = cm.__enter__()
        self.sb_handles.append(cm)
        self.tensors[name] = h
        return h

    def cconst(self, val, dtype=F32):
        key = (float(val), dtype)
        if key not in self._cmap:
            name = f"const_{len(self._cmap)}"
            h = self.sb(name, [P, 1], dtype)
            self.memset(h[:, :], float(val), [(name, "*")], tag="const")
            self._cmap[key] = (h, name)
        h, name = self._cmap[key]
        return h[:, 0:1], (name, "*")

    def dma(self, dst, src, reads, writes, eng="sync", tag="dma"):
        self.prog.op(eng, lambda e: e.dma_start(out=dst, in_=src),
                     reads, writes, kind="dma", tag=tag)

    def mm(self, out, lhsT, rhs, reads, writes, start=True, stop=True,
           tag="mm"):
        self.prog.op("tensor",
                     lambda e: e.matmul(out, lhsT, rhs, start=start,
                                        stop=stop),
                     reads, writes, tag=tag)

    def transpose(self, out, in_, reads, writes, tag="tr"):
        ident = self.tensors["identity"]
        self.prog.op("tensor",
                     lambda e: e.transpose(out, in_, ident[:, :]),
                     list(reads) + [("identity", "*")], writes, tag=tag)

    def act(self, out, in_, func, reads, writes, bias=0.0, scale=1.0,
            tag="act"):
        self.prog.op("scalar",
                     lambda e: e.activation(out=out, in_=in_, func=func,
                                            bias=bias, scale=scale),
                     reads, writes, tag=tag)

    def tt(self, out, a, b, op, reads, writes, eng="vector", tag="tt"):
        self.prog.op(eng, lambda e: e.tensor_tensor(out, a, b, op),
                     reads, writes, tag=tag)

    def ts(self, out, a, s1, op0, reads, writes, s2=None, op1=None,
           eng="vector", tag="ts"):
        if s2 is None:
            self.prog.op(eng,
                         lambda e: e.tensor_single_scalar(out, a, s1, op0),
                         reads, writes, tag=tag)
        else:
            self.prog.op(eng,
                         lambda e: e.tensor_scalar(out, a, s1, s2, op0, op1),
                         reads, writes, tag=tag)

    def stt(self, out, in0, scalar, in1, op0, op1, reads, writes,
            eng="vector", tag="stt"):
        self.prog.op(eng,
                     lambda e: e.scalar_tensor_tensor(out, in0, scalar, in1,
                                                      op0, op1),
                     reads, writes, tag=tag)

    def copy(self, out, in_, reads, writes, eng="vector", tag="copy"):
        if eng == "scalar":
            self.prog.op(eng, lambda e: e.copy(out=out, in_=in_),
                         reads, writes, tag=tag)
        else:
            self.prog.op(eng, lambda e: e.tensor_copy(out, in_),
                         reads, writes, tag=tag)

    def memset(self, ap_, val, writes, eng="vector", tag="memset"):
        self.prog.op(eng, lambda e: e.memset(ap_, val), (), writes, tag=tag)

    def scan(self, out, d0, d1, init, reads, writes, tag="scan"):
        self.prog.op("vector",
                     lambda e: e.tensor_tensor_scan(out, d0, d1, init,
                                                    ALU.mult, ALU.add),
                     reads, writes, tag=tag)

    def reduce(self, out, in_, op, reads, writes, eng="vector", tag="red"):
        self.prog.op(eng,
                     lambda e: e.tensor_reduce(out, in_,
                                               mybir.AxisListType.X, op),
                     reads, writes, tag=tag)

    def recip(self, out, in_, reads, writes, tag="recip"):
        self.prog.op("vector", lambda e: e.reciprocal(out, in_),
                     reads, writes, tag=tag)

    def select(self, out, mask, t, f, reads, writes, tag="sel"):
        self.prog.op("vector", lambda e: e.select(out, mask, t, f),
                     reads, writes, tag=tag)

    def cc(self, cin_h, cout_h, reads, writes, tag="cc"):
        groups = [[0, 1], [2, 3], [4, 5], [6, 7]]
        self.prog.op("gpsimd",
                     lambda e: e.collective_compute(
                         "AllReduce", ALU.add, replica_groups=groups,
                         ins=[cin_h.ap().opt()], outs=[cout_h.ap().opt()]),
                     reads, writes, kind="cc", tag=tag)

    def tap(self, name, sb_ap, reads):
        if name not in self.debug_taps:
            return
        o = self.dout("dbg_" + name, list(sb_ap.shape), sb_ap.dtype)
        self.dma(o, sb_ap, reads, [("dbg_" + name, "*")], eng="scalar",
                 tag="tap")

    def emit(self):
        prog = self.prog
        prog.finalize()
        nc = self.nc
        sems = {}
        semcms = []
        for e in ENGINES:
            for kind in ("cmp", "dma", "cc"):
                if any(o.published and o.kind == kind
                       for o in prog.streams[e]):
                    cm = nc.semaphore(f"s_{e}_{kind}")
                    sems[(e, kind)] = cm.__enter__()
                    semcms.append(cm)
        blk_cm = nc.Block()
        block = blk_cm.__enter__()

        def make_runner(ename):
            stream = prog.streams[ename]

            def run(eng):
                clock = {}
                for o in stream:
                    if o.kind in ("dma", "cc"):
                        num = prog.pubnum[id(o)]
                        if num > 1:
                            amt = 16 if o.kind == "dma" else 1
                            key = (o.eng, o.kind)
                            val = (num - 1) * amt
                            if clock.get(key, 0) < val:
                                clock[key] = val
                                eng.wait_ge(sems[key], val)
                    for (pe, kind, val) in prog.waits_of(o, clock):
                        eng.wait_ge(sems[(pe, kind)], val)
                    inst = o.emit(eng)
                    if o.published:
                        amt = 16 if o.kind == "dma" else 1
                        inst.then_inc(sems[(o.eng, o.kind)], amt)
            return run

        block.sync(make_runner("sync"))
        block.tensor(make_runner("tensor"))
        block.vector(make_runner("vector"))
        block.scalar(make_runner("scalar"))
        block.gpsimd(make_runner("gpsimd"))
        blk_cm.__exit__(None, None, None)
        for cm in semcms:
            cm.__exit__(None, None, None)
        for cm in reversed(self.sb_handles):
            cm.__exit__(None, None, None)


def psl(kb, i):
    t = kb.tensors["pA"] if i < 4 else kb.tensors["pB"]
    nm = "pA" if i < 4 else "pB"
    j = i % 4
    return t[:, j * 512:(j + 1) * 512], nm, f"t{j}"




def ln_full(kb, src, src_name, xnb, xnb_name, mu_ap, mu_rd, rb_ap, rb_rd,
            eps=1e-5, nchunks=DC):
    T = kb.tensors
    ones_blk = T["ones_blk"]
    sc_a = T["sc_a"]
    for c in range(nchunks):
        sc = sc_a
        scn = "sc_a"
        kb.copy(sc[:, :], src[:, c, :], [(src_name, f"c{c}")], [(scn, "x")],
                eng="scalar", tag="ln_cast")
        for t in range(NT4):
            pap, pnm, pk = psl(kb, t)
            kb.mm(pap, ones_blk[:, :], sc[:, t * 512:(t + 1) * 512],
                  [("ones_blk", "*"), (scn, "x")], [(pnm, pk)],
                  start=(c == 0), stop=(c == nchunks - 1), tag="ln_sum")
        kb.act(sc[:, :], src[:, c, :], AF.Square, [(src_name, f"c{c}")],
               [(scn, "x")], tag="ln_sq")
        for t in range(NT4):
            pap, pnm, pk = psl(kb, 4 + t)
            kb.mm(pap, ones_blk[:, :], sc[:, t * 512:(t + 1) * 512],
                  [("ones_blk", "*"), (scn, "x")], [(pnm, pk)],
                  start=(c == 0), stop=(c == nchunks - 1), tag="ln_sqsum")
    nd = nchunks * P
    for t in range(NT4):
        ts_ = slice(t * 512, (t + 1) * 512)
        pap, pnm, pk = psl(kb, t)
        pbq, pnm2, pk2 = psl(kb, 4 + t)
        kb.ts(mu_ap[:, ts_], pap, 1.0 / nd, ALU.mult, [(pnm, pk)], [mu_rd],
              tag="ln_mu")
        kb.tt(rb_ap[:, ts_], mu_ap[:, ts_], mu_ap[:, ts_], ALU.mult,
              [mu_rd], [rb_rd], tag="ln_musq")
        kb.stt(rb_ap[:, ts_], pbq, 1.0 / nd, rb_ap[:, ts_], ALU.mult,
               ALU.subtract, [(pnm2, pk2), rb_rd], [rb_rd], tag="ln_var")
    eps_ap, eps_rd = kb.cconst(eps)
    kb.act(rb_ap[:, :], rb_ap[:, :], AF.Sqrt, [rb_rd, eps_rd], [rb_rd],
           bias=eps_ap, tag="ln_sqrt")
    kb.recip(rb_ap[:, :], rb_ap[:, :], [rb_rd], [rb_rd], tag="ln_recip")
    for c in range(nchunks):
        kb.tt(xnb[:, c, :], src[:, c, :], mu_ap[:, :], ALU.subtract,
              [(src_name, f"c{c}"), mu_rd], [(xnb_name, f"c{c}")],
              tag="ln_sub")
        kb.tt(xnb[:, c, :], xnb[:, c, :], rb_ap[:, :], ALU.mult,
              [(xnb_name, f"c{c}"), rb_rd], [(xnb_name, f"c{c}")],
              tag="ln_scale")


def stream_proj(kb, wname, w_hbm, nk, nm, rhs_fn, out_cb, extra_k=None,
                wscale=None, cast_eng="vector", bank0=0, tag="proj"):
    dp = kb.tensors["dpt"]
    wld2 = dp[:, 1152:1792].bitcast(F32)
    wcb = dp[:, 0:1152]
    DPW = ("dpt", "*")
    for m in range(nm):
        for half in range((nk + 1) // 2):
            kcs = list(range(2 * half, min(2 * half + 2, nk)))
            w = len(kcs) * 128
            kb.dma(wld2[:, 0:w].rearrange("p (k q) -> p k q", q=P),
                   w_hbm[m, kcs[0]:kcs[0] + len(kcs)].transpose([1, 0, 2]),
                   [(wname, "*")], [DPW], tag=tag + "_ld")
            if wscale is None:
                kb.copy(wcb[:, kcs[0] * 128:kcs[0] * 128 + w], wld2[:, 0:w],
                        [DPW], [DPW], eng="scalar", tag=tag + "_cast")
            else:
                fn, frd = wscale
                for j, k in enumerate(kcs):
                    kb.ts(wcb[:, k * 128:(k + 1) * 128],
                          wld2[:, j * 128:(j + 1) * 128], fn(k), ALU.mult,
                          [DPW, frd], [DPW], eng=cast_eng, tag=tag + "_cast")
        for t in range(NT4):
            pap, pnm, pk = psl(kb, bank0 + (m % 2) * 2 + (t % 2))
            for k in range(nk):
                rhs, rrd = rhs_fn(k, t)
                kb.mm(pap, wcb[:, k * 128:(k + 1) * 128], rhs,
                      [DPW, rrd], [(pnm, pk)],
                      start=(k == 0),
                      stop=(k == nk - 1 and extra_k is None),
                      tag=tag + "_mm")
            if extra_k is not None:
                extra_k(m, t, pap, (pnm, pk))
            out_cb(m, t, pap, (pnm, pk))




def build_l0_attn(kb, stop_after=None):
    T = kb.tensors
    dram = kb.dram
    xr = T["xr"]
    xrb = xr.ap().bitcast(BF16)
    bq, bk, vf, bo = T["bq"], T["bk"], T["vf"], T["bo"]
    sc_a = T["sc_a"]
    dp = T["dpt"]
    chsc = T["chsc"]
    ones_blk = T["ones_blk"]
    aux = T["aux_s"]
    PAR, NPAR = aux[:, 0:1], aux[:, 1:2]
    EPS5, EPS6 = aux[:, 3:4], aux[:, 4:5]
    AUX = ("aux_s", "*")

    def xnb_ap(c):
        return (xrb[:, c // 2, (c % 2) * TK:(c % 2 + 1) * TK],
                ("xr", f"c{c // 2}"))

    bo_flat = bo.ap().rearrange("p c t -> p (c t)").bitcast(F32)
    mu_ap = bo_flat[:, 0:TK]
    rb_ap = bo_flat[:, TK:2 * TK]
    nrmq = bo_flat[:, 2 * TK:3 * TK]
    nrmk = bo_flat[:, 3 * TK:4 * TK]
    MU = [("bo", "c0"), ("bo", "c1")]
    RB = [("bo", "c2"), ("bo", "c3")]
    NQ = [("bo", "c4"), ("bo", "c5")]
    NK_ = [("bo", "c6"), ("bo", "c7")]

    def slabf(s):
        return xr[:, s, :], ("xr", f"c{s}")
    betab, BB = slabf(4)
    ccs, CCS = slabf(5)
    sst, SST = slabf(6)
    rst, RST = slabf(7)

    for c in range(DC):
        kb.copy(sc_a[:, :], xr[:, c, :], [("xr", f"c{c}")], [("sc_a", "x")],
                eng="scalar", tag="ln_cast")
        for t in range(NT4):
            pap, pnm, pk = psl(kb, t)
            kb.mm(pap, ones_blk[:, :], sc_a[:, t * 512:(t + 1) * 512],
                  [("ones_blk", "*"), ("sc_a", "x")], [(pnm, pk)],
                  start=(c == 0), stop=(c == DC - 1), tag="ln_sum")
        kb.act(sc_a[:, :], xr[:, c, :], AF.Square, [("xr", f"c{c}")],
               [("sc_a", "x")], tag="ln_sq")
        for t in range(NT4):
            pap, pnm, pk = psl(kb, 4 + t)
            kb.mm(pap, ones_blk[:, :], sc_a[:, t * 512:(t + 1) * 512],
                  [("ones_blk", "*"), ("sc_a", "x")], [(pnm, pk)],
                  start=(c == 0), stop=(c == DC - 1), tag="ln_sqsum")
    for t in range(NT4):
        ts_ = slice(t * 512, (t + 1) * 512)
        pap, pnm, pk = psl(kb, t)
        pbq, pnm2, pk2 = psl(kb, 4 + t)
        kb.ts(mu_ap[:, ts_], pap, 1.0 / D, ALU.mult, [(pnm, pk)], MU,
              tag="ln_mu")
        kb.tt(rb_ap[:, ts_], mu_ap[:, ts_], mu_ap[:, ts_], ALU.mult, MU, RB,
              tag="ln_musq")
        kb.stt(rb_ap[:, ts_], pbq, 1.0 / D, rb_ap[:, ts_], ALU.mult,
               ALU.subtract, [(pnm2, pk2)] + RB, RB, tag="ln_var")
    kb.act(rb_ap[:, :], rb_ap[:, :], AF.Sqrt, RB + [AUX], RB, bias=EPS5,
           tag="ln_sqrt")
    kb.recip(rb_ap[:, :], rb_ap[:, :], RB, RB, tag="ln_recip")
    for c in range(DC):
        kb.tt(sc_a[:, :], xr[:, c, :], mu_ap[:, :], ALU.subtract,
              [("xr", f"c{c}")] + MU, [("sc_a", "x")], tag="ln_sub")
        kb.dma(dram["x_spill"][c], xr[:, c, :], [("xr", f"c{c}")],
               [("x_spill", f"c{c}")], eng="scalar", tag="spill")
        xnbc, xnbrd = xnb_ap(c)
        kb.tt(xnbc, sc_a[:, :], rb_ap[:, :], ALU.mult, [("sc_a", "x")] + RB,
              [xnbrd], tag="ln_scale")

    kb.tap("ln1", xrb[:, 0:4, :], [("xr", f"c{i}") for i in range(4)])
    kb.tap("rb", rb_ap[:, :], RB)
    if stop_after == "ln1":
        return

    def rhs_xnb(k, t):
        ap_, rd_ = xnb_ap(k)
        return ap_[:, t * 512:(t + 1) * 512], rd_

    def evac_ext(dst, dnm, off):
        def cb(m, t, pap, prd):
            kb.copy(dst[:, m, off + t * 512:off + (t + 1) * 512], pap,
                    [prd], [(dnm, f"c{m}")], tag="proj_ev")
        return cb

    if stop_after != "beta_only":
        stream_proj(kb, "wq", dram["wq"], DC, DC, rhs_xnb,
                    evac_ext(bq, "bq", 3), tag="wq")
        stream_proj(kb, "wk", dram["wk"], DC, DC, rhs_xnb,
                    evac_ext(bk, "bk", 3), tag="wk")
        stream_proj(kb, "wv", dram["wv"], DC, DC, rhs_xnb,
                    evac_ext(vf, "vf", 63), tag="wv")

    wld2 = dp[:, 1152:1792].bitcast(F32)
    wcb = dp[:, 0:1152]
    DPW = ("dpt", "*")
    kb.dma(wld2[:, 0:32].rearrange("p (k q) -> p k q", q=4),
           dram["wb"].transpose([1, 0, 2]), [("wb", "*")], [DPW],
           tag="wb_ld")
    kb.copy(wcb[:, 0:32], wld2[:, 0:32], [DPW], [DPW], tag="wb_cast")
    for t in range(NT4):
        pap, pnm, pk = psl(kb, 4 + (t % 2))
        for k in range(DC):
            rhs, rrd = rhs_xnb(k, t)
            kb.mm(pap[0:4, :], wcb[:, k * 4:(k + 1) * 4], rhs,
                  [DPW, rrd], [(pnm, pk)], start=(k == 0),
                  stop=(k == DC - 1), tag="wb_mm")
        kb.copy(sc_a[0:4, t * 512:(t + 1) * 512], pap[0:4, :],
                [(pnm, pk)], [("sc_a", "x")], tag="wb_ev")
    kb.tap("wbev", sc_a[0:4, :], [("sc_a", "x")])
    kb.act(chsc[0:4, :], sc_a[0:4, :], AF.Sigmoid, [("sc_a", "x")],
           [("chsc", "x")], tag="beta_sig")
    BETA = ("chsc", "x")
    kb.tap("beta", chsc[0:4, :], [BETA])
    if stop_after in ("beta", "beta_only"):
        return

    exts = [(bq, "bq", 3), (bk, "bk", 3), (vf, "vf", 63)]
    for i, (ext, enm, off) in enumerate(exts):
        for c in range(DC):
            kb.ts(ccs[:, c * 9 + i * 3:c * 9 + i * 3 + 3],
                  ext[:, c, off + TK - 3:off + TK], NPAR, ALU.mult,
                  [(enm, f"c{c}"), AUX], [CCS], tag="cc1_st")
    kb.dma(dram["cc1i"], ccs[:, 0:72], [CCS], [("cc1i", "*")], eng="scalar")
    kb.cc(kb.dram_h["cc1i"], kb.dram_h["cc1o"], [("cc1i", "*")],
          [("cc1o", "*")])
    kb.dma(ccs[:, 0:72], dram["cc1o"], [("cc1o", "*")], [CCS], eng="scalar")
    for i, (ext, enm, off) in enumerate(exts):
        for c in range(DC):
            kb.ts(ext[:, c, off - 3:off],
                  ccs[:, c * 9 + i * 3:c * 9 + i * 3 + 3], PAR, ALU.mult,
                  [CCS, AUX], [(enm, f"c{c}")], tag="cc1_pf")

    cacc = mu_ap
    CACC = MU
    for (ext, enm, off, taps, tnm) in [(bq, "bq", 3, T["cq"], "cq"),
                                       (bk, "bk", 3, T["ck"], "ck"),
                                       (vf, "vf", 63, T["cv"], "cv")]:
        for c in range(DC):
            kb.ts(cacc[:, :], ext[:, c, off:off + TK], taps[:, c, 3:4],
                  ALU.mult, [(enm, f"c{c}"), (tnm, "*")], CACC, tag="cv_t0")
            for tau in range(1, 4):
                kb.stt(cacc[:, :], ext[:, c, off - tau:off + TK - tau],
                       taps[:, c, 3 - tau:4 - tau], cacc[:, :], ALU.mult,
                       ALU.add, [(enm, f"c{c}"), (tnm, "*")] + CACC,
                       CACC, tag="cv_t")
            kb.act(ext[:, c, off:off + TK], cacc[:, :], AF.Silu,
                   CACC, [(enm, f"c{c}")], tag="cv_silu")

    kb.tap("qsilu", bq[:, :, 3:3 + TK], [("bq", f"c{c}") for c in range(DC)])
    kb.tap("vsilu", vf[:, :, 63:63 + TK], [("vf", f"c{c}") for c in range(DC)])

    for c in range(DC):
        kb.ts(ccs[:, c * 63:(c + 1) * 63], vf[:, c, TK:63 + TK], NPAR,
              ALU.mult, [("vf", f"c{c}"), AUX], [CCS], tag="cc2_st")
    kb.dma(dram["cc2i"], ccs[:, 0:DC * 63], [CCS], [("cc2i", "*")],
           eng="scalar")
    kb.cc(kb.dram_h["cc2i"], kb.dram_h["cc2o"], [("cc2i", "*")],
          [("cc2o", "*")])
    kb.dma(ccs[:, 0:DC * 63], dram["cc2o"], [("cc2o", "*")], [CCS],
           eng="scalar")
    for c in range(DC):
        kb.ts(vf[:, c, 0:63], ccs[:, c * 63:(c + 1) * 63], PAR, ALU.mult,
              [CCS, AUX], [("vf", f"c{c}")], tag="cc2_pf")

    for h in range(H):
        c0, c1 = 2 * h, 2 * h + 1
        for col, (ext, enm, nap, nrd) in enumerate(
                [(bq, "bq", nrmq, NQ), (bk, "bk", nrmk, NK_)]):
            for ci, c in enumerate((c0, c1)):
                kb.act(sc_a[:, :], ext[:, c, 3:3 + TK], AF.Square,
                       [(enm, f"c{c}")], [("sc_a", "x")], tag="l2_sq")
                for t in range(NT4):
                    pap, pnm, pk = psl(kb, col * 4 + t)
                    kb.mm(pap, ones_blk[:, :], sc_a[:, t * 512:(t + 1) * 512],
                          [("ones_blk", "*"), ("sc_a", "x")], [(pnm, pk)],
                          start=(ci == 0), stop=(ci == 1), tag="l2_mm")
            for t in range(NT4):
                pap, pnm, pk = psl(kb, col * 4 + t)
                kb.act(nap[:, t * 512:(t + 1) * 512], pap, AF.Sqrt,
                       [(pnm, pk), AUX], nrd, bias=EPS6, tag="l2_sqrt")
            kb.recip(nap[:, :], nap[:, :], nrd, nrd, tag="l2_recip")
        for ci, c in enumerate((c0, c1)):
            kb.tt(bq[:, c, 3:3 + TK], bq[:, c, 3:3 + TK], nrmq[:, :],
                  ALU.mult, [("bq", f"c{c}")] + NQ, [("bq", f"c{c}")],
                  tag="l2_apq")
            kb.tt(bk[:, c, 3:3 + TK], bk[:, c, 3:3 + TK], nrmk[:, :],
                  ALU.mult, [("bk", f"c{c}")] + NK_, [("bk", f"c{c}")],
                  tag="l2_apk")
    kb.tap("qn", bq[:, :, 3:3 + TK], [("bq", f"c{c}") for c in range(DC)])
    if stop_after == "qn":
        return

    DPo = {"Am": (0, 128), "Atm": (128, 128), "Pt": (256, 128),
           "Qt": (384, 128), "St": (512, 128), "vbt": (640, 256),
           "kbt": (896, 256), "knt": (1152, 256), "utm": (1408, 256),
           "wfm": (1664, 256), "uttm": (256, 256), "sbf": (640, 512),
           "rbfb": (640, 512), "dcol": (0, 128)}

    def dpt(nm):
        off, w = DPo[nm]
        base_slots = [0, 128, 256, 384, 512, 640, 896, 1152, 1408, 1664, 1920]
        rds = [("dpt", f"r{bs}") for bs in base_slots
               if bs < off + w and bs + (256 if bs >= 640 else 128) > off]
        return dp[:, off:off + w], rds

    sbf, SBF = dpt("sbf")
    rbfb, RBF = dpt("rbfb")
    ident = T["identity"]
    mask_sl, mask_iu = T["mask_sl_s"], T["mask_iu_s"]

    def q_fm(h, kc, s):
        c = 2 * h + kc
        return bq[:, c, 3 + s * P:3 + (s + 1) * P], ("bq", f"c{c}")

    def k_fm(h, kc, s):
        c = 2 * h + kc
        return bk[:, c, 3 + s * P:3 + (s + 1) * P], ("bk", f"c{c}")

    def v_fm(h, kc, s):
        c = 2 * h + kc
        return vf[:, c, 63 + s * P:63 + (s + 1) * P], ("vf", f"c{c}")

    kb.memset(sst[:, :], 0.0, [SST], tag="s0")

    def prep_phase(h, s):
        Am, Am_r = dpt("Am")
        Atm, Atm_r = dpt("Atm")
        Pt, Pt_r = dpt("Pt")
        Qt, Qt_r = dpt("Qt")
        St, St_r = dpt("St")
        vbt, vbt_r = dpt("vbt")
        kbt, kbt_r = dpt("kbt")
        knt, knt_r = dpt("knt")
        utm, utm_r = dpt("utm")
        wfm, wfm_r = dpt("wfm")
        b6, n6, k6 = psl(kb, 6)
        b7, n7, k7 = psl(kb, 7)
        b3, n3, k3 = psl(kb, 3)
        b4, n4, k4 = psl(kb, 4)
        b3b = b3.bitcast(BF16)
        kbet = []
        for kc in range(2):
            kf, krd = k_fm(h, kc, s)
            dst = sc_a[:, 1536 + kc * P:1536 + (kc + 1) * P]
            kb.tt(dst, kf, betab[:, s * P:(s + 1) * P], ALU.mult,
                  [krd, BB], [("sc_a", "x")], tag="d_kbet")
            kbet.append((dst, ("sc_a", "x")))
        for kc in range(2):
            kbe, kbe_r = kbet[kc]
            kf, krd = k_fm(h, kc, s)
            kb.mm(b6[:, 0:P], kbe, kf, [kbe_r, krd], [(n6, k6)],
                  start=(kc == 0), stop=(kc == 1), tag="d_A")
        kb.tt(Am[:, :], b6[:, 0:P], mask_sl[:, :], ALU.mult,
              [(n6, k6), ("mask_sl_s", "*")], Am_r, tag="d_Amask")
        kb.transpose(b6.bitcast(BF16)[:, P:2 * P], Am[:, :], Am_r,
                     [(n6, k6)], tag="d_Atr")
        kb.copy(Atm[:, :], b6.bitcast(BF16)[:, P:2 * P], [(n6, k6)], Atm_r,
                tag="d_Atev")
        kb.mm(b7[:, 0:P], Atm[:, :], Am[:, :], Atm_r + Am_r, [(n7, k7)],
              tag="d_P0")
        kb.mm(b7[:, P:2 * P], Am[:, :], Atm[:, :], Am_r + Atm_r, [(n7, k7)],
              tag="d_Q0")
        kb.copy(Pt[:, :], b7[:, 0:P], [(n7, k7)], Pt_r, tag="d_P0ev")
        kb.copy(Qt[:, :], b7[:, P:2 * P], [(n7, k7)], Qt_r, tag="d_Q0ev")
        kb.tt(St[:, :], ident[:, :], Atm[:, :], ALU.subtract,
              [("identity", "*")] + Atm_r, St_r, tag="d_S0")
        for it in range(6):
            kb.mm(b6[:, 0:P], Pt[:, :], St[:, :], Pt_r + St_r, [(n6, k6)],
                  tag="d_itS")
            if it < 5:
                kb.mm(b6[:, P:2 * P], Qt[:, :], Pt[:, :], Qt_r + Pt_r,
                      [(n6, k6)], tag="d_itP")
                kb.mm(b7[:, 0:P], Pt[:, :], Qt[:, :], Pt_r + Qt_r,
                      [(n7, k7)], tag="d_itQ")
            kb.tt(St[:, :], St[:, :], b6[:, 0:P], ALU.add,
                  St_r + [(n6, k6)], St_r, tag="d_itSev")
            if it < 5:
                kb.copy(Pt[:, :], b6[:, P:2 * P], [(n6, k6)], Pt_r,
                        tag="d_itPev")
                kb.copy(Qt[:, :], b7[:, 0:P], [(n7, k7)], Qt_r,
                        tag="d_itQev")
        for kc in range(2):
            vfa, vrd = v_fm(h, kc, s)
            dst = sc_a[:, 1024:1024 + P]
            kb.tt(dst, vfa, betab[:, s * P:(s + 1) * P], ALU.mult,
                  [vrd, BB], [("sc_a", "x")], tag="d_vb")
            kb.transpose(b3b[:, 0:P], dst, [("sc_a", "x")], [(n3, k3)],
                         tag="d_vbt")
            kb.copy(vbt[:, kc * P:(kc + 1) * P], b3b[:, 0:P], [(n3, k3)],
                    vbt_r, tag="d_vbtev")
            kbe, kbe_r = kbet[kc]
            kb.transpose(b3b[:, P:2 * P], kbe, [kbe_r], [(n3, k3)],
                         tag="d_kbt")
            kb.copy(kbt[:, kc * P:(kc + 1) * P], b3b[:, P:2 * P], [(n3, k3)],
                    kbt_r, tag="d_kbtev")
            kf, krd = k_fm(h, kc, s)
            kb.transpose(b3b[:, 2 * P:3 * P], kf, [krd], [(n3, k3)],
                         tag="d_knt")
            kb.copy(knt[:, kc * P:(kc + 1) * P], b3b[:, 2 * P:3 * P],
                    [(n3, k3)], knt_r, tag="d_kntev")
        kb.mm(b4[:, 0:256], St[:, :], vbt[:, :], St_r + vbt_r, [(n4, k4)],
              tag="d_u")
        kb.copy(utm[:, :], b4[:, 0:256], [(n4, k4)], utm_r, tag="d_uev")
        for half in range(2):
            kb.mm(b4[:, 256 + half * P:256 + (half + 1) * P],
                  kbt[:, half * P:(half + 1) * P], St[:, :],
                  kbt_r + St_r, [(n4, k4)], tag="d_wfm")
        kb.copy(wfm[:, :], b4[:, 256:512], [(n4, k4)], wfm_r, tag="d_wfmev")
        attnT, attnT_r = dpt("Am")
        for kc in range(2):
            kf, krd = k_fm(h, kc, s)
            qf, qrd = q_fm(h, kc, s)
            kb.mm(b6[:, 0:P], kf, qf, [krd, qrd], [(n6, k6)],
                  start=(kc == 0), stop=(kc == 1), tag="d_attn")
        kb.tt(attnT[:, :], b6[:, 0:P], mask_iu[:, :], ALU.mult,
              [(n6, k6), ("mask_iu_s", "*")], attnT_r, tag="d_attnm")
        u = s * H + h
        kb.dma(dram["w_spill"][u, :, 0:256], wfm[:, :], wfm_r,
               [("w_spill", f"u{u}")], eng="scalar", tag="wspill")
        kb.dma(dram["w_spill"][u, :, 256:512], knt[:, :], knt_r,
               [("w_spill", f"u{u}")], eng="scalar", tag="wspill")
        kb.dma(dram["w_spill"][u, :, 512:640], attnT[:, :], attnT_r,
               [("w_spill", f"u{u}")], eng="scalar", tag="wspill")

    def chain_phase(h, s):
        knt, knt_r = dpt("knt")
        utm, utm_r = dpt("utm")
        wfm, wfm_r = dpt("wfm")
        uttm, uttm_r = dpt("uttm")
        attnT, attnT_r = dpt("Am")
        b0, n0, k0 = psl(kb, 0)
        b1, n1, k1 = psl(kb, 1)
        b2, n2, k2 = psl(kb, 2)
        hs = slice(h * 512, (h + 1) * 512)
        if s > 0:
            kb.copy(sbf[:, :], sst[:, hs], [SST], SBF, tag="ch_sbf")
            for kc in range(2):
                kb.mm(b0[:, 0:256], wfm[:, kc * P:(kc + 1) * P],
                      sbf[:, kc * 256:(kc + 1) * 256], wfm_r + SBF,
                      [(n0, k0)], start=(kc == 0), stop=(kc == 1),
                      tag="ch_wS")
            kb.tt(uttm[:, :], utm[:, :], b0[:, 0:256], ALU.subtract,
                  utm_r + [(n0, k0)], uttm_r, tag="ch_ut")
        else:
            kb.copy(uttm[:, :], utm[:, :], utm_r, uttm_r, tag="ch_ut0")
        for half in range(2):
            reg = b1[:, half * P:(half + 1) * P]
            first = True
            if s > 0:
                for kc in range(2):
                    qf, qrd = q_fm(h, kc, s)
                    kb.mm(reg, sbf[:, kc * 256 + half * P:kc * 256 + (half + 1) * P],
                          qf, SBF + [qrd], [(n1, k1)],
                          start=first, stop=False, tag="ch_oq")
                    first = False
            kb.mm(reg, uttm[:, half * P:(half + 1) * P], attnT[:, :],
                  uttm_r + attnT_r, [(n1, k1)], start=first, stop=True,
                  tag="ch_oa")
        for half in range(2):
            c = 2 * h + half
            kb.copy(bo[:, c, s * P:(s + 1) * P],
                    b1[:, half * P:(half + 1) * P], [(n1, k1)],
                    [("bo", f"c{c}")], tag="ch_oev")
        for kc in range(2):
            kb.mm(b2[:, kc * 256:(kc + 1) * 256],
                  knt[:, kc * P:(kc + 1) * P], uttm[:, :],
                  knt_r + uttm_r, [(n2, k2)], tag="ch_Supd")
        kb.tt(sst[:, hs], sst[:, hs], b2[:, :], ALU.add, [SST, (n2, k2)],
              [SST], tag="ch_Sadd")

    for h in range(H):
        kb.dma(sc_a[0:1, 0:TK], chsc[h:h + 1, :], [BETA], [("sc_a", "x")],
               eng="scalar", tag="bb_row")
        for t in range(NT4):
            pap, pnm, pk = psl(kb, 5)
            kb.mm(pap, ones_blk[0:1, :], sc_a[0:1, t * 512:(t + 1) * 512],
                  [("ones_blk", "*"), ("sc_a", "x")], [(pnm, pk)],
                  tag="bb_mm")
            kb.copy(betab[:, t * 512:(t + 1) * 512], pap, [(pnm, pk)], [BB],
                    tag="bb_ev")
        for s in range(SC):
            prep_phase(h, s)
            chain_phase(h, s)

    kb.tap("delta_o", bo[:, :, :], [("bo", f"c{c}") for c in range(DC)])
    if stop_after == "delta_o":
        return

    kb.ts(ccs[:, :], sst[:, :], NPAR, ALU.mult, [SST, AUX], [CCS],
          tag="cc3_st")
    kb.dma(dram["cc3i"], ccs[:, :], [CCS], [("cc3i", "*")], eng="scalar")
    kb.cc(kb.dram_h["cc3i"], kb.dram_h["cc3o"], [("cc3i", "*")],
          [("cc3o", "*")])
    kb.dma(ccs[:, :], dram["cc3o"], [("cc3o", "*")], [CCS], eng="scalar")
    kb.ts(rst[:, :], ccs[:, :], PAR, ALU.mult, [CCS, AUX], [RST],
          tag="cc3_R0")

    def corr_phase(h, s):
        wfm, wfm_r = dpt("wfm")
        knt, knt_r = dpt("knt")
        attnT, attnT_r = dpt("Am")
        wrn, wrn_r = dpt("uttm")
        u = s * H + h
        kb.dma(wfm[:, :], dram["w_spill"][u, :, 0:256],
               [("w_spill", f"u{u}")], wfm_r, tag="wload")
        kb.dma(knt[:, :], dram["w_spill"][u, :, 256:512],
               [("w_spill", f"u{u}")], knt_r, tag="wload")
        kb.dma(attnT[:, :], dram["w_spill"][u, :, 512:640],
               [("w_spill", f"u{u}")], attnT_r, tag="wload")
        b0, n0, k0 = psl(kb, 0)
        b1, n1, k1 = psl(kb, 1)
        b2, n2, k2 = psl(kb, 2)
        hs = slice(h * 512, (h + 1) * 512)
        kb.copy(rbfb[:, :], rst[:, hs], [RST], RBF, tag="co_rbf")
        for kc in range(2):
            kb.mm(b0[:, 0:256], wfm[:, kc * P:(kc + 1) * P],
                  rbfb[:, kc * 256:(kc + 1) * 256], wfm_r + RBF,
                  [(n0, k0)], start=(kc == 0), stop=(kc == 1), tag="co_wR")
        kb.ts(wrn[:, :], b0[:, 0:256], -1.0, ALU.mult, [(n0, k0)], wrn_r,
              tag="co_wRn")
        for half in range(2):
            reg = b1[:, half * P:(half + 1) * P]
            for kc in range(2):
                qf, qrd = q_fm(h, kc, s)
                kb.mm(reg, rbfb[:, kc * 256 + half * P:kc * 256 + (half + 1) * P],
                      qf, RBF + [qrd], [(n1, k1)], start=(kc == 0),
                      stop=False, tag="co_q")
            kb.mm(reg, wrn[:, half * P:(half + 1) * P], attnT[:, :],
                  wrn_r + attnT_r, [(n1, k1)], start=False, stop=True,
                  tag="co_a")
        for half in range(2):
            c = 2 * h + half
            kb.tt(bo[:, c, s * P:(s + 1) * P], bo[:, c, s * P:(s + 1) * P],
                  b1[:, half * P:(half + 1) * P], ALU.add,
                  [("bo", f"c{c}"), (n1, k1)], [("bo", f"c{c}")],
                  tag="co_oadd")
        if s < SC - 1:
            for kc in range(2):
                kb.mm(b2[:, kc * 256:(kc + 1) * 256],
                      knt[:, kc * P:(kc + 1) * P], wrn[:, :],
                      knt_r + wrn_r, [(n2, k2)], tag="co_Rupd")
            kb.tt(rst[:, hs], rst[:, hs], b2[:, :], ALU.add,
                  [RST, (n2, k2)], [RST], tag="co_Radd")

    for h in range(H):
        for s in range(SC):
            corr_phase(h, s)

    kb.tap("delta_corr", bo[:, :, :], [("bo", f"c{c}") for c in range(DC)])
    if stop_after == "delta_corr":
        return

    firs = T["firs"]
    itcol = T["itcol"]
    strows = chsc[0:16, :]
    STR = ("chsc", "x")
    wcbK = DPW

    kb.act(itcol[0:1, 0:H], T["ltm_s"][:, :], AF.Sigmoid, [("ltm_s", "*")],
           [("itcol", "row")], scale=-1.0, tag="it_sig")
    kb.act(itcol[0:1, 0:H], itcol[0:1, 0:H], AF.Ln, [("itcol", "row")],
           [("itcol", "row")], tag="it_ln")
    kb.ts(itcol[0:1, 0:H], itcol[0:1, 0:H], -1.0, ALU.mult,
          [("itcol", "row")], [("itcol", "row")], s2=1e-4, op1=ALU.add,
          tag="it_aff")
    kb.recip(itcol[0:1, 0:H], itcol[0:1, 0:H], [("itcol", "row")],
             [("itcol", "row")], tag="it_rec")
    kb.act(itcol[0:1, H:2 * H], T["crl_s"][:, :], AF.Sigmoid,
           [("crl_s", "*")], [("itcol", "row")], tag="crl_sig")
    kb.ts(itcol[0:1, H:2 * H], itcol[0:1, H:2 * H], 0.5, ALU.mult,
          [("itcol", "row")], [("itcol", "row")], tag="crl_half")
    ITR = ("itcol", "row")

    b1c = T["b1c_s"]
    firlc = dp[:, 256:384].bitcast(F32)
    FLC = [("dpt", "r256"), ("dpt", "r384")]
    for h in range(H):
        c0, c1 = 2 * h, 2 * h + 1
        dcol, dcol_r = dpt("dcol")
        for ci, c in enumerate((c0, c1)):
            fdst = bk[:, 4 + ci, 0:TK]
            FK = ("bk", f"c{4 + ci}")
            kb.ts(fdst, vf[:, c, 63:63 + TK], firs[:, c, 4:5], ALU.mult,
                  [("vf", f"c{c}"), ("firs", "*")], [FK], tag="fs_t0")
            for tau in range(1, 5):
                kb.stt(fdst, vf[:, c, 63 - tau:63 + TK - tau],
                       firs[:, c, 4 - tau:5 - tau], fdst, ALU.mult, ALU.add,
                       [("vf", f"c{c}"), ("firs", "*"), FK], [FK], tag="fs_t")
            ldst = bk[:, 6 + ci, 0:TK]
            LK = ("bk", f"c{6 + ci}")
            kb.dma(firlc[:, :], dram["fir_l"][:, c, :], [("fir_l", "*")],
                   FLC, tag="fl_ld")
            for tau in range(64):
                kb.ts(dcol[:, :], ident[:, :], firlc[:, 63 - tau:64 - tau],
                      ALU.mult, [("identity", "*")] + FLC,
                      dcol_r, tag="fl_diag")
                for t in range(NT4):
                    pap, pnm, pk = psl(kb, 4 + t)
                    kb.mm(pap, dcol[:, :],
                          vf[:, c, 63 + t * 512 - tau:63 + (t + 1) * 512 - tau],
                          dcol_r + [("vf", f"c{c}")], [(pnm, pk)],
                          start=(tau == 0), stop=(tau == 63), tag="fl_mm")
            for t in range(NT4):
                pap, pnm, pk = psl(kb, 4 + t)
                kb.copy(ldst[:, t * 512:(t + 1) * 512], pap, [(pnm, pk)],
                        [LK], tag="fl_ev")

        def branch_ap(bi, ci):
            c = 2 * h + ci
            if bi == 0:
                return bk[:, 4 + ci, 0:TK], ("bk", f"c{4 + ci}")
            if bi == 1:
                return bk[:, 6 + ci, 0:TK], ("bk", f"c{6 + ci}")
            if bi == 2:
                return bo[:, c, :], ("bo", f"c{c}")
            return vf[:, c, 63:63 + TK], ("vf", f"c{c}")
        for bi in range(4):
            for ci in range(2):
                bap, brd = branch_ap(bi, ci)
                for t in range(NT4):
                    pap, pnm, pk = psl(kb, t)
                    kb.mm(pap[0:1, :], ones_blk[:, 0:1],
                          bap[:, t * 512:(t + 1) * 512],
                          [("ones_blk", "*"), brd], [(pnm, pk)],
                          start=(ci == 0), stop=(ci == 1), tag="st_sum")
                kb.act(sc_a[:, :], bap, AF.Square, [brd], [("sc_a", "x")],
                       tag="st_sq")
                for t in range(NT4):
                    pap, pnm, pk = psl(kb, 4 + t)
                    kb.mm(pap[0:1, :], ones_blk[:, 0:1],
                          sc_a[:, t * 512:(t + 1) * 512],
                          [("ones_blk", "*"), ("sc_a", "x")], [(pnm, pk)],
                          start=(ci == 0), stop=(ci == 1), tag="st_sqsum")
            for t in range(NT4):
                ts_ = slice(t * 512, (t + 1) * 512)
                pap, pnm, pk = psl(kb, t)
                pbq, pnm2, pk2 = psl(kb, 4 + t)
                row = sc_a[0:1, :]
                kb.ts(row[:, 0:512], pap[0:1, :], 1.0 / 256.0, ALU.mult,
                      [(pnm, pk)], [("sc_a", "x")], tag="st_mean")
                kb.tt(row[:, 1536:2048], row[:, 0:512], row[:, 0:512],
                      ALU.mult, [("sc_a", "x")], [("sc_a", "x")],
                      tag="st_musq")
                kb.stt(row[:, 512:1024], pbq[0:1, :], 1.0 / 256.0,
                       row[:, 1536:2048], ALU.mult, ALU.subtract,
                       [(pnm2, pk2), ("sc_a", "x")], [("sc_a", "x")],
                       tag="st_var")
                kb.act(row[:, 1024:1536], pbq[0:1, :], AF.Sqrt,
                       [(pnm2, pk2)], [("sc_a", "x")], tag="st_l2")
                for j, r in [(0, 0), (1, 1), (2, 3)]:
                    kb.dma(strows[4 * bi + r:4 * bi + r + 1, ts_],
                           row[:, j * 512:(j + 1) * 512], [("sc_a", "x")],
                           [STR], eng="scalar", tag="st_row")
            for ci in range(2):
                bap, brd = branch_ap(bi, ci)
                kb.act(sc_a[:, :], bap, AF.Abs, [brd], [("sc_a", "x")],
                       tag="st_abs")
                for t in range(NT4):
                    pap, pnm, pk = psl(kb, t)
                    kb.mm(pap[0:1, :], ones_blk[:, 0:1],
                          sc_a[:, t * 512:(t + 1) * 512],
                          [("ones_blk", "*"), ("sc_a", "x")], [(pnm, pk)],
                          start=(ci == 0), stop=(ci == 1), tag="st_asum")
            for t in range(NT4):
                pap, pnm, pk = psl(kb, t)
                kb.ts(sc_a[0:1, 0:512], pap[0:1, :], 1.0 / 256.0, ALU.mult,
                      [(pnm, pk)], [("sc_a", "x")], tag="st_amean")
                kb.dma(strows[4 * bi + 2:4 * bi + 3, t * 512:(t + 1) * 512],
                       sc_a[0:1, 0:512], [("sc_a", "x")], [STR],
                       eng="scalar", tag="st_row")
        if h == 0:
            kb.tap("stats0", strows[:, :], [STR])

        for m in range(DC):
            kb.dma(wld2[0:16, 128:256], dram["w1s"][m], [("w1s", "*")],
                   [DPW], tag="w1s_ld")
            kb.copy(wcb[0:16, 1024:1152], wld2[0:16, 128:256], [DPW], [DPW],
                    tag="w1s_cast")
            for half in range(4):
                kb.dma(wld2[:, 0:256].rearrange("p (k q) -> p k q", q=P),
                       dram["w1h"][m, 2 * half:2 * half + 2].transpose(
                           [1, 0, 2]),
                       [("w1h", "*")], [DPW], tag="w1_ld")
                kb.copy(wcb[:, half * 256:(half + 1) * 256], wld2[:, 0:256],
                        [DPW], [DPW], tag="w1_cast")
            for t in range(NT4):
                pap, pnm, pk = psl(kb, (m % 2) * 2 + (t % 2))
                for k in range(DC):
                    rhs, rrd = xnb_ap(k)
                    kb.mm(pap, wcb[:, k * 128:(k + 1) * 128],
                          rhs[:, t * 512:(t + 1) * 512], [DPW, rrd],
                          [(pnm, pk)], start=(k == 0), stop=False,
                          tag="w1_mm")
                kb.mm(pap, wcb[0:16, 1024:1152],
                      strows[:, t * 512:(t + 1) * 512], [DPW, STR],
                      [(pnm, pk)], start=False, stop=True, tag="w1s_mm")
                kb.act(bq[:, m, t * 512:(t + 1) * 512], pap, AF.Gelu,
                       [(pnm, pk), ("b1c_s", "*")], [("bq", f"c{m}")],
                       bias=b1c[:, m:m + 1], tag="w1_gelu")

        kb.dma(wld2[:, 0:32].rearrange("p (k q) -> p k q", q=4),
               dram["w2"].transpose([1, 0, 2]), [("w2", "*")], [DPW],
               tag="w2_ld")
        kb.copy(wcb[:, 0:32], wld2[:, 0:32], [DPW], [DPW], tag="w2_cast")
        kb.copy(sc_a[0:1, 32:36], T["gb_s"][0:1, h * 4:(h + 1) * 4],
                [("gb_s", "*")], [("sc_a", "x")], tag="gb_cast")
        kb.copy(sc_a[0:1, 40:41], itcol[0:1, h:h + 1], [ITR], [("sc_a", "x")],
                tag="it_cast")
        pap9, pn9, pk9 = psl(kb, 6)
        kb.mm(pap9[:, 0:1], ones_blk[0:1, :], sc_a[0:1, 40:41],
              [("ones_blk", "*"), ("sc_a", "x")], [(pn9, pk9)], tag="it_bc")
        kb.copy(itcol[:, 8:9], pap9[:, 0:1], [(pn9, pk9)], [("itcol", "col")],
                tag="it_bcev")
        ITC = ("itcol", "col")
        lgf = wld2[:, 256:264]
        LGF = DPW
        for tb in range(SC):
            pap, pnm, pk = psl(kb, 7 if tb % 2 == 0 else 6)
            for k in range(DC):
                kb.mm(pap[:, 0:4], bq[:, k, tb * P:(tb + 1) * P],
                      wcb[:, k * 4:(k + 1) * 4],
                      [("bq", f"c{k}"), DPW], [(pnm, pk)],
                      start=(k == 0), stop=False, tag="w2_mm")
            kb.mm(pap[:, 0:4], ones_blk[0:1, :], sc_a[0:1, 32:36],
                  [("ones_blk", "*"), ("sc_a", "x")], [(pnm, pk)],
                  start=False, stop=True, tag="w2_bias")
            lg = lgf[:, 0:4]
            mx = lgf[:, 4:5]
            kb.reduce(mx, pap[:, 0:4], ALU.max, [(pnm, pk)], [LGF],
                      tag="sm_max")
            kb.ts(lg, pap[:, 0:4], mx, ALU.subtract, [(pnm, pk), LGF],
                  [LGF], tag="sm_sub")
            kb.act(lg, lg, AF.Exp, [LGF, ITC], [LGF],
                   scale=itcol[:, 8:9], tag="sm_exp")
            kb.reduce(mx, lg, ALU.add, [LGF], [LGF], tag="sm_sum")
            kb.recip(mx, mx, [LGF], [LGF], tag="sm_rec")
            kb.ts(lg, lg, mx, ALU.mult, [LGF], [LGF], tag="sm_div")
            kb.tt(lg, lg, T["pfloor_s"][:, 0:4], ALU.max,
                  [LGF, ("pfloor_s", "*")], [LGF], tag="sm_floor")
            kb.reduce(mx, lg, ALU.add, [LGF], [LGF], tag="sm_sum2")
            kb.recip(mx, mx, [LGF], [LGF], tag="sm_rec2")
            kb.ts(lg, lg, mx, ALU.mult, [LGF], [LGF], tag="sm_div2")
            kb.copy(sc_a[:, 48:52], lg, [LGF], [("sc_a", "x")], tag="p_cast")
            pap2, pnm2, pk2 = psl(kb, 5)
            kb.transpose(pap2.bitcast(BF16)[0:4, 0:P], sc_a[:, 48:52],
                         [("sc_a", "x")], [(pnm2, pk2)], tag="p_tr")
            kb.copy(strows[0:4, tb * P:(tb + 1) * P],
                    pap2.bitcast(BF16)[0:4, 0:P], [(pnm2, pk2)], [STR],
                    tag="p_rows")
        kb.ts(strows[0:1, :], strows[0:1, :], itcol[0:1, H + h:H + h + 1],
              ALU.add, [STR, ITR], [STR], tag="p0_crl")
        if h == 0:
            kb.tap("p0", strows[0:4, :], [STR])
        for i in range(4):
            kb.dma(sc_a[0:1, 0:TK], strows[i:i + 1, :], [STR],
                   [("sc_a", "x")], eng="scalar", tag="pb_row")
            for t in range(NT4):
                pap, pnm, pk = psl(kb, 4 + (t % 2))
                kb.mm(pap, ones_blk[0:1, :],
                      sc_a[0:1, t * 512:(t + 1) * 512],
                      [("ones_blk", "*"), ("sc_a", "x")], [(pnm, pk)],
                      tag="pb_mm")
                kb.copy(bk[:, i, t * 512:(t + 1) * 512], pap, [(pnm, pk)],
                        [("bk", f"c{i}")], tag="pb_ev")

        for ci, c in enumerate((c0, c1)):
            acc = bq[:, ci, 0:TK]
            ACC = ("bq", f"c{ci}")
            b0ap, b0rd = branch_ap(0, ci)
            kb.tt(acc, b0ap, bk[:, 0, 0:TK], ALU.mult,
                  [b0rd, ("bk", "c0")], [ACC], tag="cb_fs")
            for bi in range(1, 4):
                bap, brd = branch_ap(bi, ci)
                kb.tt(sc_a[:, :], bap, bk[:, bi, 0:TK], ALU.mult,
                      [brd, ("bk", f"c{bi}")], [("sc_a", "x")], tag="cb_mul")
                kb.tt(acc, acc, sc_a[:, :], ALU.add, [ACC, ("sc_a", "x")],
                      [ACC], tag="cb_add")
        for ci in range(2):
            kb.act(sc_a[:, :], bq[:, ci, 0:TK], AF.Square,
                   [("bq", f"c{ci}")], [("sc_a", "x")], tag="rms_sq")
            for t in range(NT4):
                pap, pnm, pk = psl(kb, 4 + t)
                kb.mm(pap, ones_blk[:, :], sc_a[:, t * 512:(t + 1) * 512],
                      [("ones_blk", "*"), ("sc_a", "x")], [(pnm, pk)],
                      start=(ci == 0), stop=(ci == 1), tag="rms_mm")
        for t in range(NT4):
            pap, pnm, pk = psl(kb, 4 + t)
            kb.act(pap, pap, AF.Sqrt, [(pnm, pk), AUX], [(pnm, pk)],
                   bias=EPS5, scale=1.0 / 256.0, tag="rms_sqrt")
            kb.recip(pap, pap, [(pnm, pk)], [(pnm, pk)], tag="rms_rec")
            for ci, c in enumerate((c0, c1)):
                kb.tt(bo[:, c, t * 512:(t + 1) * 512],
                      bq[:, ci, t * 512:(t + 1) * 512], pap, ALU.mult,
                      [("bq", f"c{ci}"), (pnm, pk)], [("bo", f"c{c}")],
                      tag="rms_ap")

    kb.tap("o_comb", bo[:, :, :], [("bo", f"c{c}") for c in range(DC)])
    if stop_after == "o_comb":
        return

    for c in range(DC):
        kb.dma(xr[:, c, :], dram["x_spill"][c], [("x_spill", f"c{c}")],
               [("xr", f"c{c}")], tag="x_reload")

    def rhs_bo(k, t):
        return bo[:, k, t * 512:(t + 1) * 512], ("bo", f"c{k}")

    def evac_res(m, t, pap, prd):
        kb.tt(xr[:, m, t * 512:(t + 1) * 512],
              xr[:, m, t * 512:(t + 1) * 512], pap, ALU.add,
              [("xr", f"c{m}"), prd], [("xr", f"c{m}")], tag="wo_res")

    stream_proj(kb, "wo", dram["wo"], DC, DC, rhs_bo, evac_res,
                wscale=(lambda k: T["rmsw_s"][:, k % 2:k % 2 + 1],
                        ("rmsw_s", "*")), tag="wo")




def build(debug_taps=(), stop_after=None):
    kb = KB(debug_taps)
    dram = kb.dram

    kb.din("x_fm", [DC, P, TK])
    kb.din("aux", [P, 10])
    kb.din("ident_bf", [P, P], BF16)
    kb.din("ones_bf", [P, P], BF16)
    for nm, shape, dt in [
        ("wq", [DC, DC, P, P], F32), ("wk", [DC, DC, P, P], F32),
        ("wv", [DC, DC, P, P], F32), ("wb", [DC, P, 4], F32),
        ("convq", [P, DC, 4], F32), ("convk", [P, DC, 4], F32),
        ("convv", [P, DC, 4], F32),
        ("fir_s", [P, DC, 5], F32), ("fir_l", [P, DC, 64], F32),
        ("w1h", [DC, DC, P, P], F32), ("w1s", [DC, 16, P], F32),
        ("b1c", [P, DC], F32), ("w2", [DC, P, 4], F32),
        ("gate_bias", [1, H * 4], F32), ("log_temp", [1, H], F32),
        ("crl", [1, H], F32), ("rmsw", [P, 2], F32),
        ("wo", [DC, DC, P, P], F32),
        ("mask_sl", [P, P], BF16), ("mask_iu", [P, P], BF16), ("pfloor", [P, 4], BF16),
        ("fw1", [4 * DC, DC, P, P], F32), ("fw2", [DC, 4 * DC, P, P], F32),
        ("fcwin", [P, 32, 3], F32), ("fcbin", [P, 32], F32),
        ("mwi", [5, 32, 8, P, P], F32), ("mwx", [5, 16, P, P], F32),
        ("mwdt", [5, 16, 64, P], F32), ("mwo", [5, 8, 16, P, P], F32),
        ("malog", [5, 8, P, 32], F32), ("mcw", [5, 2, P, 8, 4], F32),
        ("mcb", [5, P, 16], F32), ("mbdt", [5, P, 16], F32),
        ("mdp", [5, P, 16], F32), ("mlnw", [5, P, DC], F32),
        ("mlnb", [5, P, DC], F32), ("sel16", [16, 16 * P], BF16),
        ("erw", [DC, P, 8], F32), ("esgw", [P, DC], F32),
        ("esgu", [16, DC, P, P], F32), ("esd", [DC, DC, P, P], F32),
        ("ew1", [8, DC, DC, P, P], F32), ("ew2", [8, DC, DC, P, P], F32),
    ]:
        kb.din(nm, shape, dt)

    o_x = kb.dout("o_x", [DC, P, TK])

    kb.dscratch("x_spill", [DC, P, TK])
    kb.dscratch("w_spill", [SC * H, P, 640], BF16)
    kb.dscratch("rowtmp", [1, TK])
    kb.dscratch("cc4i", [P, 64], BF16)
    kb.dscratch("cc4o", [P, 64], BF16)
    kb.dscratch("zsp", [16, P, TK], BF16)
    kb.dscratch("cc5i", [P, 304])
    kb.dscratch("cc5o", [P, 304])
    for nm, shape in [("cc1i", [P, 72]), ("cc1o", [P, 72]),
                      ("cc2i", [P, DC * 63]), ("cc2o", [P, DC * 63]),
                      ("cc3i", [P, TK]), ("cc3o", [P, TK])]:
        kb.dscratch(nm, shape)

    xr = kb.sb("xr", [P, DC, TK])
    kb.sb("bq", [P, DC, 3 + TK], BF16)
    kb.sb("bk", [P, DC, 3 + TK], BF16)
    kb.sb("vf", [P, DC, 63 + TK], BF16)
    kb.sb("bo", [P, DC, TK], BF16)
    kb.sb("sc_a", [P, TK], BF16)
    kb.sb("dpt", [P, 1920], BF16)
    kb.sb("chsc", [P, TK], BF16)
    kb.sb("identity", [P, P], BF16)
    kb.sb("ones_blk", [P, P], BF16)
    kb.sb("aux_s", [P, 10])
    kb.sb("mapo_s", [P, 32])
    kb.sb("mcw_s", [P, 8, 4])
    kb.sb("msml", [P, 64])
    kb.sb("itcol", [P, 9])
    kb.sb("mask_sl_s", [P, P], BF16)
    kb.sb("mask_iu_s", [P, P], BF16)
    kb.sb("pfloor_s", [P, 4], BF16)
    kb.sb("cq", [P, DC, 4])
    kb.sb("ck", [P, DC, 4])
    kb.sb("cv", [P, DC, 4])
    kb.sb("firs", [P, DC, 5])
    kb.sb("b1c_s", [P, DC])
    kb.sb("rmsw_s", [P, 2])
    kb.sb("gb_s", [1, H * 4])
    kb.sb("ltm_s", [1, H])
    kb.sb("crl_s", [1, H])
    kb.sb("fcw", [P, 32, 3])
    kb.sb("fcb_s", [P, 32])

    kb.ps("pA", [P, TK])
    kb.ps("pB", [P, TK])

    for c in range(DC):
        kb.dma(xr[:, c, :], dram["x_fm"][c], [("x_fm", "*")],
               [("xr", f"c{c}")])
    for nm, dst in [("ident_bf", "identity"), ("ones_bf", "ones_blk"),
                    ("aux", "aux_s"), ("mask_sl", "mask_sl_s"), ("mask_iu", "mask_iu_s"),
                    ("pfloor", "pfloor_s"),
                    ("b1c", "b1c_s"), ("rmsw", "rmsw_s"),
                    ("gate_bias", "gb_s"), ("log_temp", "ltm_s"),
                    ("crl", "crl_s"),
                    ("fcwin", "fcw"), ("fcbin", "fcb_s")]:
        h = kb.tensors[dst]
        kb.dma(h.ap(), dram[nm], [(nm, "*")], [(dst, "*")])
    for nm, dst in [("convq", "cq"), ("convk", "ck"), ("convv", "cv"),
                    ("fir_s", "firs")]:
        h = kb.tensors[dst]
        kb.dma(h.ap(), dram[nm], [(nm, "*")], [(dst, "*")])

    build_l0_attn(kb, stop_after)
    if stop_after is None or stop_after.startswith(("xq1", "wq1", "h1pre",
                                                    "hsilu", "ffn", "l1")):
        build_l0_ffn(kb, stop_after if stop_after in ("xq1", "wq1", "h1pre",
                                                      "hsilu") else None)
    if stop_after is None or (stop_after or "").startswith("l1"):
        build_l1_attn(kb, stop_after)
    if stop_after is None or (stop_after or "").startswith("e_"):
        if stop_after is not None:
            build_l1_attn(kb, None)
        build_l1_ffn(kb, stop_after)

    for c in range(DC):
        kb.dma(o_x[c], xr[:, c, :], [("xr", f"c{c}")], [("o_x", f"c{c}")],
               eng="scalar", tag="store_x")

    kb.prog.op("gpsimd", lambda e: e.engine_nop(),
               [("o_x", f"c{c}") for c in range(DC)] +
               [("dbg_" + t, "*") for t in kb.debug_taps
                if ("dbg_" + t) in kb.dram],
               [], tag="fin")

    kb.emit()
    return kb




def _bf16np():
    import ml_dtypes
    return ml_dtypes.bfloat16


def _blocks(w):
    din, dout = w.shape
    return np.ascontiguousarray(
        w.reshape(din // P, P, dout // P, P).transpose(2, 0, 1, 3)
    ).astype(np.float32)


def _prep_shared(inputs):
    bf = _bf16np()
    g = {}
    g["wq"] = _blocks(np.asarray(inputs["L0_attn_Wq"]))
    g["wk"] = _blocks(np.asarray(inputs["L0_attn_Wk"]))
    g["wv"] = _blocks(np.asarray(inputs["L0_attn_Wv"]))
    wb = np.asarray(inputs["L0_attn_Wb"])
    g["wb"] = np.ascontiguousarray(wb.reshape(DC, P, 4)).astype(np.float32)
    for nm, key in [("convq", "L0_attn_conv_q"), ("convk", "L0_attn_conv_k"),
                    ("convv", "L0_attn_conv_v")]:
        cw = np.asarray(inputs[key])
        g[nm] = np.ascontiguousarray(
            cw.reshape(DC, P, 4).transpose(1, 0, 2)).astype(np.float32)
    fir_s = np.asarray(inputs["L0_attn_fir_short"]).reshape(D, 5)
    fir_l = np.asarray(inputs["L0_attn_fir_long"]).reshape(D, 64)
    g["fir_s"] = np.ascontiguousarray(
        fir_s.reshape(DC, P, 5).transpose(1, 0, 2)).astype(np.float32)
    g["fir_l"] = np.ascontiguousarray(
        fir_l.reshape(DC, P, 64).transpose(1, 0, 2)).astype(np.float32)
    w1 = np.asarray(inputs["L0_attn_W1"])
    g["w1h"] = _blocks(w1[:D])
    g["w1s"] = np.ascontiguousarray(
        w1[D:].reshape(16, DC, P).transpose(1, 0, 2)).astype(np.float32)
    g["b1c"] = np.ascontiguousarray(
        np.asarray(inputs["L0_attn_b1"]).reshape(DC, P).T).astype(np.float32)
    w2 = np.asarray(inputs["L0_attn_W2"])
    g["w2"] = np.ascontiguousarray(w2.reshape(DC, P, 4)).astype(np.float32)
    g["gate_bias"] = (np.asarray(inputs["L0_attn_b2"])[None, :] +
                      np.asarray(inputs["L0_attn_base_bias"])
                      ).astype(np.float32).reshape(1, H * 4)
    g["log_temp"] = np.asarray(inputs["L0_attn_log_temp"]).reshape(1, H).astype(np.float32)
    g["crl"] = np.asarray(
        inputs["L0_attn_conv_residual_logit"]).reshape(1, H).astype(np.float32)
    rw = np.asarray(inputs["L0_attn_rms_w"])
    g["rmsw"] = np.ascontiguousarray(rw.reshape(2, P).T).astype(np.float32)
    g["wo"] = _blocks(np.asarray(inputs["L0_attn_Wo"]))
    g["mask_sl"] = np.tril(np.ones((P, P), np.float32), -1).astype(bf)
    g["mask_iu"] = np.triu(np.ones((P, P), np.float32), 0).astype(bf)
    pf = np.zeros((P, 4), np.float32)
    pf[:, 0] = 0.05
    pf[:, 1] = 0.05
    g["pfloor"] = pf.astype(bf)
    g["ident_bf"] = np.eye(P, dtype=np.float32).astype(bf)
    g["ones_bf"] = np.ones((P, P), np.float32).astype(bf)
    g["fw1"] = _blocks(np.ascontiguousarray(np.asarray(inputs["L0_ffn_w1"]).T))
    g["fw2"] = _blocks(np.ascontiguousarray(np.asarray(inputs["L0_ffn_w2"]).T))
    cw = np.asarray(inputs["L0_ffn_conv_w"]).reshape(4096, 3)
    g["fcwin"] = np.ascontiguousarray(
        cw.reshape(32, P, 3).transpose(1, 0, 2)).astype(np.float32)
    cb = np.asarray(inputs["L0_ffn_conv_b"]).reshape(32, P)
    g["fcbin"] = np.ascontiguousarray(cb.T).astype(np.float32)

    Wi = np.asarray(inputs["L1_attn_W_in"])
    Wx = np.asarray(inputs["L1_attn_W_x"])
    Wdt = np.asarray(inputs["L1_attn_W_dt"])
    Wo = np.asarray(inputs["L1_attn_W_out"])
    g["_mwi"] = [_blocks(np.ascontiguousarray(Wi[l].T)) for l in range(4)]
    wxp = np.zeros((4, P, 2048), np.float32)
    wxp[:, 0:64] = Wx[:, 0:64]
    wxp[:, 64:80] = Wx[:, 64:80]
    wxp[:, 96:112] = Wx[:, 80:96]
    g["_mwx"] = [
        np.ascontiguousarray(
            wxp[l].T.reshape(16, P, P)).astype(np.float32)
        for l in range(4)]
    g["_mwdt"] = [
        np.ascontiguousarray(
            Wdt[l].T.reshape(64, 16, P).transpose(1, 0, 2)).astype(np.float32)
        for l in range(4)]
    g["_mwo"] = [_blocks(np.ascontiguousarray(Wo[l].T)) for l in range(4)]
    g["_malog"] = np.asarray(inputs["L1_attn_A_log"]).reshape(4, 16, P, 16)
    g["_mcw"] = np.asarray(inputs["L1_attn_conv_w"]).reshape(4, 16, P, 4)
    g["_mcb"] = np.asarray(inputs["L1_attn_conv_b"]).reshape(4, 16, P)
    g["_mbdt"] = np.asarray(inputs["L1_attn_b_dt"]).reshape(4, 16, P)
    g["_mdp"] = np.asarray(inputs["L1_attn_Dp"]).reshape(4, 16, P)
    g["_mlnw"] = np.asarray(inputs["L1_attn_ln_w"]).reshape(4, DC, P)
    g["_mlnb"] = np.asarray(inputs["L1_attn_ln_b"]).reshape(4, DC, P)
    sel = np.zeros((16, 16 * P), np.float32)
    for s in range(16):
        sel[s, s * P:(s + 1) * P] = 1.0
    g["sel16"] = sel.astype(bf)

    g["erw"] = np.ascontiguousarray(
        np.asarray(inputs["L1_ffn_router_w"]).reshape(DC, P, 8)
    ).astype(np.float32)
    g["esgw"] = np.ascontiguousarray(
        np.asarray(inputs["L1_ffn_shared_gate_w"]).reshape(DC, P).T
    ).astype(np.float32)
    g["esgu"] = _blocks(np.asarray(inputs["L1_ffn_shared_gate_up"]))
    g["esd"] = _blocks(np.asarray(inputs["L1_ffn_shared_down"]))
    w1e = np.asarray(inputs["L1_ffn_w1"])
    w2e = np.asarray(inputs["L1_ffn_w2"])
    g["ew1"] = np.stack([_blocks(w1e[e]) for e in range(8)])
    g["ew2"] = np.stack([_blocks(w2e[e]) for e in range(8)])
    return g


def _prep_core(inputs, shared, core):
    b, half = core // 2, core % 2
    x = np.asarray(inputs["L0_attn_x"])
    shard = x[b, half * TK:(half + 1) * TK, :]
    x_fm = np.ascontiguousarray(shard.T).reshape(DC, P, TK).astype(np.float32)
    aux = np.zeros((P, 10), np.float32)
    aux[:, 0] = float(half)
    aux[:, 1] = 1.0 - float(half)
    aux[:, 3] = 1e-5
    aux[:, 4] = 1e-6
    lm = [0, 1, 2, 3, 3] if half == 0 else [0, 0, 1, 2, 3]
    msk = [1, 1, 1, 1, 0] if half == 0 else [0, 1, 1, 1, 1]
    for s in range(5):
        aux[:, 5 + s] = float(msk[s])
    im = dict(shared)
    im["x_fm"] = x_fm
    im["aux"] = aux
    im["mwi"] = np.stack([shared["_mwi"][l] for l in lm])
    im["mwx"] = np.stack([shared["_mwx"][l] for l in lm])
    im["mwdt"] = np.stack([shared["_mwdt"][l] for l in lm])
    im["mwo"] = np.stack([shared["_mwo"][l] for l in lm])
    im["malog"] = np.ascontiguousarray(
        np.stack([shared["_malog"][l] for l in lm])
        .reshape(5, 8, 2, P, 16)
        .transpose(0, 1, 3, 2, 4).reshape(5, 8, P, 32)).astype(np.float32)
    im["mcw"] = np.ascontiguousarray(
        np.stack([shared["_mcw"][l] for l in lm])
        .reshape(5, 2, 8, P, 4).transpose(0, 1, 3, 2, 4)).astype(np.float32)
    im["mcb"] = np.ascontiguousarray(
        np.stack([shared["_mcb"][l] for l in lm])
        .transpose(0, 2, 1)).astype(np.float32)
    im["mbdt"] = np.ascontiguousarray(
        np.stack([shared["_mbdt"][l] for l in lm])
        .transpose(0, 2, 1)).astype(np.float32)
    im["mdp"] = np.ascontiguousarray(
        np.stack([shared["_mdp"][l] for l in lm])
        .transpose(0, 2, 1)).astype(np.float32)
    im["mlnw"] = np.ascontiguousarray(
        np.stack([shared["_mlnw"][l] for l in lm])
        .transpose(0, 2, 1)).astype(np.float32)
    im["mlnb"] = np.ascontiguousarray(
        np.stack([shared["_mlnb"][l] for l in lm])
        .transpose(0, 2, 1)).astype(np.float32)
    for k in list(im):
        if k.startswith("_"):
            del im[k]
    return im


def kernel(**inputs):
    kb = build()
    shared = _prep_shared(inputs)
    in_maps = [_prep_core(inputs, shared, c) for c in range(8)]
    res = run_bass_kernel_spmd(kb.nc, in_maps, core_ids=list(range(8)))
    out = np.zeros((4, 4096, 1024), np.float32)
    for c in range(8):
        b, half = c // 2, c % 2
        xf = res.results[c]["o_x"].reshape(D, TK)
        out[b, half * TK:(half + 1) * TK, :] = xf.T
    return out




def build_l0_ffn(kb, stop_after=None):
    T = kb.tensors
    dram = kb.dram
    xr = T["xr"]
    xrb = xr.ap().bitcast(BF16)
    bq, bk, vf, bo = T["bq"], T["bk"], T["vf"], T["bo"]
    sc_a = T["sc_a"]
    dp = T["dpt"]
    chsc = T["chsc"]
    ones_blk = T["ones_blk"]
    aux = T["aux_s"]
    PAR, NPAR = aux[:, 0:1], aux[:, 1:2]
    EPS5 = aux[:, 3:4]
    AUX = ("aux_s", "*")
    wld2 = dp[:, 1152:1792].bitcast(F32)
    wcb = dp[:, 0:1152]
    DPW = ("dpt", "*")

    bo_flat = bo.ap().rearrange("p c t -> p (c t)").bitcast(F32)
    mu_ap = bo_flat[:, 0:TK]
    rb_ap = bo_flat[:, TK:2 * TK]
    am_ap = bo_flat[:, 2 * TK:3 * TK]
    iv_ap = bo_flat[:, 3 * TK:4 * TK]
    MU = [("bo", "c0"), ("bo", "c1")]
    RB = [("bo", "c2"), ("bo", "c3")]
    AM = [("bo", "c4"), ("bo", "c5")]
    IV = [("bo", "c6"), ("bo", "c7")]

    def xnb_ap(c):
        return (xrb[:, c // 2, (c % 2) * TK:(c % 2 + 1) * TK],
                ("xr", f"c{c // 2}"))

    for c in range(DC):
        kb.copy(sc_a[:, :], xr[:, c, :], [("xr", f"c{c}")], [("sc_a", "x")],
                eng="scalar", tag="f_ln_cast")
        for t in range(NT4):
            pap, pnm, pk = psl(kb, t)
            kb.mm(pap, ones_blk[:, :], sc_a[:, t * 512:(t + 1) * 512],
                  [("ones_blk", "*"), ("sc_a", "x")], [(pnm, pk)],
                  start=(c == 0), stop=(c == DC - 1), tag="f_ln_sum")
        kb.act(sc_a[:, :], xr[:, c, :], AF.Square, [("xr", f"c{c}")],
               [("sc_a", "x")], tag="f_ln_sq")
        for t in range(NT4):
            pap, pnm, pk = psl(kb, 4 + t)
            kb.mm(pap, ones_blk[:, :], sc_a[:, t * 512:(t + 1) * 512],
                  [("ones_blk", "*"), ("sc_a", "x")], [(pnm, pk)],
                  start=(c == 0), stop=(c == DC - 1), tag="f_ln_sqsum")
    for t in range(NT4):
        ts_ = slice(t * 512, (t + 1) * 512)
        pap, pnm, pk = psl(kb, t)
        pbq, pnm2, pk2 = psl(kb, 4 + t)
        kb.ts(mu_ap[:, ts_], pap, 1.0 / D, ALU.mult, [(pnm, pk)], MU,
              tag="f_ln_mu")
        kb.tt(rb_ap[:, ts_], mu_ap[:, ts_], mu_ap[:, ts_], ALU.mult, MU, RB,
              tag="f_ln_musq")
        kb.stt(rb_ap[:, ts_], pbq, 1.0 / D, rb_ap[:, ts_], ALU.mult,
               ALU.subtract, [(pnm2, pk2)] + RB, RB, tag="f_ln_var")
    kb.act(rb_ap[:, :], rb_ap[:, :], AF.Sqrt, RB + [AUX], RB, bias=EPS5,
           tag="f_ln_sqrt")
    kb.recip(rb_ap[:, :], rb_ap[:, :], RB, RB, tag="f_ln_recip")
    for c in range(DC):
        kb.tt(sc_a[:, :], xr[:, c, :], mu_ap[:, :], ALU.subtract,
              [("xr", f"c{c}")] + MU, [("sc_a", "x")], tag="f_ln_sub")
        kb.dma(dram["x_spill"][c], xr[:, c, :], [("xr", f"c{c}")],
               [("x_spill", f"c{c}")], eng="scalar", tag="f_spill")
        xnbc, xnbrd = xnb_ap(c)
        kb.tt(xnbc, sc_a[:, :], rb_ap[:, :], ALU.mult, [("sc_a", "x")] + RB,
              [xnbrd], tag="f_ln_scale")

    chi = chsc.ap().bitcast(I32)
    CHS = ("chsc", "x")

    def act_quant(src_ap_fn, nch, inv_dst, inv_keys, xq_dst_fn, tag):
        kb.memset(am_ap[:, :], 0.0, AM, tag=tag + "_amz")
        for c in range(nch):
            sap, srd = src_ap_fn(c)
            kb.stt(am_ap[:, :], sap, -1.0, am_ap[:, :], ALU.mult, ALU.max,
                   [srd] + AM, AM, tag=tag + "_amn")
            kb.stt(am_ap[:, :], sap, 1.0, am_ap[:, :], ALU.mult, ALU.max,
                   [srd] + AM, AM, tag=tag + "_amp")
        for tb in range(SC):
            pap, pnm, pk = psl(kb, 6 + tb % 2)
            kb.copy(chsc[:, 0:128], am_ap[:, tb * P:(tb + 1) * P], AM,
                    [CHS], tag=tag + "_bf")
            kb.transpose(pap.bitcast(BF16)[:, 0:128], chsc[:, 0:128],
                         [CHS], [(pnm, pk)], tag=tag + "_tr")
            kb.reduce(wld2[:, 300 + tb % 2:301 + tb % 2],
                      pap.bitcast(BF16)[:, 0:128],
                      ALU.max, [(pnm, pk)], [DPW], tag=tag + "_red")
            kb.dma(dram["rowtmp"][0:1, tb * P:(tb + 1) * P].transpose([1, 0]),
                   wld2[:, 300 + tb % 2:301 + tb % 2],
                   [DPW], [("rowtmp", "*")], eng="scalar",
                   tag=tag + "_rowst")
        kb.dma(mu_ap[0:1, :], dram["rowtmp"], [("rowtmp", "*")], MU,
               tag=tag + "_rowld")
        kb.ts(mu_ap[0:1, :], mu_ap[0:1, :], 1e-5, ALU.max, MU, MU,
              tag=tag + "_clip")
        kb.ts(rb_ap[0:1, 0:TK], mu_ap[0:1, :], 1.0 / 127.0, ALU.mult,
              MU, RB, tag=tag + "_inv")
        kb.memset(wld2[0:1, 128:256], 1.0, [DPW], tag=tag + "_ones")
        for t in range(NT4):
            pap, pnm, pk = psl(kb, 4 + t % 2)
            kb.mm(pap, wld2[0:1, 128:256], rb_ap[0:1, t * 512:(t + 1) * 512],
                  [DPW] + RB, [(pnm, pk)], tag=tag + "_invbc")
            kb.copy(inv_dst[:, t * 512:(t + 1) * 512], pap, [(pnm, pk)],
                    inv_keys, tag=tag + "_invev")
        kb.recip(rb_ap[0:1, 0:TK], rb_ap[0:1, 0:TK], RB, RB,
                 tag=tag + "_qrec")
        for t in range(NT4):
            pap, pnm, pk = psl(kb, 4 + t % 2)
            kb.mm(pap, wld2[0:1, 128:256], rb_ap[0:1, t * 512:(t + 1) * 512],
                  [DPW] + RB, [(pnm, pk)], tag=tag + "_qbc")
            kb.copy(am_ap[:, t * 512:(t + 1) * 512], pap, [(pnm, pk)], AM,
                    tag=tag + "_qev")
        mui = mu_ap.bitcast(I32)
        for c in range(nch):
            sap, srd = src_ap_fn(c)
            dap, drd = xq_dst_fn(c)
            kb.tt(mui[:, 0:TK], sap, am_ap[:, :], ALU.mult, [srd] + AM, MU,
                  tag=tag + "_qmul")
            kb.copy(dap, mui[:, 0:TK], MU, [drd], tag=tag + "_qcast")

    def xq_dst(c):
        return vf[:, c, 0:TK], ("vf", f"c{c}")

    act_quant(xnb_ap, DC, iv_ap, IV, xq_dst, "aq1")
    kb.tap("xq1", vf[:, :, 0:TK], [("vf", f"c{c}") for c in range(DC)])
    if stop_after == "xq1":
        return

    def wq_scale(wname, nmc, nkc, dst_col, tag):
        for m in range(nmc):
            for half in range((nkc + 1) // 2):
                kcs = list(range(2 * half, min(2 * half + 2, nkc)))
                w = len(kcs) * 128
                kb.dma(wld2[:, 0:w].rearrange("p (k q) -> p k q", q=P),
                       dram[wname][m, kcs[0]:kcs[0] + len(kcs)].transpose(
                           [1, 0, 2]),
                       [(wname, "*")], [DPW], tag=tag + "_ld")
                kb.prog.op("vector",
                           lambda e, o=wld2[:, 281:282], i=wld2[:, 0:w]:
                           e.tensor_reduce(o, i, mybir.AxisListType.X,
                                           ALU.add, True),
                           [DPW], [DPW], tag=tag + "_red")
                if m == 0 and half == 0:
                    kb.copy(wld2[:, 280:281], wld2[:, 281:282], [DPW], [DPW],
                            tag=tag + "_acc0")
                else:
                    kb.tt(wld2[:, 280:281], wld2[:, 280:281],
                          wld2[:, 281:282], ALU.add, [DPW], [DPW],
                          tag=tag + "_acc")
        kb.copy(sc_a[:, 0:1], wld2[:, 280:281], [DPW], [("sc_a", "x")],
                tag=tag + "_cp")
        pap, pnm, pk = psl(kb, 6)
        kb.mm(pap[0:1, 0:1], ones_blk[:, 0:1], sc_a[:, 0:1],
              [("ones_blk", "*"), ("sc_a", "x")], [(pnm, pk)],
              tag=tag + "_psum")
        nel = float(nmc * P * nkc * P)
        kb.ts(wld2[0:1, 285:286], pap[0:1, 0:1], 1.0 / nel,
              ALU.mult, [(pnm, pk)], [DPW], s2=1e-5, op1=ALU.max,
              tag=tag + "_mean")
        kb.recip(wld2[0:1, 284:285],
                 wld2[0:1, 285:286], [DPW], [DPW],
                 tag=tag + "_wrec")
        pap2, pnm2, pk2 = psl(kb, 7)
        kb.memset(wld2[0:1, 128:256], 1.0, [DPW], tag=tag + "_ones")
        kb.mm(pap2[:, 0:1], wld2[0:1, 128:256][0:1, 0:P],
              wld2[0:1, 284:285], [DPW], [(pnm2, pk2)],
              tag=tag + "_wbc")
        kb.copy(wld2[:, 286:287], pap2[:, 0:1],
                [(pnm2, pk2)], [DPW], tag=tag + "_wbcev")
        kb.mm(pap2[:, 1:2], wld2[0:1, 128:256][0:1, 0:P],
              wld2[0:1, 285:286], [DPW], [(pnm2, pk2)],
              tag=tag + "_wbci")
        kb.copy(wld2[:, 287:288], pap2[:, 1:2],
                [(pnm2, pk2)], [DPW], tag=tag + "_wbciev")

    wq_scale("fw1", 4 * DC, DC, 392, "wq1")
    if stop_after == "wq1":
        return

    kb.ts(iv_ap[:, :], iv_ap[:, :], wld2[:, 287:288], ALU.mult,
          IV + [DPW], IV, tag="iv1_w")

    def hchunk(j):
        if j < 8:
            return bq[:, j, 0:TK], ("bq", f"c{j}")
        if j < 16:
            return bk[:, j - 8, 0:TK], ("bk", f"c{j - 8}")
        jj = j - 16
        return (xrb[:, jj // 2, (jj % 2) * TK:(jj % 2 + 1) * TK],
                ("xr", f"c{jj // 2}"))

    def wq_stream_mm(wname, nmc, nkc, wscale_col, rhs_fn, out_cb, tag):
        KG = 4
        for m in range(nmc):
            for g in range(0, nkc, KG):
                kcs = list(range(g, min(g + KG, nkc)))
                for half in range(0, len(kcs), 2):
                    pk2_ = kcs[half:half + 2]
                    w = len(pk2_) * 128
                    kb.dma(wld2[:, 0:w].rearrange("p (k q) -> p k q", q=P),
                           dram[wname][m, pk2_[0]:pk2_[0] + len(pk2_)]
                           .transpose([1, 0, 2]),
                           [(wname, "*")], [DPW], tag=tag + "_ld")
                    kb.ts(chi[:, 0:w], wld2[:, 0:w], wld2[:, 286:287],
                          ALU.mult, [DPW], [CHS], tag=tag + "_qm")
                    kb.ts(chi[:, 0:w], chi[:, 0:w], -1, ALU.max, [CHS],
                          [CHS], s2=1, op1=ALU.min, tag=tag + "_clip")
                    kb.copy(wcb[:, half * 128:half * 128 + w], chi[:, 0:w],
                            [CHS], [DPW], eng="scalar", tag=tag + "_cast")
                for t in range(NT4):
                    pap, pnm, pk = psl(kb, (m % 2) * 4 + t)
                    for j, k in enumerate(kcs):
                        rhs, rrd = rhs_fn(k, t)
                        kb.mm(pap, wcb[:, j * 128:(j + 1) * 128], rhs,
                              [DPW, rrd], [(pnm, pk)], start=(k == 0),
                              stop=(k == nkc - 1), tag=tag + "_mm")
                    if kcs[-1] == nkc - 1:
                        out_cb(m, t, pap, (pnm, pk))

    def rhs_xq(k, t):
        return vf[:, k, t * 512:(t + 1) * 512], ("vf", f"c{k}")

    def ev1(m, t, pap, prd):
        hap, hrd = hchunk(m)
        kb.tt(hap[:, t * 512:(t + 1) * 512], pap,
              iv_ap[:, t * 512:(t + 1) * 512], ALU.mult, [prd] + IV,
              [hrd], tag="ev1")

    wq_stream_mm("fw1", 4 * DC, DC, 394, rhs_xq, ev1, "bm1")
    kb.tap("h1pre", bq[:, :, 0:TK], [("bq", f"c{c}") for c in range(DC)])
    if stop_after == "h1pre":
        return

    ccs2 = sc_a[:, 0:128]
    for j in range(32):
        hap, hrd = hchunk(j)
        kb.ts(sc_a[:, j:j + 1], hap[:, 0:1], PAR, ALU.mult, [hrd, AUX],
              [("sc_a", "x")], tag="cc4_stf")
        kb.ts(sc_a[:, 32 + j:33 + j], hap[:, TK - 1:TK], NPAR, ALU.mult,
              [hrd, AUX], [("sc_a", "x")], tag="cc4_stl")
    kb.dma(dram["cc4i"], sc_a[:, 0:64], [("sc_a", "x")], [("cc4i", "*")],
           eng="scalar")
    kb.cc(kb.dram_h["cc4i"], kb.dram_h["cc4o"], [("cc4i", "*")],
          [("cc4o", "*")])
    kb.dma(sc_a[:, 64:128], dram["cc4o"], [("cc4o", "*")], [("sc_a", "x")],
           eng="scalar")
    kb.ts(sc_a[:, 96:128], sc_a[:, 96:128], PAR, ALU.mult,
          [("sc_a", "x"), AUX], [("sc_a", "x")], tag="cc4_hl")
    kb.ts(sc_a[:, 64:96], sc_a[:, 64:96], NPAR, ALU.mult,
          [("sc_a", "x"), AUX], [("sc_a", "x")], tag="cc4_hr")

    fcw, fcb = T["fcw"], T["fcb_s"]
    for j in range(32):
        hap, hrd = hchunk(j)
        kb.ts(mu_ap[:, :], hap[:, :], fcw[:, j, 1:2], ALU.mult,
              [hrd, ("fcw", "*")], MU, tag="f_cv1")
        kb.stt(mu_ap[:, 1:TK], hap[:, 0:TK - 1], fcw[:, j, 0:1],
               mu_ap[:, 1:TK], ALU.mult, ALU.add, [hrd, ("fcw", "*")] + MU,
               MU, tag="f_cv0")
        kb.stt(mu_ap[:, 0:TK - 1], hap[:, 1:TK], fcw[:, j, 2:3],
               mu_ap[:, 0:TK - 1], ALU.mult, ALU.add,
               [hrd, ("fcw", "*")] + MU, MU, tag="f_cv2")
        kb.stt(mu_ap[:, 0:1], sc_a[:, 96 + j:97 + j], fcw[:, j, 0:1],
               mu_ap[:, 0:1], ALU.mult, ALU.add,
               [("sc_a", "x"), ("fcw", "*")] + MU, MU, tag="f_cve0")
        kb.stt(mu_ap[:, TK - 1:TK], sc_a[:, 64 + j:65 + j], fcw[:, j, 2:3],
               mu_ap[:, TK - 1:TK], ALU.mult, ALU.add,
               [("sc_a", "x"), ("fcw", "*")] + MU, MU, tag="f_cve1")
        kb.act(hap[:, :], mu_ap[:, :], AF.Silu, MU + [("fcb_s", "*")],
               [hrd], bias=fcb[:, j:j + 1], tag="f_silu")
    kb.tap("hsilu", bq[:, :, 0:TK], [("bq", f"c{c}") for c in range(DC)])
    if stop_after == "hsilu":
        return

    wq_scale("fw2", DC, 4 * DC, 392, "wq2")
    act_quant(hchunk, 32, iv_ap, IV, hchunk, "aq2")
    kb.ts(iv_ap[:, :], iv_ap[:, :], wld2[:, 287:288], ALU.mult,
          IV + [DPW], IV, tag="iv2_w")

    def rhs_h(k, t):
        hap, hrd = hchunk(k)
        return hap[:, t * 512:(t + 1) * 512], hrd

    def ev2(m, t, pap, prd):
        kb.tt(vf[:, m, t * 512:(t + 1) * 512], pap,
              iv_ap[:, t * 512:(t + 1) * 512], ALU.mult, [prd] + IV,
              [("vf", f"c{m}")], tag="ev2")

    wq_stream_mm("fw2", DC, 4 * DC, 394, rhs_h, ev2, "bm2")

    for c in range(DC):
        kb.dma(xr[:, c, :], dram["x_spill"][c], [("x_spill", f"c{c}")],
               [("xr", f"c{c}")], tag="f_reload")
        kb.tt(xr[:, c, :], xr[:, c, :], vf[:, c, 0:TK], ALU.add,
              [("xr", f"c{c}"), ("vf", f"c{c}")], [("xr", f"c{c}")],
              tag="f_res")
    kb.tap("l0ffn_out", xr[:, :, :], [("xr", f"c{c}") for c in range(DC)])




def build_l1_attn(kb, stop_after=None):
    T = kb.tensors
    dram = kb.dram
    xr = T["xr"]
    xrb = xr.ap().rearrange("p c k -> p (c k)").bitcast(BF16)
    bq, bk, vf, bo = T["bq"], T["bk"], T["vf"], T["bo"]
    sc_a, chsc = T["sc_a"], T["chsc"]
    scf = sc_a.ap().bitcast(F32)
    dp = T["dpt"]
    wld2 = dp[:, 1152:1792].bitcast(F32)
    wcb = dp[:, 0:1152]
    DPW = ("dpt", "*")
    aux = T["aux_s"]
    AUX = ("aux_s", "*")
    PAR, NPAR = aux[:, 0:1], aux[:, 1:2]
    apo = T["mapo_s"].ap()
    APO = ("mapo_s", "*")
    mcw = T["mcw_s"]
    msml = T["msml"].ap()
    MSM = ("msml", "*")
    mcb_ = msml[:, 0:16]
    nbdt = msml[:, 16:32]
    mdp = msml[:, 32:48]
    mlnw = msml[:, 48:56]
    mlnb = msml[:, 56:64]
    ident = T["identity"]
    pA, pB = T["pA"], T["pB"]
    pAf = pA.ap()
    pBf = pB.ap()
    PA_ALL = [("pA", f"t{j}") for j in range(4)]
    PB_ALL = [("pB", f"t{j}") for j in range(4)]
    SCA = ("sc_a", "x")
    CHS = ("chsc", "x")

    def uslab(j):
        if j < 8:
            return bq[:, j, :], ("bq", f"c{j}")
        return bk[:, j - 8, :], ("bk", f"c{j - 8}")

    def boslab(j):
        return bo[:, j, :], ("bo", f"c{j}")

    def Lslab(db):
        return (xrb[:, db * TK:(db + 1) * TK], ("xr", f"c{db // 2}"))

    eps_ap, eps_rd = kb.cconst(1e-5)

    def m_ln(src_fn, src_f32, dst_fn, with_wb, tag):
        ones_blk = T["ones_blk"]
        for c in range(DC):
            sap, srd = src_fn(c)
            if src_f32:
                kb.copy(sc_a[:, :], sap, [srd], [SCA], eng="scalar",
                        tag=tag + "_cast")
                rhs_ap, rhs_rd = sc_a[:, :], SCA
            else:
                rhs_ap, rhs_rd = sap, srd
            for t in range(NT4):
                pap, pnm, pk = psl(kb, t)
                kb.mm(pap, ones_blk[:, :], rhs_ap[:, t * 512:(t + 1) * 512],
                      [("ones_blk", "*"), rhs_rd], [(pnm, pk)],
                      start=(c == 0), stop=(c == DC - 1), tag=tag + "_sum")
            kb.act(chsc[:, :], sap, AF.Square, [srd], [CHS], tag=tag + "_sq")
            for t in range(NT4):
                pap, pnm, pk = psl(kb, 4 + t)
                kb.mm(pap, ones_blk[:, :], chsc[:, t * 512:(t + 1) * 512],
                      [("ones_blk", "*"), CHS], [(pnm, pk)],
                      start=(c == 0), stop=(c == DC - 1), tag=tag + "_sqs")
        chf = chsc.ap().bitcast(F32)
        for t in range(NT4):
            pap, pnm, pk = psl(kb, t)
            pbq, pnm2, pk2 = psl(kb, 4 + t)
            kb.ts(pap, pap, 1.0 / D, ALU.mult, [(pnm, pk)], [(pnm, pk)],
                  tag=tag + "_mu")
            kb.act(chf[:, 0:512], pap, AF.Square, [(pnm, pk)], [CHS],
                   tag=tag + "_musq")
            kb.stt(pbq, pbq, 1.0 / D, chf[:, 0:512], ALU.mult, ALU.subtract,
                   [(pnm2, pk2), CHS], [(pnm2, pk2)], tag=tag + "_var")
            kb.act(pbq, pbq, AF.Sqrt, [(pnm2, pk2), eps_rd], [(pnm2, pk2)],
                   bias=eps_ap, tag=tag + "_sqrt")
            kb.recip(pbq, pbq, [(pnm2, pk2)], [(pnm2, pk2)],
                     tag=tag + "_rcp")
        for c in range(DC):
            sap, srd = src_fn(c)
            dap, dwr = dst_fn(c)
            for t in range(NT4):
                ts_ = slice(t * 512, (t + 1) * 512)
                pap, pnm, pk = psl(kb, t)
                pbq, pnm2, pk2 = psl(kb, 4 + t)
                kb.tt(chsc[:, ts_], sap[:, ts_], pap, ALU.subtract,
                      [srd, (pnm, pk)], [CHS], tag=tag + "_sub")
                if not with_wb:
                    kb.tt(dap[:, ts_], chsc[:, ts_], pbq, ALU.mult,
                          [CHS, (pnm2, pk2)], [dwr], tag=tag + "_scl")
                else:
                    kb.tt(chsc[:, ts_], chsc[:, ts_], pbq, ALU.mult,
                          [CHS, (pnm2, pk2)], [CHS], tag=tag + "_scl")
                    kb.ts(dap[:, ts_], chsc[:, ts_], mlnw[:, c:c + 1],
                          ALU.mult, [CHS, MSM],
                          [dwr], s2=mlnb[:, c:c + 1], op1=ALU.add,
                          tag=tag + "_wb")

    m_ln(lambda c: (xr[:, c, :], ("xr", f"c{c}")), True,
         lambda c: (vf[:, c, 0:TK], ("vf", f"c{c}")), False, "m_ln0")
    for c in range(DC):
        kb.dma(dram["x_spill"][c], xr[:, c, :], [("xr", f"c{c}")],
               [("x_spill", f"c{c}")], eng="scalar", tag="m_spill")
    kb.tap("l1_ln0", vf[:, :, 0:TK], [("vf", f"c{c}") for c in range(DC)])
    if stop_after == "l1_ln0":
        return

    for st in range(5):
        stg = f"s{st}"
        mask_col = aux[:, 5 + st:6 + st]

        for nm, dsl in [("mlnw", msml[:, 48:56]), ("mlnb", msml[:, 56:64]),
                        ("mcb", msml[:, 0:16]), ("mdp", msml[:, 32:48]),
                        ("mbdt", msml[:, 16:32])]:
            kb.dma(dsl, dram[nm][st], [(nm, "*")], [MSM],
                   tag="m_prm" + stg)
        kb.ts(nbdt[:, :], nbdt[:, :], -1.0, ALU.mult, [MSM], [MSM],
              tag="m_nbdt" + stg)

        if st == 0:
            kb.memset(scf[:, 0:304], 0.0, [SCA], tag="m_hrecv0")
        else:
            kb.dma(scf[:, 0:304], dram["cc5o"], [("cc5o", "*")], [SCA],
                   tag="m_hrecv" + stg)

        m_ln(lambda c: (vf[:, c, 0:TK], ("vf", f"c{c}")), False,
             lambda c: (bo[:, c, :], ("bo", f"c{c}")), True, "m_ln" + stg)

        def wi_evac(m, t, pap, prd, st=st):
            if m < 16:
                uap, urd = uslab(m)
                kb.act(uap[:, 3 + t * 512:3 + (t + 1) * 512], pap, AF.Copy,
                       [prd], [urd], tag="m_xs_ev")
            else:
                kb.copy(chsc[:, t * 512:(t + 1) * 512], pap, [prd], [CHS],
                        eng="scalar", tag="m_z_ev")
                if t == NT4 - 1:
                    kb.dma(dram["zsp"][m - 16], chsc[:, 0:TK], [CHS],
                           [("zsp", f"c{m - 16}")], eng="scalar",
                           tag="m_z_sp")

        stream_proj(kb, "mwi", dram["mwi"][st], 8, 32,
                    lambda k, t: (bo[:, k, t * 512:(t + 1) * 512],
                                  ("bo", f"c{k}")),
                    wi_evac, tag="m_wi" + stg)

        for m in range(16):
            uap, urd = uslab(m)
            kb.copy(scf[:, 816 + m * 3:816 + (m + 1) * 3],
                    uap[:, 2048:2051], [urd], [SCA], tag="m_tailcap")
        for m in range(16):
            uap, urd = uslab(m)
            kb.ts(uap[:, 0:3], scf[:, 256 + m * 3:256 + (m + 1) * 3], PAR,
                  ALU.mult, [SCA, AUX], [urd], tag="m_halo")

        for c in range(16):
            uap, urd = uslab(c)
            if c % 8 == 0:
                kb.dma(T["mcw_s"].ap(), dram["mcw"][st, c // 8],
                       [("mcw", "*")], [("mcw_s", "*")], tag="m_cwld")
            for j in range(4):
                kb.ts(wcb[:, j * P:(j + 1) * P], ident[:, :],
                      mcw[:, c % 8, j:j + 1], ALU.mult,
                      [("identity", "*"), ("mcw_s", "*")], [DPW],
                      tag="m_cvd")
            for tt4 in range(NT4):
                pap, pnm, pk = psl(kb, (c % 2) * 4 + tt4)
                for j in range(4):
                    kb.mm(pap, wcb[:, j * P:(j + 1) * P],
                          uap[:, tt4 * 512 + j:tt4 * 512 + j + 512],
                          [DPW, urd], [(pnm, pk)], start=(j == 0),
                          stop=(j == 3), tag="m_cvmm")
            for tt4 in range(NT4):
                pap, pnm, pk = psl(kb, (c % 2) * 4 + tt4)
                kb.act(uap[:, 3 + tt4 * 512:3 + (tt4 + 1) * 512], pap,
                       AF.Silu, [(pnm, pk), MSM], [urd],
                       bias=mcb_[:, c:c + 1], tag="m_cvsilu")

        for g in range(2):
            for half in range(4):
                k0 = g * 8 + half * 2
                kb.dma(wld2[:, 0:256].rearrange("p (k q) -> p k q", q=P),
                       dram["mwx"][st, k0:k0 + 2].transpose([1, 0, 2]),
                       [("mwx", "*")], [DPW], tag="m_wx_ld")
                kb.copy(wcb[:, (half * 2) * P:(half * 2 + 2) * P],
                        wld2[:, 0:256], [DPW], [DPW], eng="scalar", tag="m_wx_cast")
            for t in range(NT4):
                pap, pnm, pk = psl(kb, t)
                for j in range(8):
                    k = g * 8 + j
                    uap, urd = uslab(k)
                    kb.mm(pap, wcb[:, j * P:(j + 1) * P],
                          uap[:, 3 + t * 512:3 + (t + 1) * 512],
                          [DPW, urd], [(pnm, pk)], start=(k == 0),
                          stop=(k == 15), tag="m_wx_mm")
                if g == 1:
                    kb.act(bo[64:128, 4, t * 512:(t + 1) * 512],
                           pap[0:64, :], AF.Copy, [(pnm, pk)],
                           [("bo", "c4")], tag="m_dtin_ev")
                    kb.act(bo[64:80, 4, t * 512:(t + 1) * 512], pap[64:80, :],
                           AF.Copy, [(pnm, pk)], [("bo", "c4")],
                           tag="m_B_ev")
                    kb.act(bo[96:112, 7, t * 512:(t + 1) * 512],
                           pap[96:112, :], AF.Copy, [(pnm, pk)],
                           [("bo", "c7")], tag="m_C_ev")
        kb.dma(bo[0:16, 4, :], bo[64:80, 4, :], [("bo", "c4")],
               [("bo", "c4")], eng="scalar", tag="m_Bmv")
        kb.dma(bo[0:16, 7, :], bo[96:112, 7, :], [("bo", "c7")],
               [("bo", "c7")], eng="scalar", tag="m_Cmv")
        kb.dma(chsc[0:16, :], dram["sel16"], [("sel16", "*")],
               [CHS], tag="m_selld")

        kb.ts(scf[:, 304:560], scf[:, 0:256], PAR, ALU.mult, [SCA, AUX],
              [SCA], tag="m_hinit")

        dA_pair = [boslab(0), boslab(6)]
        d1_ap, d1_rd = boslab(1)
        g_ap, g_rd = boslab(2)
        ya_ap, ya_rd = boslab(3)
        du_ap, du_rd = boslab(5)
        for db in range(16):
            kb.dma(wcb[64:128, 0:P], dram["mwdt"][st, db],
                   [("mwdt", "*")], [DPW], eng="gpsimd", tag="m_wdt_ld")
            Lap, Lrd = Lslab(db)
            for t in range(NT4):
                pap, pnm, pk = psl(kb, (db % 2) * 4 + t)
                kb.mm(pap, wcb[64:128, 0:P],
                      bo[64:128, 4, t * 512:(t + 1) * 512],
                      [DPW, ("bo", "c4")], [(pnm, pk)], tag="m_wdt_mm")
            for t in range(NT4):
                pap, pnm, pk = psl(kb, (db % 2) * 4 + t)
                kb.act(pap, pap, AF.Sigmoid, [(pnm, pk), MSM],
                       [(pnm, pk)], bias=nbdt[:, db:db + 1],
                       scale=-1.0, tag="m_sig")
            for t in range(NT4):
                pap, pnm, pk = psl(kb, (db % 2) * 4 + t)
                kb.act(Lap[:, t * 512:(t + 1) * 512], pap, AF.Ln,
                       [(pnm, pk)], [Lrd], tag="m_lnact")
        for db in range(16):
            if db % 2 == 0:
                kb.dma(apo[:, :], dram["malog"][st, db // 2],
                       [("malog", "*")], [APO], tag="m_apold")
                kb.act(apo[:, :], apo[:, :], AF.Exp, [APO], [APO],
                       tag="m_apos")
            Lap, Lrd = Lslab(db)
            uap, urd = uslab(db)
            kb.stt(du_ap[:, 0:TK], uap[:, 3:3 + TK], -1.0, Lap[:, 0:TK],
                   ALU.mult, ALU.mult, [urd, Lrd], [du_rd], tag="m_du")
            for s in range(16):
                dA_ap, dA_rd = dA_pair[s % 2]
                scol = apo[:, (db % 2) * 16 + s:(db % 2) * 16 + s + 1]
                kb.act(dA_ap[:, 0:TK], Lap[:, 0:TK], AF.Exp, [Lrd, APO],
                       [dA_rd], scale=scol, tag="m_dA")
                for t in range(NT4):
                    pap, pnm, pk = psl(kb, t)
                    kb.mm(pap, chsc[0:16, s * P:(s + 1) * P],
                          bo[0:16, 4, t * 512:(t + 1) * 512],
                          [CHS, ("bo", "c4")], [(pnm, pk)], tag="m_Bbc")
                kb.tt(d1_ap[:, 0:TK], du_ap[:, 0:TK], pAf[:, 0:TK],
                      ALU.mult, [du_rd] + PA_ALL, [d1_rd], tag="m_d1")
                kb.scan(g_ap[:, 0:TK], dA_ap[:, 0:TK], d1_ap[:, 0:TK],
                        scf[:, 304 + db * 16 + s:304 + db * 16 + s + 1],
                        [dA_rd, d1_rd, SCA], [g_rd], tag="m_scan")
                kb.copy(scf[:, 560 + db * 16 + s:560 + db * 16 + s + 1],
                        g_ap[:, TK - 1:TK], [g_rd], [SCA], tag="m_hfin")
                for t in range(NT4):
                    pap, pnm, pk = psl(kb, 4 + t)
                    kb.mm(pap, chsc[0:16, s * P:(s + 1) * P],
                          bo[0:16, 7, t * 512:(t + 1) * 512],
                          [CHS, ("bo", "c7")], [(pnm, pk)], tag="m_Cbc")
                if s == 0:
                    kb.tt(ya_ap[:, 0:TK], g_ap[:, 0:TK], pBf[:, 0:TK],
                          ALU.mult, [g_rd] + PB_ALL, [ya_rd], tag="m_w0")
                else:
                    kb.tt(d1_ap[:, 0:TK], g_ap[:, 0:TK], pBf[:, 0:TK],
                          ALU.mult, [g_rd] + PB_ALL, [d1_rd], tag="m_w")
                    kb.tt(ya_ap[:, 0:TK], ya_ap[:, 0:TK], d1_ap[:, 0:TK],
                          ALU.add, [ya_rd, d1_rd], [ya_rd], tag="m_yadd")
            dA_ap, dA_rd = dA_pair[0]
            kb.dma(dA_ap[:, 0:TK], dram["zsp"][db], [("zsp", f"c{db}")],
                   [dA_rd], tag="m_zrel")
            kb.stt(d1_ap[:, 0:TK], uap[:, 3:3 + TK], mdp[:, db:db + 1],
                   ya_ap[:, 0:TK], ALU.mult, ALU.add,
                   [urd, MSM, ya_rd], [d1_rd], tag="m_udp")
            kb.act(g_ap[:, 0:TK], dA_ap[:, 0:TK], AF.Silu, [dA_rd], [g_rd],
                   tag="m_zsilu")
            kb.tt(uap[:, 3:3 + TK], d1_ap[:, 0:TK], g_ap[:, 0:TK], ALU.mult,
                  [d1_rd, g_rd], [urd], tag="m_ymix")

        for m in range(8):
            for g in range(2):
                for half in range(4):
                    k0 = g * 8 + half * 2
                    kb.dma(wld2[:, 0:256].rearrange("p (k q) -> p k q", q=P),
                           dram["mwo"][st, m, k0:k0 + 2].transpose([1, 0, 2]),
                           [("mwo", "*")], [DPW], tag="m_wo_ld")
                    kb.copy(wcb[:, (half * 2) * P:(half * 2 + 2) * P],
                            wld2[:, 0:256], [DPW], [DPW], eng="scalar", tag="m_wo_cast")
                for t in range(NT4):
                    pap, pnm, pk = psl(kb, (m % 2) * 4 + t)
                    for j in range(8):
                        k = g * 8 + j
                        uap, urd = uslab(k)
                        kb.mm(pap, wcb[:, j * P:(j + 1) * P],
                              uap[:, 3 + t * 512:3 + (t + 1) * 512],
                              [DPW, urd], [(pnm, pk)], start=(k == 0),
                              stop=(k == 15), tag="m_wo_mm")
                    if g == 1:
                        kb.stt(vf[:, m, t * 512:(t + 1) * 512], pap,
                               mask_col, vf[:, m, t * 512:(t + 1) * 512],
                               ALU.mult, ALU.add,
                               [(pnm, pk), AUX, ("vf", f"c{m}")],
                               [("vf", f"c{m}")], tag="m_resid")

        if st < 4:
            kb.ts(scf[:, 560:864], scf[:, 560:864], NPAR, ALU.mult,
                  [SCA, AUX], [SCA], tag="m_ccscale")
            kb.dma(dram["cc5i"], scf[:, 560:864], [SCA], [("cc5i", "*")],
                   eng="scalar", tag="m_ccst")
            kb.cc(kb.dram_h["cc5i"], kb.dram_h["cc5o"], [("cc5i", "*")],
                  [("cc5o", "*")], tag="m_cc")

        kb.tap(f"l1_y{st}", vf[:, :, 0:TK],
               [("vf", f"c{c}") for c in range(DC)])
        if stop_after == f"l1_y{st}":
            return

    for c in range(DC):
        kb.dma(xr[:, c, :], dram["x_spill"][c], [("x_spill", f"c{c}")],
               [("xr", f"c{c}")], tag="m_xrel")
        kb.tt(xr[:, c, :], xr[:, c, :], vf[:, c, 0:TK], ALU.add,
              [("xr", f"c{c}"), ("vf", f"c{c}")], [("xr", f"c{c}")],
              tag="m_xres")
    kb.tap("l1attn_out", xr[:, :, :], [("xr", f"c{c}") for c in range(DC)])



def build_l1_ffn(kb, stop_after=None):
    T = kb.tensors
    dram = kb.dram
    xr = T["xr"]
    bq, bk, vf, bo = T["bq"], T["bk"], T["vf"], T["bo"]
    sc_a, chsc = T["sc_a"], T["chsc"]
    ident = T["identity"]
    ones_blk = T["ones_blk"]
    SCA = ("sc_a", "x")
    CHS = ("chsc", "x")
    BK3 = ("bk", "c3")
    eps_ap, eps_rd = kb.cconst(1e-5)

    for c in range(DC):
        kb.copy(sc_a[:, :], xr[:, c, :], [("xr", f"c{c}")], [SCA],
                eng="scalar", tag="e_ln_cast")
        for t in range(NT4):
            pap, pnm, pk = psl(kb, t)
            kb.mm(pap, ones_blk[:, :], sc_a[:, t * 512:(t + 1) * 512],
                  [("ones_blk", "*"), SCA], [(pnm, pk)],
                  start=(c == 0), stop=(c == DC - 1), tag="e_ln_sum")
        kb.act(chsc[:, :], xr[:, c, :], AF.Square, [("xr", f"c{c}")], [CHS],
               tag="e_ln_sq")
        for t in range(NT4):
            pap, pnm, pk = psl(kb, 4 + t)
            kb.mm(pap, ones_blk[:, :], chsc[:, t * 512:(t + 1) * 512],
                  [("ones_blk", "*"), CHS], [(pnm, pk)],
                  start=(c == 0), stop=(c == DC - 1), tag="e_ln_sqs")
    chf = chsc.ap().bitcast(F32)
    for t in range(NT4):
        pap, pnm, pk = psl(kb, t)
        pbq, pnm2, pk2 = psl(kb, 4 + t)
        kb.ts(pap, pap, 1.0 / D, ALU.mult, [(pnm, pk)], [(pnm, pk)],
              tag="e_ln_mu")
        kb.act(chf[:, 0:512], pap, AF.Square, [(pnm, pk)], [CHS],
               tag="e_ln_musq")
        kb.stt(pbq, pbq, 1.0 / D, chf[:, 0:512], ALU.mult, ALU.subtract,
               [(pnm2, pk2), CHS], [(pnm2, pk2)], tag="e_ln_var")
        kb.act(pbq, pbq, AF.Sqrt, [(pnm2, pk2), eps_rd], [(pnm2, pk2)],
               bias=eps_ap, tag="e_ln_sqrt")
        kb.recip(pbq, pbq, [(pnm2, pk2)], [(pnm2, pk2)], tag="e_ln_rcp")
    for c in range(DC):
        for t in range(NT4):
            ts_ = slice(t * 512, (t + 1) * 512)
            pap, pnm, pk = psl(kb, t)
            pbq, pnm2, pk2 = psl(kb, 4 + t)
            kb.tt(chsc[:, ts_], xr[:, c, ts_], pap, ALU.subtract,
                  [("xr", f"c{c}"), (pnm, pk)], [CHS], tag="e_ln_sub")
            kb.tt(bo[:, c, ts_], chsc[:, ts_], pbq, ALU.mult,
                  [CHS, (pnm2, pk2)], [("bo", f"c{c}")], tag="e_ln_scl")
    kb.tap("e_ln", bo[:, :, :], [("bo", f"c{c}") for c in range(DC)])
    if stop_after == "e_ln":
        return

    dpv = T["dpt"]
    wld2 = dpv[:, 1152:1792].bitcast(F32)
    wcb = dpv[:, 0:1152]
    DPW = ("dpt", "*")
    kb.dma(wld2[:, 0:64].rearrange("p (k q) -> p k q", q=8),
           dram["erw"].transpose([1, 0, 2]), [("erw", "*")], [DPW],
           tag="e_rw_ld")
    kb.copy(wcb[:, 0:64], wld2[:, 0:64], [DPW], [DPW], eng="scalar", tag="e_rw_cast")
    kb.dma(wld2[:, 64:72], dram["esgw"], [("esgw", "*")], [DPW],
           tag="e_sgw_ld")
    kb.copy(wcb[:, 64:72], wld2[:, 64:72], [DPW], [DPW], eng="scalar", tag="e_sgw_cast")
    for t in range(NT4):
        pap, pnm, pk = psl(kb, t)
        for k in range(DC):
            kb.mm(pap[0:8, :], wcb[:, k * 8:(k + 1) * 8],
                  bo[:, k, t * 512:(t + 1) * 512],
                  [DPW, ("bo", f"c{k}")], [(pnm, pk)],
                  start=(k == 0), stop=(k == DC - 1), tag="e_rt_mm")
        kb.act(bk[0:8, 1, t * 512:(t + 1) * 512], pap[0:8, :], AF.Copy,
               [(pnm, pk)], [("bk", "c1")], tag="e_rt_ev")
    bkf = chsc.ap().bitcast(F32)
    q0 = 0
    q_ = bkf[:, q0:q0 + 8]
    q2 = bkf[:, q0 + 8:q0 + 16]
    cnd = bkf[:, q0 + 16:q0 + 24]
    mk1 = bkf[:, q0 + 24:q0 + 32]
    mk2 = bkf[:, q0 + 32:q0 + 40]
    mcol = bkf[:, q0 + 40:q0 + 41]
    icol = bkf[:, q0 + 41:q0 + 42]
    it8 = bkf[:, q0 + 42:q0 + 50]
    IT8 = CHS
    BK3 = CHS
    combb = chsc[:, 200:208]
    for j in range(8):
        kb.memset(it8[:, j:j + 1], float(j), [BK3], tag="e_iota")
    for ti in range(16):
        cs = slice(ti * P, (ti + 1) * P)
        pap, pnm, pk = psl(kb, 4 + ti % 2)
        kb.prog.op("tensor",
                   lambda e, o=pap.bitcast(BF16)[:, 0:8], i=bk[0:8, 1, cs],
                   idn=ident[0:8, 0:8]: e.transpose(o, i, idn),
                   [("bk", "c1"), ("identity", "*")], [(pnm, pk)],
                   tag="e_rt_tr")
        kb.reduce(mcol, pap.bitcast(BF16)[:, 0:8], ALU.max, [(pnm, pk)],
                  [BK3], tag="e_sm_max")
        kb.ts(icol, mcol, -1.0, ALU.mult, [BK3], [BK3], tag="e_sm_neg")
        kb.act(q_, pap.bitcast(BF16)[:, 0:8], AF.Exp, [(pnm, pk), BK3],
               [BK3], bias=icol, tag="e_sm_exp")
        kb.reduce(mcol, q_, ALU.add, [BK3], [BK3], tag="e_sm_sum")
        kb.recip(mcol, mcol, [BK3], [BK3], tag="e_sm_rcp")
        kb.ts(q_, q_, mcol, ALU.mult, [BK3], [BK3], tag="e_sm_div")
        kb.reduce(mcol, q_, ALU.max, [BK3], [BK3], tag="e_t1_max")
        kb.ts(cnd, q_, mcol, ALU.is_equal, [BK3], [BK3], tag="e_t1_eq")
        kb.ts(cnd, cnd, -99.0, ALU.mult, [BK3], [BK3], s2=99.0, op1=ALU.add,
              tag="e_t1_pen")
        kb.tt(cnd, cnd, it8, ALU.add, [BK3, IT8], [BK3], tag="e_t1_sel")
        kb.reduce(icol, cnd, ALU.min, [BK3], [BK3], tag="e_t1_idx")
        kb.ts(mk1, it8, icol, ALU.is_equal, [IT8, BK3], [BK3],
              tag="e_t1_msk")
        kb.stt(q2, mk1, -1e9, q_, ALU.mult, ALU.add, [BK3], [BK3],
               tag="e_t2_q")
        kb.reduce(mcol, q2, ALU.max, [BK3], [BK3], tag="e_t2_max")
        kb.ts(cnd, q2, mcol, ALU.is_equal, [BK3], [BK3], tag="e_t2_eq")
        kb.ts(cnd, cnd, -99.0, ALU.mult, [BK3], [BK3], s2=99.0, op1=ALU.add,
              tag="e_t2_pen")
        kb.tt(cnd, cnd, it8, ALU.add, [BK3, IT8], [BK3], tag="e_t2_sel")
        kb.reduce(icol, cnd, ALU.min, [BK3], [BK3], tag="e_t2_idx")
        kb.ts(mk2, it8, icol, ALU.is_equal, [IT8, BK3], [BK3],
              tag="e_t2_msk")
        kb.tt(mk1, mk1, mk2, ALU.add, [BK3], [BK3], tag="e_cb_m")
        kb.tt(combb, q_, mk1, ALU.mult, [BK3], [BK3], tag="e_cb")
        pap2, pnm2, pk2 = psl(kb, 6 + ti % 2)
        kb.transpose(pap2.bitcast(BF16)[0:8, 0:128], combb, [BK3],
                     [(pnm2, pk2)], tag="e_cb_tr")
        kb.act(sc_a[0:8, cs], pap2.bitcast(BF16)[0:8, 0:128], AF.Copy,
               [(pnm2, pk2)], [SCA], tag="e_cb_ev")
    kb.tap("e_comb", sc_a[:, :], [SCA])
    if stop_after == "e_comb":
        return

    for t in range(NT4):
        pap, pnm, pk = psl(kb, t)
        for k in range(DC):
            kb.mm(pap[0:1, :], wcb[:, 64 + k:65 + k],
                  bo[:, k, t * 512:(t + 1) * 512],
                  [DPW, ("bo", f"c{k}")], [(pnm, pk)],
                  start=(k == 0), stop=(k == DC - 1), tag="e_sg_mm")
        kb.act(bk[0:1, 1, t * 512:(t + 1) * 512], pap[0:1, :], AF.Sigmoid,
               [(pnm, pk)], [("bk", "c1")], tag="e_sg_ev")
    for t in range(NT4):
        pap, pnm, pk = psl(kb, 4 + t)
        kb.mm(pap, ones_blk[0:1, :], bk[0:1, 1, t * 512:(t + 1) * 512],
              [("ones_blk", "*"), ("bk", "c1")], [(pnm, pk)], tag="e_sg_bc")
        kb.act(bk[:, 2, t * 512:(t + 1) * 512], pap, AF.Copy, [(pnm, pk)],
               [("bk", "c2")], tag="e_sg_bcev")

    def sgu_evac2(m, t, pap, prd):
        if m < 8:
            kb.act(bq[:, m, t * 512:(t + 1) * 512], pap, AF.Silu, [prd],
                   [("bq", f"c{m}")], tag="e_gate_ev")
        elif m < 12:
            kb.act(bk[:, 4 + (m - 8), t * 512:(t + 1) * 512], pap, AF.Copy,
                   [prd], [("bk", f"c{4 + (m - 8)}")], tag="e_up_ev")
        else:
            kb.act(vf[:, m - 12, t * 512:(t + 1) * 512], pap, AF.Copy,
                   [prd], [("vf", f"c{m - 12}")], tag="e_up_ev2")

    stream_proj(kb, "esgu", dram["esgu"], DC, 16,
                lambda k, t: (bo[:, k, t * 512:(t + 1) * 512],
                              ("bo", f"c{k}")),
                sgu_evac2, tag="e_sgu")
    for m in range(8):
        if m < 4:
            uap, urd = bk[:, 4 + m, 0:TK], ("bk", f"c{4 + m}")
        else:
            uap, urd = vf[:, m - 4, 0:TK], ("vf", f"c{m - 4}")
        kb.tt(bq[:, m, 0:TK], bq[:, m, 0:TK], uap, ALU.mult,
              [("bq", f"c{m}"), urd], [("bq", f"c{m}")], tag="e_sh")
    def sd_evac(m, t, pap, prd):
        kb.act(chsc[:, t * 512:(t + 1) * 512], pap, AF.Copy, [prd], [CHS],
               tag="e_sd_ev")
        kb.tt(vf[:, m, t * 512:(t + 1) * 512],
              chsc[:, t * 512:(t + 1) * 512],
              bk[:, 2, t * 512:(t + 1) * 512], ALU.mult,
              [CHS, ("bk", "c2")], [("vf", f"c{m}")], tag="e_sd_scl")

    stream_proj(kb, "esd", dram["esd"], DC, DC,
                lambda k, t: (bq[:, k, t * 512:(t + 1) * 512],
                              ("bq", f"c{k}")),
                sd_evac, tag="e_sd")
    kb.tap("e_shared", vf[:, :, 0:TK], [("vf", f"c{c}") for c in range(DC)])
    if stop_after == "e_shared":
        return

    kb.dma(chsc[0:16, :], dram["sel16"], [("sel16", "*")], [CHS],
           tag="e_selld")
    for e in range(8):
        for t in range(NT4):
            pap, pnm, pk = psl(kb, 4 + t)
            kb.mm(pap, chsc[0:8, e * P:(e + 1) * P],
                  sc_a[0:8, t * 512:(t + 1) * 512],
                  [CHS, SCA], [(pnm, pk)], tag="e_cbc_mm")
            kb.act(bk[:, 0, t * 512:(t + 1) * 512], pap, AF.Copy,
                   [(pnm, pk)], [("bk", "c0")], tag="e_cbc_ev")

        def w1_evac(m, t, pap, prd, e=e):
            kb.act(bq[:, m, t * 512:(t + 1) * 512], pap, AF.Silu, [prd],
                   [("bq", f"c{m}")], tag="e_h_ev")

        stream_proj(kb, "ew1", dram["ew1"][e], DC, DC,
                    lambda k, t: (bo[:, k, t * 512:(t + 1) * 512],
                                  ("bo", f"c{k}")),
                    w1_evac, tag=f"e_w1_{e}")
        for m in range(8):
            kb.tt(bq[:, m, 0:TK], bq[:, m, 0:TK], bk[:, 0, 0:TK], ALU.mult,
                  [("bq", f"c{m}"), ("bk", "c0")], [("bq", f"c{m}")],
                  tag="e_hscl")

        def w2_evac(m, t, pap, prd, e=e):
            kb.tt(vf[:, m, t * 512:(t + 1) * 512],
                  vf[:, m, t * 512:(t + 1) * 512], pap, ALU.add,
                  [("vf", f"c{m}"), prd], [("vf", f"c{m}")], tag="e_racc")

        stream_proj(kb, "ew2", dram["ew2"][e], DC, DC,
                    lambda k, t: (bq[:, k, t * 512:(t + 1) * 512],
                                  ("bq", f"c{k}")),
                    w2_evac, tag=f"e_w2_{e}")
    kb.tap("e_racc", vf[:, :, 0:TK], [("vf", f"c{c}") for c in range(DC)])

    for c in range(DC):
        kb.tt(xr[:, c, :], xr[:, c, :], vf[:, c, 0:TK], ALU.add,
              [("xr", f"c{c}"), ("vf", f"c{c}")], [("xr", f"c{c}")],
              tag="e_xres")


# version 07
# speedup vs baseline: 1.2263x; 1.2263x over previous
import numpy as np
import concourse.bass as bass
import concourse.mybir as mybir
from concourse.bass_utils import run_bass_kernel_spmd

F32 = mybir.dt.float32
BF16 = mybir.dt.bfloat16
I32 = mybir.dt.int32
AF = mybir.ActivationFunctionType
ALU = mybir.AluOpType

ENGINES = ("sync", "tensor", "vector", "scalar", "gpsimd")
import os as _os
SKIP_ENGS = tuple(x for x in (_os.environ.get("SKIP_ENGS") or "").split(",") if x)

P = 128
D = 1024
DC = D // P
TK = 2048
NT4 = TK // 512
H = 4
SC = TK // P



class Op:
    __slots__ = ("eng", "emit", "reads", "writes", "kind", "idx", "deps",
                 "published", "tag")

    def __init__(self, eng, emit, reads, writes, kind, idx, tag):
        self.eng, self.emit = eng, emit
        self.reads, self.writes = reads, writes
        self.kind = kind
        self.idx = idx
        self.deps = []
        self.published = kind in ("dma", "cc")
        self.tag = tag


class Prog:
    def __init__(self):
        self.streams = {e: [] for e in ENGINES}
        self.res = {}
        import os
        self.strict = os.environ.get("BASS_SIM_FP64") == "1"

    def op(self, eng, emit, reads=(), writes=(), kind="cmp", tag=""):
        o = Op(eng, emit, list(reads), list(writes), kind,
               len(self.streams[eng]), tag)
        wdeps = []
        rdeps = []
        for (name, key) in o.reads:
            ent = self.res.get(name)
            if ent:
                ks = list(ent.keys()) if key == "*" else [key, "*"]
                for k in ks:
                    if k in ent:
                        wdeps.append(ent[k][0])
        for (name, key) in o.writes:
            ent = self.res.get(name)
            if ent:
                ks = list(ent.keys()) if key == "*" else [key, "*"]
                for k in ks:
                    if k in ent:
                        w, readers = ent[k]
                        wdeps.append(w)
                        rdeps.extend(readers)
        seen = set()
        for p in wdeps:
            if p is None or p is o or id(p) in seen:
                continue
            seen.add(id(p))
            if (p.eng == o.eng and p.kind == "cmp" and o.kind == "cmp"
                    and (o.eng == "tensor" or
                         (not self.strict and o.eng in SKIP_ENGS))):
                continue
            o.deps.append(p)
            p.published = True
        for p in rdeps:
            if p is None or p is o or id(p) in seen:
                continue
            seen.add(id(p))
            if (p.eng == o.eng and p.kind == "cmp" and o.kind == "cmp"
                    and (o.eng == "tensor" or
                         (not self.strict and o.eng in SKIP_ENGS))):
                continue
            o.deps.append(p)
            p.published = True
        for (name, key) in o.reads:
            ent = self.res.setdefault(name, {})
            ks = (set(ent.keys()) | {"*"}) if key == "*" else {key}
            for k in ks:
                if k not in ent:
                    ent[k] = (None, [])
                ent[k][1].append(o)
        for (name, key) in o.writes:
            ent = self.res.setdefault(name, {})
            if key == "*":
                ent.clear()
            ent[key] = (o, [])
        self.streams[eng].append(o)
        return o

    def finalize(self):
        self.pubnum = {}
        for e in ENGINES:
            counts = {"cmp": 0, "dma": 0, "cc": 0}
            for o in self.streams[e]:
                if o.published:
                    counts[o.kind] += 1
                    self.pubnum[id(o)] = counts[o.kind]

    def waits_of(self, o, clock):
        out = []
        for p in o.deps:
            num = self.pubnum[id(p)]
            val = num * (16 if p.kind == "dma" else 1)
            key = (p.eng, p.kind)
            if clock.get(key, 0) >= val:
                continue
            clock[key] = val
            out.append((p.eng, p.kind, val))
        return out




class KB:
    def __init__(self, debug_taps=()):
        self.nc = bass.Bass("TRN2", target_bir_lowering=False, debug=False)
        self.prog = Prog()
        self.dram = {}
        self.dram_h = {}
        self.sb_handles = []
        self.tensors = {}
        self.debug_taps = list(debug_taps)
        self._cmap = {}

    def din(self, name, shape, dtype=F32):
        ap = self.nc.dram_tensor(name, list(shape), dtype,
                                 kind="ExternalInput").ap()
        self.dram[name] = ap
        return ap

    def dout(self, name, shape, dtype=F32):
        ap = self.nc.dram_tensor(name, list(shape), dtype,
                                 kind="ExternalOutput").ap()
        self.dram[name] = ap
        return ap

    def dscratch(self, name, shape, dtype=F32):
        h = self.nc.dram_tensor(name, list(shape), dtype)
        self.dram_h[name] = h
        self.dram[name] = h.ap()
        return h.ap()

    def sb(self, name, shape, dtype=F32):
        cm = self.nc.sbuf_tensor(name, list(shape), dtype)
        h = cm.__enter__()
        self.sb_handles.append(cm)
        self.tensors[name] = h
        return h

    def ps(self, name, shape, dtype=F32):
        cm = self.nc.psum_tensor(name, list(shape), dtype)
        h = cm.__enter__()
        self.sb_handles.append(cm)
        self.tensors[name] = h
        return h

    def cconst(self, val, dtype=F32):
        key = (float(val), dtype)
        if key not in self._cmap:
            name = f"const_{len(self._cmap)}"
            h = self.sb(name, [P, 1], dtype)
            self.memset(h[:, :], float(val), [(name, "*")], tag="const")
            self._cmap[key] = (h, name)
        h, name = self._cmap[key]
        return h[:, 0:1], (name, "*")

    def dma(self, dst, src, reads, writes, eng="sync", tag="dma"):
        self.prog.op(eng, lambda e: e.dma_start(out=dst, in_=src),
                     reads, writes, kind="dma", tag=tag)

    def mm(self, out, lhsT, rhs, reads, writes, start=True, stop=True,
           tag="mm"):
        self.prog.op("tensor",
                     lambda e: e.matmul(out, lhsT, rhs, start=start,
                                        stop=stop),
                     reads, writes, tag=tag)

    def transpose(self, out, in_, reads, writes, tag="tr"):
        ident = self.tensors["identity"]
        self.prog.op("tensor",
                     lambda e: e.transpose(out, in_, ident[:, :]),
                     list(reads) + [("identity", "*")], writes, tag=tag)

    def act(self, out, in_, func, reads, writes, bias=0.0, scale=1.0,
            tag="act"):
        self.prog.op("scalar",
                     lambda e: e.activation(out=out, in_=in_, func=func,
                                            bias=bias, scale=scale),
                     reads, writes, tag=tag)

    def tt(self, out, a, b, op, reads, writes, eng="vector", tag="tt"):
        self.prog.op(eng, lambda e: e.tensor_tensor(out, a, b, op),
                     reads, writes, tag=tag)

    def ts(self, out, a, s1, op0, reads, writes, s2=None, op1=None,
           eng="vector", tag="ts"):
        if s2 is None:
            self.prog.op(eng,
                         lambda e: e.tensor_single_scalar(out, a, s1, op0),
                         reads, writes, tag=tag)
        else:
            self.prog.op(eng,
                         lambda e: e.tensor_scalar(out, a, s1, s2, op0, op1),
                         reads, writes, tag=tag)

    def stt(self, out, in0, scalar, in1, op0, op1, reads, writes,
            eng="vector", tag="stt"):
        self.prog.op(eng,
                     lambda e: e.scalar_tensor_tensor(out, in0, scalar, in1,
                                                      op0, op1),
                     reads, writes, tag=tag)

    def copy(self, out, in_, reads, writes, eng="vector", tag="copy"):
        if eng == "scalar":
            self.prog.op(eng, lambda e: e.copy(out=out, in_=in_),
                         reads, writes, tag=tag)
        else:
            self.prog.op(eng, lambda e: e.tensor_copy(out, in_),
                         reads, writes, tag=tag)

    def memset(self, ap_, val, writes, eng="vector", tag="memset"):
        self.prog.op(eng, lambda e: e.memset(ap_, val), (), writes, tag=tag)

    def scan(self, out, d0, d1, init, reads, writes, tag="scan"):
        self.prog.op("vector",
                     lambda e: e.tensor_tensor_scan(out, d0, d1, init,
                                                    ALU.mult, ALU.add),
                     reads, writes, tag=tag)

    def reduce(self, out, in_, op, reads, writes, eng="vector", tag="red"):
        self.prog.op(eng,
                     lambda e: e.tensor_reduce(out, in_,
                                               mybir.AxisListType.X, op),
                     reads, writes, tag=tag)

    def recip(self, out, in_, reads, writes, tag="recip"):
        self.prog.op("vector", lambda e: e.reciprocal(out, in_),
                     reads, writes, tag=tag)

    def select(self, out, mask, t, f, reads, writes, tag="sel"):
        self.prog.op("vector", lambda e: e.select(out, mask, t, f),
                     reads, writes, tag=tag)

    def cc(self, cin_h, cout_h, reads, writes, tag="cc"):
        groups = [[0, 1], [2, 3], [4, 5], [6, 7]]
        self.prog.op("gpsimd",
                     lambda e: e.collective_compute(
                         "AllReduce", ALU.add, replica_groups=groups,
                         ins=[cin_h.ap().opt()], outs=[cout_h.ap().opt()]),
                     reads, writes, kind="cc", tag=tag)

    def tap(self, name, sb_ap, reads):
        if name not in self.debug_taps:
            return
        o = self.dout("dbg_" + name, list(sb_ap.shape), sb_ap.dtype)
        self.dma(o, sb_ap, reads, [("dbg_" + name, "*")], eng="scalar",
                 tag="tap")

    def emit(self):
        prog = self.prog
        prog.finalize()
        nc = self.nc
        sems = {}
        semcms = []
        for e in ENGINES:
            for kind in ("cmp", "dma", "cc"):
                if any(o.published and o.kind == kind
                       for o in prog.streams[e]):
                    cm = nc.semaphore(f"s_{e}_{kind}")
                    sems[(e, kind)] = cm.__enter__()
                    semcms.append(cm)
        blk_cm = nc.Block()
        block = blk_cm.__enter__()

        def make_runner(ename):
            stream = prog.streams[ename]

            def run(eng):
                clock = {}
                for o in stream:
                    if o.kind in ("dma", "cc"):
                        num = prog.pubnum[id(o)]
                        if num > 1:
                            amt = 16 if o.kind == "dma" else 1
                            key = (o.eng, o.kind)
                            val = (num - 1) * amt
                            if clock.get(key, 0) < val:
                                clock[key] = val
                                eng.wait_ge(sems[key], val)
                    for (pe, kind, val) in prog.waits_of(o, clock):
                        eng.wait_ge(sems[(pe, kind)], val)
                    inst = o.emit(eng)
                    if o.published:
                        amt = 16 if o.kind == "dma" else 1
                        inst.then_inc(sems[(o.eng, o.kind)], amt)
            return run

        block.sync(make_runner("sync"))
        block.tensor(make_runner("tensor"))
        block.vector(make_runner("vector"))
        block.scalar(make_runner("scalar"))
        block.gpsimd(make_runner("gpsimd"))
        blk_cm.__exit__(None, None, None)
        for cm in semcms:
            cm.__exit__(None, None, None)
        for cm in reversed(self.sb_handles):
            cm.__exit__(None, None, None)


def psl(kb, i):
    t = kb.tensors["pA"] if i < 4 else kb.tensors["pB"]
    nm = "pA" if i < 4 else "pB"
    j = i % 4
    return t[:, j * 512:(j + 1) * 512], nm, f"t{j}"




def ln_full(kb, src, src_name, xnb, xnb_name, mu_ap, mu_rd, rb_ap, rb_rd,
            eps=1e-5, nchunks=DC):
    T = kb.tensors
    ones_blk = T["ones_blk"]
    sc_a = T["sc_a"]
    for c in range(nchunks):
        sc = sc_a
        scn = "sc_a"
        kb.copy(sc[:, :], src[:, c, :], [(src_name, f"c{c}")], [(scn, "x")],
                eng="scalar", tag="ln_cast")
        for t in range(NT4):
            pap, pnm, pk = psl(kb, t)
            kb.mm(pap, ones_blk[:, :], sc[:, t * 512:(t + 1) * 512],
                  [("ones_blk", "*"), (scn, "x")], [(pnm, pk)],
                  start=(c == 0), stop=(c == nchunks - 1), tag="ln_sum")
        kb.act(sc[:, :], src[:, c, :], AF.Square, [(src_name, f"c{c}")],
               [(scn, "x")], tag="ln_sq")
        for t in range(NT4):
            pap, pnm, pk = psl(kb, 4 + t)
            kb.mm(pap, ones_blk[:, :], sc[:, t * 512:(t + 1) * 512],
                  [("ones_blk", "*"), (scn, "x")], [(pnm, pk)],
                  start=(c == 0), stop=(c == nchunks - 1), tag="ln_sqsum")
    nd = nchunks * P
    for t in range(NT4):
        ts_ = slice(t * 512, (t + 1) * 512)
        pap, pnm, pk = psl(kb, t)
        pbq, pnm2, pk2 = psl(kb, 4 + t)
        kb.ts(mu_ap[:, ts_], pap, 1.0 / nd, ALU.mult, [(pnm, pk)], [mu_rd],
              tag="ln_mu")
        kb.tt(rb_ap[:, ts_], mu_ap[:, ts_], mu_ap[:, ts_], ALU.mult,
              [mu_rd], [rb_rd], tag="ln_musq")
        kb.stt(rb_ap[:, ts_], pbq, 1.0 / nd, rb_ap[:, ts_], ALU.mult,
               ALU.subtract, [(pnm2, pk2), rb_rd], [rb_rd], tag="ln_var")
    eps_ap, eps_rd = kb.cconst(eps)
    kb.act(rb_ap[:, :], rb_ap[:, :], AF.Sqrt, [rb_rd, eps_rd], [rb_rd],
           bias=eps_ap, tag="ln_sqrt")
    kb.recip(rb_ap[:, :], rb_ap[:, :], [rb_rd], [rb_rd], tag="ln_recip")
    for c in range(nchunks):
        kb.tt(xnb[:, c, :], src[:, c, :], mu_ap[:, :], ALU.subtract,
              [(src_name, f"c{c}"), mu_rd], [(xnb_name, f"c{c}")],
              tag="ln_sub")
        kb.tt(xnb[:, c, :], xnb[:, c, :], rb_ap[:, :], ALU.mult,
              [(xnb_name, f"c{c}"), rb_rd], [(xnb_name, f"c{c}")],
              tag="ln_scale")


def stream_proj(kb, wname, w_hbm, nk, nm, rhs_fn, out_cb, extra_k=None,
                wscale=None, cast_eng="vector", bank0=0, tag="proj"):
    dp = kb.tensors["dpt"]
    wld2 = dp[:, 1152:1792].bitcast(F32)
    wcb = dp[:, 0:1152]
    DPW = ("dpt", "*")
    for m in range(nm):
        if wscale is None and m % 2 == 0:
            kb.dma(wcb[:, 0:nk * 128].rearrange("p (k q) -> p k q", q=P),
                   w_hbm[m].transpose([1, 0, 2]),
                   [(wname, "*")], [DPW], eng="gpsimd", tag=tag + "_ld")
        else:
            for half in range((nk + 1) // 2):
                kcs = list(range(2 * half, min(2 * half + 2, nk)))
                w = len(kcs) * 128
                kb.dma(wld2[:, 0:w].rearrange("p (k q) -> p k q", q=P),
                       w_hbm[m, kcs[0]:kcs[0] + len(kcs)]
                       .transpose([1, 0, 2]),
                       [(wname, "*")], [DPW], tag=tag + "_ld")
                if wscale is None:
                    kb.copy(wcb[:, kcs[0] * 128:kcs[0] * 128 + w],
                            wld2[:, 0:w], [DPW], [DPW], eng="scalar",
                            tag=tag + "_cast")
                else:
                    fn, frd = wscale
                    for j, k in enumerate(kcs):
                        kb.ts(wcb[:, k * 128:(k + 1) * 128],
                              wld2[:, j * 128:(j + 1) * 128], fn(k),
                              ALU.mult, [DPW, frd], [DPW], eng=cast_eng,
                              tag=tag + "_cast")
        for t in range(NT4):
            pap, pnm, pk = psl(kb, bank0 + (m % 2) * 2 + (t % 2))
            for k in range(nk):
                rhs, rrd = rhs_fn(k, t)
                kb.mm(pap, wcb[:, k * 128:(k + 1) * 128], rhs,
                      [DPW, rrd], [(pnm, pk)],
                      start=(k == 0),
                      stop=(k == nk - 1 and extra_k is None),
                      tag=tag + "_mm")
            if extra_k is not None:
                extra_k(m, t, pap, (pnm, pk))
            out_cb(m, t, pap, (pnm, pk))




def build_l0_attn(kb, stop_after=None):
    T = kb.tensors
    dram = kb.dram
    xr = T["xr"]
    xrb = xr.ap().bitcast(BF16)
    bq, bk, vf, bo = T["bq"], T["bk"], T["vf"], T["bo"]
    sc_a = T["sc_a"]
    dp = T["dpt"]
    chsc = T["chsc"]
    ones_blk = T["ones_blk"]
    aux = T["aux_s"]
    PAR, NPAR = aux[:, 0:1], aux[:, 1:2]
    EPS5, EPS6 = aux[:, 3:4], aux[:, 4:5]
    AUX = ("aux_s", "*")

    def xnb_ap(c):
        return (xrb[:, c // 2, (c % 2) * TK:(c % 2 + 1) * TK],
                ("xr", f"c{c // 2}"))

    bo_flat = bo.ap().rearrange("p c t -> p (c t)").bitcast(F32)
    mu_ap = bo_flat[:, 0:TK]
    rb_ap = bo_flat[:, TK:2 * TK]
    nrmq = bo_flat[:, 2 * TK:3 * TK]
    nrmk = bo_flat[:, 3 * TK:4 * TK]
    MU = [("bo", "c0"), ("bo", "c1")]
    RB = [("bo", "c2"), ("bo", "c3")]
    NQ = [("bo", "c4"), ("bo", "c5")]
    NK_ = [("bo", "c6"), ("bo", "c7")]

    def slabf(s):
        return xr[:, s, :], ("xr", f"c{s}")
    betab, BB = slabf(4)
    ccs, CCS = slabf(5)
    sst, SST = slabf(6)
    rst, RST = slabf(7)

    for c in range(DC):
        kb.copy(sc_a[:, :], xr[:, c, :], [("xr", f"c{c}")], [("sc_a", "x")],
                eng="scalar", tag="ln_cast")
        for t in range(NT4):
            pap, pnm, pk = psl(kb, t)
            kb.mm(pap, ones_blk[:, :], sc_a[:, t * 512:(t + 1) * 512],
                  [("ones_blk", "*"), ("sc_a", "x")], [(pnm, pk)],
                  start=(c == 0), stop=(c == DC - 1), tag="ln_sum")
        kb.act(sc_a[:, :], xr[:, c, :], AF.Square, [("xr", f"c{c}")],
               [("sc_a", "x")], tag="ln_sq")
        for t in range(NT4):
            pap, pnm, pk = psl(kb, 4 + t)
            kb.mm(pap, ones_blk[:, :], sc_a[:, t * 512:(t + 1) * 512],
                  [("ones_blk", "*"), ("sc_a", "x")], [(pnm, pk)],
                  start=(c == 0), stop=(c == DC - 1), tag="ln_sqsum")
    for t in range(NT4):
        ts_ = slice(t * 512, (t + 1) * 512)
        pap, pnm, pk = psl(kb, t)
        pbq, pnm2, pk2 = psl(kb, 4 + t)
        kb.ts(mu_ap[:, ts_], pap, 1.0 / D, ALU.mult, [(pnm, pk)], MU,
              tag="ln_mu")
        kb.tt(rb_ap[:, ts_], mu_ap[:, ts_], mu_ap[:, ts_], ALU.mult, MU, RB,
              tag="ln_musq")
        kb.stt(rb_ap[:, ts_], pbq, 1.0 / D, rb_ap[:, ts_], ALU.mult,
               ALU.subtract, [(pnm2, pk2)] + RB, RB, tag="ln_var")
    kb.act(rb_ap[:, :], rb_ap[:, :], AF.Sqrt, RB + [AUX], RB, bias=EPS5,
           tag="ln_sqrt")
    kb.recip(rb_ap[:, :], rb_ap[:, :], RB, RB, tag="ln_recip")
    for c in range(DC):
        kb.tt(sc_a[:, :], xr[:, c, :], mu_ap[:, :], ALU.subtract,
              [("xr", f"c{c}")] + MU, [("sc_a", "x")], tag="ln_sub")
        kb.dma(dram["x_spill"][c], xr[:, c, :], [("xr", f"c{c}")],
               [("x_spill", f"c{c}")], eng="scalar", tag="spill")
        xnbc, xnbrd = xnb_ap(c)
        kb.tt(xnbc, sc_a[:, :], rb_ap[:, :], ALU.mult, [("sc_a", "x")] + RB,
              [xnbrd], tag="ln_scale")

    kb.tap("ln1", xrb[:, 0:4, :], [("xr", f"c{i}") for i in range(4)])
    kb.tap("rb", rb_ap[:, :], RB)
    if stop_after == "ln1":
        return

    def rhs_xnb(k, t):
        ap_, rd_ = xnb_ap(k)
        return ap_[:, t * 512:(t + 1) * 512], rd_

    def evac_ext(dst, dnm, off):
        def cb(m, t, pap, prd):
            kb.copy(dst[:, m, off + t * 512:off + (t + 1) * 512], pap,
                    [prd], [(dnm, f"c{m}")], tag="proj_ev")
        return cb

    if stop_after != "beta_only":
        stream_proj(kb, "wq", dram["wq"], DC, DC, rhs_xnb,
                    evac_ext(bq, "bq", 3), tag="wq")
        stream_proj(kb, "wk", dram["wk"], DC, DC, rhs_xnb,
                    evac_ext(bk, "bk", 3), tag="wk")
        stream_proj(kb, "wv", dram["wv"], DC, DC, rhs_xnb,
                    evac_ext(vf, "vf", 63), tag="wv")

    wld2 = dp[:, 1152:1792].bitcast(F32)
    wcb = dp[:, 0:1152]
    DPW = ("dpt", "*")
    kb.dma(wld2[:, 0:32].rearrange("p (k q) -> p k q", q=4),
           dram["wb"].transpose([1, 0, 2]), [("wb", "*")], [DPW],
           tag="wb_ld")
    kb.copy(wcb[:, 0:32], wld2[:, 0:32], [DPW], [DPW], tag="wb_cast")
    for t in range(NT4):
        pap, pnm, pk = psl(kb, 4 + (t % 2))
        for k in range(DC):
            rhs, rrd = rhs_xnb(k, t)
            kb.mm(pap[0:4, :], wcb[:, k * 4:(k + 1) * 4], rhs,
                  [DPW, rrd], [(pnm, pk)], start=(k == 0),
                  stop=(k == DC - 1), tag="wb_mm")
        kb.copy(sc_a[0:4, t * 512:(t + 1) * 512], pap[0:4, :],
                [(pnm, pk)], [("sc_a", "x")], tag="wb_ev")
    kb.tap("wbev", sc_a[0:4, :], [("sc_a", "x")])
    kb.act(chsc[0:4, :], sc_a[0:4, :], AF.Sigmoid, [("sc_a", "x")],
           [("chsc", "x")], tag="beta_sig")
    BETA = ("chsc", "x")
    kb.tap("beta", chsc[0:4, :], [BETA])
    if stop_after in ("beta", "beta_only"):
        return

    exts = [(bq, "bq", 3), (bk, "bk", 3), (vf, "vf", 63)]
    for i, (ext, enm, off) in enumerate(exts):
        for c in range(DC):
            kb.ts(ccs[:, c * 9 + i * 3:c * 9 + i * 3 + 3],
                  ext[:, c, off + TK - 3:off + TK], NPAR, ALU.mult,
                  [(enm, f"c{c}"), AUX], [CCS], tag="cc1_st")
    kb.dma(dram["cc1i"], ccs[:, 0:72], [CCS], [("cc1i", "*")], eng="scalar")
    kb.cc(kb.dram_h["cc1i"], kb.dram_h["cc1o"], [("cc1i", "*")],
          [("cc1o", "*")])
    kb.dma(ccs[:, 0:72], dram["cc1o"], [("cc1o", "*")], [CCS], eng="scalar")
    for i, (ext, enm, off) in enumerate(exts):
        for c in range(DC):
            kb.ts(ext[:, c, off - 3:off],
                  ccs[:, c * 9 + i * 3:c * 9 + i * 3 + 3], PAR, ALU.mult,
                  [CCS, AUX], [(enm, f"c{c}")], tag="cc1_pf")

    cacc = mu_ap
    CACC = MU
    for (ext, enm, off, taps, tnm) in [(bq, "bq", 3, T["cq"], "cq"),
                                       (bk, "bk", 3, T["ck"], "ck"),
                                       (vf, "vf", 63, T["cv"], "cv")]:
        for c in range(DC):
            kb.ts(cacc[:, :], ext[:, c, off:off + TK], taps[:, c, 3:4],
                  ALU.mult, [(enm, f"c{c}"), (tnm, "*")], CACC, tag="cv_t0")
            for tau in range(1, 4):
                kb.stt(cacc[:, :], ext[:, c, off - tau:off + TK - tau],
                       taps[:, c, 3 - tau:4 - tau], cacc[:, :], ALU.mult,
                       ALU.add, [(enm, f"c{c}"), (tnm, "*")] + CACC,
                       CACC, tag="cv_t")
            kb.act(ext[:, c, off:off + TK], cacc[:, :], AF.Silu,
                   CACC, [(enm, f"c{c}")], tag="cv_silu")

    kb.tap("qsilu", bq[:, :, 3:3 + TK], [("bq", f"c{c}") for c in range(DC)])
    kb.tap("vsilu", vf[:, :, 63:63 + TK], [("vf", f"c{c}") for c in range(DC)])

    for c in range(DC):
        kb.ts(ccs[:, c * 63:(c + 1) * 63], vf[:, c, TK:63 + TK], NPAR,
              ALU.mult, [("vf", f"c{c}"), AUX], [CCS], tag="cc2_st")
    kb.dma(dram["cc2i"], ccs[:, 0:DC * 63], [CCS], [("cc2i", "*")],
           eng="scalar")
    kb.cc(kb.dram_h["cc2i"], kb.dram_h["cc2o"], [("cc2i", "*")],
          [("cc2o", "*")])
    kb.dma(ccs[:, 0:DC * 63], dram["cc2o"], [("cc2o", "*")], [CCS],
           eng="scalar")
    for c in range(DC):
        kb.ts(vf[:, c, 0:63], ccs[:, c * 63:(c + 1) * 63], PAR, ALU.mult,
              [CCS, AUX], [("vf", f"c{c}")], tag="cc2_pf")

    for h in range(H):
        c0, c1 = 2 * h, 2 * h + 1
        for col, (ext, enm, nap, nrd) in enumerate(
                [(bq, "bq", nrmq, NQ), (bk, "bk", nrmk, NK_)]):
            for ci, c in enumerate((c0, c1)):
                kb.act(sc_a[:, :], ext[:, c, 3:3 + TK], AF.Square,
                       [(enm, f"c{c}")], [("sc_a", "x")], tag="l2_sq")
                for t in range(NT4):
                    pap, pnm, pk = psl(kb, col * 4 + t)
                    kb.mm(pap, ones_blk[:, :], sc_a[:, t * 512:(t + 1) * 512],
                          [("ones_blk", "*"), ("sc_a", "x")], [(pnm, pk)],
                          start=(ci == 0), stop=(ci == 1), tag="l2_mm")
            for t in range(NT4):
                pap, pnm, pk = psl(kb, col * 4 + t)
                kb.act(nap[:, t * 512:(t + 1) * 512], pap, AF.Sqrt,
                       [(pnm, pk), AUX], nrd, bias=EPS6, tag="l2_sqrt")
            kb.recip(nap[:, :], nap[:, :], nrd, nrd, tag="l2_recip")
        for ci, c in enumerate((c0, c1)):
            kb.tt(bq[:, c, 3:3 + TK], bq[:, c, 3:3 + TK], nrmq[:, :],
                  ALU.mult, [("bq", f"c{c}")] + NQ, [("bq", f"c{c}")],
                  tag="l2_apq")
            kb.tt(bk[:, c, 3:3 + TK], bk[:, c, 3:3 + TK], nrmk[:, :],
                  ALU.mult, [("bk", f"c{c}")] + NK_, [("bk", f"c{c}")],
                  tag="l2_apk")
    kb.tap("qn", bq[:, :, 3:3 + TK], [("bq", f"c{c}") for c in range(DC)])
    if stop_after == "qn":
        return

    DPo = {"Am": (0, 128), "Atm": (128, 128), "Pt": (256, 128),
           "Qt": (384, 128), "St": (512, 128), "vbt": (640, 256),
           "kbt": (896, 256), "knt": (1152, 256), "utm": (1408, 256),
           "wfm": (1664, 256), "uttm": (256, 256), "sbf": (640, 512),
           "rbfb": (640, 512), "dcol": (0, 128)}

    def dpt(nm):
        off, w = DPo[nm]
        base_slots = [0, 128, 256, 384, 512, 640, 896, 1152, 1408, 1664, 1920]
        rds = [("dpt", f"r{bs}") for bs in base_slots
               if bs < off + w and bs + (256 if bs >= 640 else 128) > off]
        return dp[:, off:off + w], rds

    sbf, SBF = dpt("sbf")
    rbfb, RBF = dpt("rbfb")
    ident = T["identity"]
    mask_sl, mask_iu = T["mask_sl_s"], T["mask_iu_s"]

    def q_fm(h, kc, s):
        c = 2 * h + kc
        return bq[:, c, 3 + s * P:3 + (s + 1) * P], ("bq", f"c{c}")

    def k_fm(h, kc, s):
        c = 2 * h + kc
        return bk[:, c, 3 + s * P:3 + (s + 1) * P], ("bk", f"c{c}")

    def v_fm(h, kc, s):
        c = 2 * h + kc
        return vf[:, c, 63 + s * P:63 + (s + 1) * P], ("vf", f"c{c}")

    kb.memset(sst[:, :], 0.0, [SST], tag="s0")

    def prep_phase(h, s):
        Am, Am_r = dpt("Am")
        Atm, Atm_r = dpt("Atm")
        Pt, Pt_r = dpt("Pt")
        Qt, Qt_r = dpt("Qt")
        St, St_r = dpt("St")
        vbt, vbt_r = dpt("vbt")
        kbt, kbt_r = dpt("kbt")
        knt, knt_r = dpt("knt")
        utm, utm_r = dpt("utm")
        wfm, wfm_r = dpt("wfm")
        b6, n6, k6 = psl(kb, 6)
        b7, n7, k7 = psl(kb, 7)
        b3, n3, k3 = psl(kb, 3)
        b4, n4, k4 = psl(kb, 4)
        b3b = b3.bitcast(BF16)
        kbet = []
        for kc in range(2):
            kf, krd = k_fm(h, kc, s)
            dst = sc_a[:, 1536 + kc * P:1536 + (kc + 1) * P]
            kb.tt(dst, kf, betab[:, s * P:(s + 1) * P], ALU.mult,
                  [krd, BB], [("sc_a", "x")], tag="d_kbet")
            kbet.append((dst, ("sc_a", "x")))
        for kc in range(2):
            kbe, kbe_r = kbet[kc]
            kf, krd = k_fm(h, kc, s)
            kb.mm(b6[:, 0:P], kbe, kf, [kbe_r, krd], [(n6, k6)],
                  start=(kc == 0), stop=(kc == 1), tag="d_A")
        kb.tt(Am[:, :], b6[:, 0:P], mask_sl[:, :], ALU.mult,
              [(n6, k6), ("mask_sl_s", "*")], Am_r, tag="d_Amask")
        kb.transpose(b6.bitcast(BF16)[:, P:2 * P], Am[:, :], Am_r,
                     [(n6, k6)], tag="d_Atr")
        kb.copy(Atm[:, :], b6.bitcast(BF16)[:, P:2 * P], [(n6, k6)], Atm_r,
                tag="d_Atev")
        kb.mm(b7[:, 0:P], Atm[:, :], Am[:, :], Atm_r + Am_r, [(n7, k7)],
              tag="d_P0")
        kb.mm(b7[:, P:2 * P], Am[:, :], Atm[:, :], Am_r + Atm_r, [(n7, k7)],
              tag="d_Q0")
        kb.copy(Pt[:, :], b7[:, 0:P], [(n7, k7)], Pt_r, tag="d_P0ev")
        kb.copy(Qt[:, :], b7[:, P:2 * P], [(n7, k7)], Qt_r, tag="d_Q0ev")
        kb.tt(St[:, :], ident[:, :], Atm[:, :], ALU.subtract,
              [("identity", "*")] + Atm_r, St_r, tag="d_S0")
        for it in range(6):
            kb.mm(b6[:, 0:P], Pt[:, :], St[:, :], Pt_r + St_r, [(n6, k6)],
                  tag="d_itS")
            if it < 5:
                kb.mm(b6[:, P:2 * P], Qt[:, :], Pt[:, :], Qt_r + Pt_r,
                      [(n6, k6)], tag="d_itP")
                kb.mm(b7[:, 0:P], Pt[:, :], Qt[:, :], Pt_r + Qt_r,
                      [(n7, k7)], tag="d_itQ")
            kb.tt(St[:, :], St[:, :], b6[:, 0:P], ALU.add,
                  St_r + [(n6, k6)], St_r, tag="d_itSev")
            if it < 5:
                kb.copy(Pt[:, :], b6[:, P:2 * P], [(n6, k6)], Pt_r,
                        tag="d_itPev")
                kb.copy(Qt[:, :], b7[:, 0:P], [(n7, k7)], Qt_r,
                        tag="d_itQev")
        for kc in range(2):
            vfa, vrd = v_fm(h, kc, s)
            dst = sc_a[:, 1024:1024 + P]
            kb.tt(dst, vfa, betab[:, s * P:(s + 1) * P], ALU.mult,
                  [vrd, BB], [("sc_a", "x")], tag="d_vb")
            kb.transpose(b3b[:, 0:P], dst, [("sc_a", "x")], [(n3, k3)],
                         tag="d_vbt")
            kb.copy(vbt[:, kc * P:(kc + 1) * P], b3b[:, 0:P], [(n3, k3)],
                    vbt_r, tag="d_vbtev")
            kbe, kbe_r = kbet[kc]
            kb.transpose(b3b[:, P:2 * P], kbe, [kbe_r], [(n3, k3)],
                         tag="d_kbt")
            kb.copy(kbt[:, kc * P:(kc + 1) * P], b3b[:, P:2 * P], [(n3, k3)],
                    kbt_r, tag="d_kbtev")
            kf, krd = k_fm(h, kc, s)
            kb.transpose(b3b[:, 2 * P:3 * P], kf, [krd], [(n3, k3)],
                         tag="d_knt")
            kb.copy(knt[:, kc * P:(kc + 1) * P], b3b[:, 2 * P:3 * P],
                    [(n3, k3)], knt_r, tag="d_kntev")
        kb.mm(b4[:, 0:256], St[:, :], vbt[:, :], St_r + vbt_r, [(n4, k4)],
              tag="d_u")
        kb.copy(utm[:, :], b4[:, 0:256], [(n4, k4)], utm_r, tag="d_uev")
        for half in range(2):
            kb.mm(b4[:, 256 + half * P:256 + (half + 1) * P],
                  kbt[:, half * P:(half + 1) * P], St[:, :],
                  kbt_r + St_r, [(n4, k4)], tag="d_wfm")
        kb.copy(wfm[:, :], b4[:, 256:512], [(n4, k4)], wfm_r, tag="d_wfmev")
        attnT, attnT_r = dpt("Am")
        for kc in range(2):
            kf, krd = k_fm(h, kc, s)
            qf, qrd = q_fm(h, kc, s)
            kb.mm(b6[:, 0:P], kf, qf, [krd, qrd], [(n6, k6)],
                  start=(kc == 0), stop=(kc == 1), tag="d_attn")
        kb.tt(attnT[:, :], b6[:, 0:P], mask_iu[:, :], ALU.mult,
              [(n6, k6), ("mask_iu_s", "*")], attnT_r, tag="d_attnm")
        u = s * H + h
        kb.dma(dram["w_spill"][u, :, 0:256], wfm[:, :], wfm_r,
               [("w_spill", f"u{u}")], eng="scalar", tag="wspill")
        kb.dma(dram["w_spill"][u, :, 256:512], knt[:, :], knt_r,
               [("w_spill", f"u{u}")], eng="scalar", tag="wspill")
        kb.dma(dram["w_spill"][u, :, 512:640], attnT[:, :], attnT_r,
               [("w_spill", f"u{u}")], eng="scalar", tag="wspill")

    def chain_phase(h, s):
        knt, knt_r = dpt("knt")
        utm, utm_r = dpt("utm")
        wfm, wfm_r = dpt("wfm")
        uttm, uttm_r = dpt("uttm")
        attnT, attnT_r = dpt("Am")
        b0, n0, k0 = psl(kb, 0)
        b1, n1, k1 = psl(kb, 1)
        b2, n2, k2 = psl(kb, 2)
        hs = slice(h * 512, (h + 1) * 512)
        if s > 0:
            kb.copy(sbf[:, :], sst[:, hs], [SST], SBF, tag="ch_sbf")
            for kc in range(2):
                kb.mm(b0[:, 0:256], wfm[:, kc * P:(kc + 1) * P],
                      sbf[:, kc * 256:(kc + 1) * 256], wfm_r + SBF,
                      [(n0, k0)], start=(kc == 0), stop=(kc == 1),
                      tag="ch_wS")
            kb.tt(uttm[:, :], utm[:, :], b0[:, 0:256], ALU.subtract,
                  utm_r + [(n0, k0)], uttm_r, tag="ch_ut")
        else:
            kb.copy(uttm[:, :], utm[:, :], utm_r, uttm_r, tag="ch_ut0")
        for half in range(2):
            reg = b1[:, half * P:(half + 1) * P]
            first = True
            if s > 0:
                for kc in range(2):
                    qf, qrd = q_fm(h, kc, s)
                    kb.mm(reg, sbf[:, kc * 256 + half * P:kc * 256 + (half + 1) * P],
                          qf, SBF + [qrd], [(n1, k1)],
                          start=first, stop=False, tag="ch_oq")
                    first = False
            kb.mm(reg, uttm[:, half * P:(half + 1) * P], attnT[:, :],
                  uttm_r + attnT_r, [(n1, k1)], start=first, stop=True,
                  tag="ch_oa")
        for half in range(2):
            c = 2 * h + half
            kb.copy(bo[:, c, s * P:(s + 1) * P],
                    b1[:, half * P:(half + 1) * P], [(n1, k1)],
                    [("bo", f"c{c}")], tag="ch_oev")
        for kc in range(2):
            kb.mm(b2[:, kc * 256:(kc + 1) * 256],
                  knt[:, kc * P:(kc + 1) * P], uttm[:, :],
                  knt_r + uttm_r, [(n2, k2)], tag="ch_Supd")
        kb.tt(sst[:, hs], sst[:, hs], b2[:, :], ALU.add, [SST, (n2, k2)],
              [SST], tag="ch_Sadd")

    for h in range(H):
        kb.dma(sc_a[0:1, 0:TK], chsc[h:h + 1, :], [BETA], [("sc_a", "x")],
               eng="scalar", tag="bb_row")
        for t in range(NT4):
            pap, pnm, pk = psl(kb, 5)
            kb.mm(pap, ones_blk[0:1, :], sc_a[0:1, t * 512:(t + 1) * 512],
                  [("ones_blk", "*"), ("sc_a", "x")], [(pnm, pk)],
                  tag="bb_mm")
            kb.copy(betab[:, t * 512:(t + 1) * 512], pap, [(pnm, pk)], [BB],
                    tag="bb_ev")
        for s in range(SC):
            prep_phase(h, s)
            chain_phase(h, s)

    kb.tap("delta_o", bo[:, :, :], [("bo", f"c{c}") for c in range(DC)])
    if stop_after == "delta_o":
        return

    kb.ts(ccs[:, :], sst[:, :], NPAR, ALU.mult, [SST, AUX], [CCS],
          tag="cc3_st")
    kb.dma(dram["cc3i"], ccs[:, :], [CCS], [("cc3i", "*")], eng="scalar")
    kb.cc(kb.dram_h["cc3i"], kb.dram_h["cc3o"], [("cc3i", "*")],
          [("cc3o", "*")])
    kb.dma(ccs[:, :], dram["cc3o"], [("cc3o", "*")], [CCS], eng="scalar")
    kb.ts(rst[:, :], ccs[:, :], PAR, ALU.mult, [CCS, AUX], [RST],
          tag="cc3_R0")

    def corr_phase(h, s):
        wfm, wfm_r = dpt("wfm")
        knt, knt_r = dpt("knt")
        attnT, attnT_r = dpt("Am")
        wrn, wrn_r = dpt("uttm")
        u = s * H + h
        kb.dma(wfm[:, :], dram["w_spill"][u, :, 0:256],
               [("w_spill", f"u{u}")], wfm_r, tag="wload")
        kb.dma(knt[:, :], dram["w_spill"][u, :, 256:512],
               [("w_spill", f"u{u}")], knt_r, tag="wload")
        kb.dma(attnT[:, :], dram["w_spill"][u, :, 512:640],
               [("w_spill", f"u{u}")], attnT_r, tag="wload")
        b0, n0, k0 = psl(kb, 0)
        b1, n1, k1 = psl(kb, 1)
        b2, n2, k2 = psl(kb, 2)
        hs = slice(h * 512, (h + 1) * 512)
        kb.copy(rbfb[:, :], rst[:, hs], [RST], RBF, tag="co_rbf")
        for kc in range(2):
            kb.mm(b0[:, 0:256], wfm[:, kc * P:(kc + 1) * P],
                  rbfb[:, kc * 256:(kc + 1) * 256], wfm_r + RBF,
                  [(n0, k0)], start=(kc == 0), stop=(kc == 1), tag="co_wR")
        kb.ts(wrn[:, :], b0[:, 0:256], -1.0, ALU.mult, [(n0, k0)], wrn_r,
              tag="co_wRn")
        for half in range(2):
            reg = b1[:, half * P:(half + 1) * P]
            for kc in range(2):
                qf, qrd = q_fm(h, kc, s)
                kb.mm(reg, rbfb[:, kc * 256 + half * P:kc * 256 + (half + 1) * P],
                      qf, RBF + [qrd], [(n1, k1)], start=(kc == 0),
                      stop=False, tag="co_q")
            kb.mm(reg, wrn[:, half * P:(half + 1) * P], attnT[:, :],
                  wrn_r + attnT_r, [(n1, k1)], start=False, stop=True,
                  tag="co_a")
        for half in range(2):
            c = 2 * h + half
            kb.tt(bo[:, c, s * P:(s + 1) * P], bo[:, c, s * P:(s + 1) * P],
                  b1[:, half * P:(half + 1) * P], ALU.add,
                  [("bo", f"c{c}"), (n1, k1)], [("bo", f"c{c}")],
                  tag="co_oadd")
        if s < SC - 1:
            for kc in range(2):
                kb.mm(b2[:, kc * 256:(kc + 1) * 256],
                      knt[:, kc * P:(kc + 1) * P], wrn[:, :],
                      knt_r + wrn_r, [(n2, k2)], tag="co_Rupd")
            kb.tt(rst[:, hs], rst[:, hs], b2[:, :], ALU.add,
                  [RST, (n2, k2)], [RST], tag="co_Radd")

    for h in range(H):
        for s in range(SC):
            corr_phase(h, s)

    kb.tap("delta_corr", bo[:, :, :], [("bo", f"c{c}") for c in range(DC)])
    if stop_after == "delta_corr":
        return

    firs = T["firs"]
    itcol = T["itcol"]
    strows = chsc[0:16, :]
    STR = ("chsc", "x")
    wcbK = DPW

    kb.act(itcol[0:1, 0:H], T["ltm_s"][:, :], AF.Sigmoid, [("ltm_s", "*")],
           [("itcol", "row")], scale=-1.0, tag="it_sig")
    kb.act(itcol[0:1, 0:H], itcol[0:1, 0:H], AF.Ln, [("itcol", "row")],
           [("itcol", "row")], tag="it_ln")
    kb.ts(itcol[0:1, 0:H], itcol[0:1, 0:H], -1.0, ALU.mult,
          [("itcol", "row")], [("itcol", "row")], s2=1e-4, op1=ALU.add,
          tag="it_aff")
    kb.recip(itcol[0:1, 0:H], itcol[0:1, 0:H], [("itcol", "row")],
             [("itcol", "row")], tag="it_rec")
    kb.act(itcol[0:1, H:2 * H], T["crl_s"][:, :], AF.Sigmoid,
           [("crl_s", "*")], [("itcol", "row")], tag="crl_sig")
    kb.ts(itcol[0:1, H:2 * H], itcol[0:1, H:2 * H], 0.5, ALU.mult,
          [("itcol", "row")], [("itcol", "row")], tag="crl_half")
    ITR = ("itcol", "row")

    b1c = T["b1c_s"]
    firlc = dp[:, 256:384].bitcast(F32)
    FLC = [("dpt", "r256"), ("dpt", "r384")]
    for h in range(H):
        c0, c1 = 2 * h, 2 * h + 1
        dcol, dcol_r = dpt("dcol")
        for ci, c in enumerate((c0, c1)):
            fdst = bk[:, 4 + ci, 0:TK]
            FK = ("bk", f"c{4 + ci}")
            kb.ts(fdst, vf[:, c, 63:63 + TK], firs[:, c, 4:5], ALU.mult,
                  [("vf", f"c{c}"), ("firs", "*")], [FK], tag="fs_t0")
            for tau in range(1, 5):
                kb.stt(fdst, vf[:, c, 63 - tau:63 + TK - tau],
                       firs[:, c, 4 - tau:5 - tau], fdst, ALU.mult, ALU.add,
                       [("vf", f"c{c}"), ("firs", "*"), FK], [FK], tag="fs_t")
            ldst = bk[:, 6 + ci, 0:TK]
            LK = ("bk", f"c{6 + ci}")
            kb.dma(firlc[:, :], dram["fir_l"][:, c, :], [("fir_l", "*")],
                   FLC, tag="fl_ld")
            for tau in range(64):
                kb.ts(dcol[:, :], ident[:, :], firlc[:, 63 - tau:64 - tau],
                      ALU.mult, [("identity", "*")] + FLC,
                      dcol_r, tag="fl_diag")
                for t in range(NT4):
                    pap, pnm, pk = psl(kb, 4 + t)
                    kb.mm(pap, dcol[:, :],
                          vf[:, c, 63 + t * 512 - tau:63 + (t + 1) * 512 - tau],
                          dcol_r + [("vf", f"c{c}")], [(pnm, pk)],
                          start=(tau == 0), stop=(tau == 63), tag="fl_mm")
            for t in range(NT4):
                pap, pnm, pk = psl(kb, 4 + t)
                kb.copy(ldst[:, t * 512:(t + 1) * 512], pap, [(pnm, pk)],
                        [LK], tag="fl_ev")

        def branch_ap(bi, ci):
            c = 2 * h + ci
            if bi == 0:
                return bk[:, 4 + ci, 0:TK], ("bk", f"c{4 + ci}")
            if bi == 1:
                return bk[:, 6 + ci, 0:TK], ("bk", f"c{6 + ci}")
            if bi == 2:
                return bo[:, c, :], ("bo", f"c{c}")
            return vf[:, c, 63:63 + TK], ("vf", f"c{c}")
        for bi in range(4):
            for ci in range(2):
                bap, brd = branch_ap(bi, ci)
                for t in range(NT4):
                    pap, pnm, pk = psl(kb, t)
                    kb.mm(pap[0:1, :], ones_blk[:, 0:1],
                          bap[:, t * 512:(t + 1) * 512],
                          [("ones_blk", "*"), brd], [(pnm, pk)],
                          start=(ci == 0), stop=(ci == 1), tag="st_sum")
                kb.act(sc_a[:, :], bap, AF.Square, [brd], [("sc_a", "x")],
                       tag="st_sq")
                for t in range(NT4):
                    pap, pnm, pk = psl(kb, 4 + t)
                    kb.mm(pap[0:1, :], ones_blk[:, 0:1],
                          sc_a[:, t * 512:(t + 1) * 512],
                          [("ones_blk", "*"), ("sc_a", "x")], [(pnm, pk)],
                          start=(ci == 0), stop=(ci == 1), tag="st_sqsum")
            for t in range(NT4):
                ts_ = slice(t * 512, (t + 1) * 512)
                pap, pnm, pk = psl(kb, t)
                pbq, pnm2, pk2 = psl(kb, 4 + t)
                row = sc_a[0:1, :]
                kb.ts(row[:, 0:512], pap[0:1, :], 1.0 / 256.0, ALU.mult,
                      [(pnm, pk)], [("sc_a", "x")], tag="st_mean")
                kb.tt(row[:, 1536:2048], row[:, 0:512], row[:, 0:512],
                      ALU.mult, [("sc_a", "x")], [("sc_a", "x")],
                      tag="st_musq")
                kb.stt(row[:, 512:1024], pbq[0:1, :], 1.0 / 256.0,
                       row[:, 1536:2048], ALU.mult, ALU.subtract,
                       [(pnm2, pk2), ("sc_a", "x")], [("sc_a", "x")],
                       tag="st_var")
                kb.act(row[:, 1024:1536], pbq[0:1, :], AF.Sqrt,
                       [(pnm2, pk2)], [("sc_a", "x")], tag="st_l2")
                for j, r in [(0, 0), (1, 1), (2, 3)]:
                    kb.dma(strows[4 * bi + r:4 * bi + r + 1, ts_],
                           row[:, j * 512:(j + 1) * 512], [("sc_a", "x")],
                           [STR], eng="scalar", tag="st_row")
            for ci in range(2):
                bap, brd = branch_ap(bi, ci)
                kb.act(sc_a[:, :], bap, AF.Abs, [brd], [("sc_a", "x")],
                       tag="st_abs")
                for t in range(NT4):
                    pap, pnm, pk = psl(kb, t)
                    kb.mm(pap[0:1, :], ones_blk[:, 0:1],
                          sc_a[:, t * 512:(t + 1) * 512],
                          [("ones_blk", "*"), ("sc_a", "x")], [(pnm, pk)],
                          start=(ci == 0), stop=(ci == 1), tag="st_asum")
            for t in range(NT4):
                pap, pnm, pk = psl(kb, t)
                kb.ts(sc_a[0:1, 0:512], pap[0:1, :], 1.0 / 256.0, ALU.mult,
                      [(pnm, pk)], [("sc_a", "x")], tag="st_amean")
                kb.dma(strows[4 * bi + 2:4 * bi + 3, t * 512:(t + 1) * 512],
                       sc_a[0:1, 0:512], [("sc_a", "x")], [STR],
                       eng="scalar", tag="st_row")
        if h == 0:
            kb.tap("stats0", strows[:, :], [STR])

        for m in range(DC):
            kb.dma(wld2[0:16, 128:256], dram["w1s"][m], [("w1s", "*")],
                   [DPW], tag="w1s_ld")
            kb.copy(wcb[0:16, 1024:1152], wld2[0:16, 128:256], [DPW], [DPW],
                    tag="w1s_cast")
            for half in range(4):
                kb.dma(wld2[:, 0:256].rearrange("p (k q) -> p k q", q=P),
                       dram["w1h"][m, 2 * half:2 * half + 2].transpose(
                           [1, 0, 2]),
                       [("w1h", "*")], [DPW], tag="w1_ld")
                kb.copy(wcb[:, half * 256:(half + 1) * 256], wld2[:, 0:256],
                        [DPW], [DPW], tag="w1_cast")
            for t in range(NT4):
                pap, pnm, pk = psl(kb, (m % 2) * 2 + (t % 2))
                for k in range(DC):
                    rhs, rrd = xnb_ap(k)
                    kb.mm(pap, wcb[:, k * 128:(k + 1) * 128],
                          rhs[:, t * 512:(t + 1) * 512], [DPW, rrd],
                          [(pnm, pk)], start=(k == 0), stop=False,
                          tag="w1_mm")
                kb.mm(pap, wcb[0:16, 1024:1152],
                      strows[:, t * 512:(t + 1) * 512], [DPW, STR],
                      [(pnm, pk)], start=False, stop=True, tag="w1s_mm")
                kb.act(bq[:, m, t * 512:(t + 1) * 512], pap, AF.Gelu,
                       [(pnm, pk), ("b1c_s", "*")], [("bq", f"c{m}")],
                       bias=b1c[:, m:m + 1], tag="w1_gelu")

        kb.dma(wld2[:, 0:32].rearrange("p (k q) -> p k q", q=4),
               dram["w2"].transpose([1, 0, 2]), [("w2", "*")], [DPW],
               tag="w2_ld")
        kb.copy(wcb[:, 0:32], wld2[:, 0:32], [DPW], [DPW], tag="w2_cast")
        kb.copy(sc_a[0:1, 32:36], T["gb_s"][0:1, h * 4:(h + 1) * 4],
                [("gb_s", "*")], [("sc_a", "x")], tag="gb_cast")
        kb.copy(sc_a[0:1, 40:41], itcol[0:1, h:h + 1], [ITR], [("sc_a", "x")],
                tag="it_cast")
        pap9, pn9, pk9 = psl(kb, 6)
        kb.mm(pap9[:, 0:1], ones_blk[0:1, :], sc_a[0:1, 40:41],
              [("ones_blk", "*"), ("sc_a", "x")], [(pn9, pk9)], tag="it_bc")
        kb.copy(itcol[:, 8:9], pap9[:, 0:1], [(pn9, pk9)], [("itcol", "col")],
                tag="it_bcev")
        ITC = ("itcol", "col")
        lgf = wld2[:, 256:264]
        LGF = DPW
        for tb in range(SC):
            pap, pnm, pk = psl(kb, 7 if tb % 2 == 0 else 6)
            for k in range(DC):
                kb.mm(pap[:, 0:4], bq[:, k, tb * P:(tb + 1) * P],
                      wcb[:, k * 4:(k + 1) * 4],
                      [("bq", f"c{k}"), DPW], [(pnm, pk)],
                      start=(k == 0), stop=False, tag="w2_mm")
            kb.mm(pap[:, 0:4], ones_blk[0:1, :], sc_a[0:1, 32:36],
                  [("ones_blk", "*"), ("sc_a", "x")], [(pnm, pk)],
                  start=False, stop=True, tag="w2_bias")
            lg = lgf[:, 0:4]
            mx = lgf[:, 4:5]
            kb.reduce(mx, pap[:, 0:4], ALU.max, [(pnm, pk)], [LGF],
                      tag="sm_max")
            kb.ts(lg, pap[:, 0:4], mx, ALU.subtract, [(pnm, pk), LGF],
                  [LGF], tag="sm_sub")
            kb.act(lg, lg, AF.Exp, [LGF, ITC], [LGF],
                   scale=itcol[:, 8:9], tag="sm_exp")
            kb.reduce(mx, lg, ALU.add, [LGF], [LGF], tag="sm_sum")
            kb.recip(mx, mx, [LGF], [LGF], tag="sm_rec")
            kb.ts(lg, lg, mx, ALU.mult, [LGF], [LGF], tag="sm_div")
            kb.tt(lg, lg, T["pfloor_s"][:, 0:4], ALU.max,
                  [LGF, ("pfloor_s", "*")], [LGF], tag="sm_floor")
            kb.reduce(mx, lg, ALU.add, [LGF], [LGF], tag="sm_sum2")
            kb.recip(mx, mx, [LGF], [LGF], tag="sm_rec2")
            kb.ts(lg, lg, mx, ALU.mult, [LGF], [LGF], tag="sm_div2")
            kb.copy(sc_a[:, 48:52], lg, [LGF], [("sc_a", "x")], tag="p_cast")
            pap2, pnm2, pk2 = psl(kb, 5)
            kb.transpose(pap2.bitcast(BF16)[0:4, 0:P], sc_a[:, 48:52],
                         [("sc_a", "x")], [(pnm2, pk2)], tag="p_tr")
            kb.copy(strows[0:4, tb * P:(tb + 1) * P],
                    pap2.bitcast(BF16)[0:4, 0:P], [(pnm2, pk2)], [STR],
                    tag="p_rows")
        kb.ts(strows[0:1, :], strows[0:1, :], itcol[0:1, H + h:H + h + 1],
              ALU.add, [STR, ITR], [STR], tag="p0_crl")
        if h == 0:
            kb.tap("p0", strows[0:4, :], [STR])
        for i in range(4):
            kb.dma(sc_a[0:1, 0:TK], strows[i:i + 1, :], [STR],
                   [("sc_a", "x")], eng="scalar", tag="pb_row")
            for t in range(NT4):
                pap, pnm, pk = psl(kb, 4 + (t % 2))
                kb.mm(pap, ones_blk[0:1, :],
                      sc_a[0:1, t * 512:(t + 1) * 512],
                      [("ones_blk", "*"), ("sc_a", "x")], [(pnm, pk)],
                      tag="pb_mm")
                kb.copy(bk[:, i, t * 512:(t + 1) * 512], pap, [(pnm, pk)],
                        [("bk", f"c{i}")], tag="pb_ev")

        for ci, c in enumerate((c0, c1)):
            acc = bq[:, ci, 0:TK]
            ACC = ("bq", f"c{ci}")
            b0ap, b0rd = branch_ap(0, ci)
            kb.tt(acc, b0ap, bk[:, 0, 0:TK], ALU.mult,
                  [b0rd, ("bk", "c0")], [ACC], tag="cb_fs")
            for bi in range(1, 4):
                bap, brd = branch_ap(bi, ci)
                kb.tt(sc_a[:, :], bap, bk[:, bi, 0:TK], ALU.mult,
                      [brd, ("bk", f"c{bi}")], [("sc_a", "x")], tag="cb_mul")
                kb.tt(acc, acc, sc_a[:, :], ALU.add, [ACC, ("sc_a", "x")],
                      [ACC], tag="cb_add")
        for ci in range(2):
            kb.act(sc_a[:, :], bq[:, ci, 0:TK], AF.Square,
                   [("bq", f"c{ci}")], [("sc_a", "x")], tag="rms_sq")
            for t in range(NT4):
                pap, pnm, pk = psl(kb, 4 + t)
                kb.mm(pap, ones_blk[:, :], sc_a[:, t * 512:(t + 1) * 512],
                      [("ones_blk", "*"), ("sc_a", "x")], [(pnm, pk)],
                      start=(ci == 0), stop=(ci == 1), tag="rms_mm")
        for t in range(NT4):
            pap, pnm, pk = psl(kb, 4 + t)
            kb.act(pap, pap, AF.Sqrt, [(pnm, pk), AUX], [(pnm, pk)],
                   bias=EPS5, scale=1.0 / 256.0, tag="rms_sqrt")
            kb.recip(pap, pap, [(pnm, pk)], [(pnm, pk)], tag="rms_rec")
            for ci, c in enumerate((c0, c1)):
                kb.tt(bo[:, c, t * 512:(t + 1) * 512],
                      bq[:, ci, t * 512:(t + 1) * 512], pap, ALU.mult,
                      [("bq", f"c{ci}"), (pnm, pk)], [("bo", f"c{c}")],
                      tag="rms_ap")

    kb.tap("o_comb", bo[:, :, :], [("bo", f"c{c}") for c in range(DC)])
    if stop_after == "o_comb":
        return

    for c in range(DC):
        kb.dma(xr[:, c, :], dram["x_spill"][c], [("x_spill", f"c{c}")],
               [("xr", f"c{c}")], tag="x_reload")

    def rhs_bo(k, t):
        return bo[:, k, t * 512:(t + 1) * 512], ("bo", f"c{k}")

    def evac_res(m, t, pap, prd):
        kb.tt(xr[:, m, t * 512:(t + 1) * 512],
              xr[:, m, t * 512:(t + 1) * 512], pap, ALU.add,
              [("xr", f"c{m}"), prd], [("xr", f"c{m}")], tag="wo_res")

    stream_proj(kb, "wo", dram["wo"], DC, DC, rhs_bo, evac_res,
                wscale=(lambda k: T["rmsw_s"][:, k % 2:k % 2 + 1],
                        ("rmsw_s", "*")), tag="wo")




def build(debug_taps=(), stop_after=None):
    kb = KB(debug_taps)
    dram = kb.dram

    kb.din("x_fm", [DC, P, TK])
    kb.din("aux", [P, 10])
    kb.din("ident_bf", [P, P], BF16)
    kb.din("ones_bf", [P, P], BF16)
    for nm, shape, dt in [
        ("wq", [DC, DC, P, P], F32), ("wk", [DC, DC, P, P], F32),
        ("wv", [DC, DC, P, P], F32), ("wb", [DC, P, 4], F32),
        ("convq", [P, DC, 4], F32), ("convk", [P, DC, 4], F32),
        ("convv", [P, DC, 4], F32),
        ("fir_s", [P, DC, 5], F32), ("fir_l", [P, DC, 64], F32),
        ("w1h", [DC, DC, P, P], F32), ("w1s", [DC, 16, P], F32),
        ("b1c", [P, DC], F32), ("w2", [DC, P, 4], F32),
        ("gate_bias", [1, H * 4], F32), ("log_temp", [1, H], F32),
        ("crl", [1, H], F32), ("rmsw", [P, 2], F32),
        ("wo", [DC, DC, P, P], F32),
        ("mask_sl", [P, P], BF16), ("mask_iu", [P, P], BF16), ("pfloor", [P, 4], BF16),
        ("fw1", [4 * DC, DC, P, P], F32), ("fw2", [DC, 4 * DC, P, P], F32),
        ("fcwin", [P, 32, 3], F32), ("fcbin", [P, 32], F32),
        ("mwi", [5, 32, 8, P, P], F32), ("mwx", [5, 16, P, P], F32),
        ("mwdt", [5, 16, 64, P], F32), ("mwo", [5, 8, 16, P, P], F32),
        ("malog", [5, 8, P, 32], F32), ("mcw", [5, 2, P, 8, 4], F32),
        ("mcb", [5, P, 16], F32), ("mbdt", [5, P, 16], F32),
        ("mdp", [5, P, 16], F32), ("mlnw", [5, P, DC], F32),
        ("mlnb", [5, P, DC], F32), ("sel16", [16, 16 * P], BF16),
        ("erw", [DC, P, 8], F32), ("esgw", [P, DC], F32),
        ("esgu", [16, DC, P, P], F32), ("esd", [DC, DC, P, P], F32),
        ("ew1", [8, DC, DC, P, P], F32), ("ew2", [8, DC, DC, P, P], F32),
    ]:
        kb.din(nm, shape, dt)

    o_x = kb.dout("o_x", [DC, P, TK])

    kb.dscratch("x_spill", [DC, P, TK])
    kb.dscratch("w_spill", [SC * H, P, 640], BF16)
    kb.dscratch("rowtmp", [1, TK])
    kb.dscratch("cc4i", [P, 64], BF16)
    kb.dscratch("cc4o", [P, 64], BF16)
    kb.dscratch("zsp", [16, P, TK], BF16)
    kb.dscratch("cc5i", [P, 304])
    kb.dscratch("cc5o", [P, 304])
    for nm, shape in [("cc1i", [P, 72]), ("cc1o", [P, 72]),
                      ("cc2i", [P, DC * 63]), ("cc2o", [P, DC * 63]),
                      ("cc3i", [P, TK]), ("cc3o", [P, TK])]:
        kb.dscratch(nm, shape)

    xr = kb.sb("xr", [P, DC, TK])
    kb.sb("bq", [P, DC, 3 + TK], BF16)
    kb.sb("bk", [P, DC, 3 + TK], BF16)
    kb.sb("vf", [P, DC, 63 + TK], BF16)
    kb.sb("bo", [P, DC, TK], BF16)
    kb.sb("sc_a", [P, TK], BF16)
    kb.sb("dpt", [P, 1920], BF16)
    kb.sb("chsc", [P, TK], BF16)
    kb.sb("identity", [P, P], BF16)
    kb.sb("ones_blk", [P, P], BF16)
    kb.sb("aux_s", [P, 10])
    kb.sb("mapo_s", [P, 32])
    kb.sb("mcw_s", [P, 8, 4])
    kb.sb("msml", [P, 64])
    kb.sb("itcol", [P, 9])
    kb.sb("mask_sl_s", [P, P], BF16)
    kb.sb("mask_iu_s", [P, P], BF16)
    kb.sb("pfloor_s", [P, 4], BF16)
    kb.sb("cq", [P, DC, 4])
    kb.sb("ck", [P, DC, 4])
    kb.sb("cv", [P, DC, 4])
    kb.sb("firs", [P, DC, 5])
    kb.sb("b1c_s", [P, DC])
    kb.sb("rmsw_s", [P, 2])
    kb.sb("gb_s", [1, H * 4])
    kb.sb("ltm_s", [1, H])
    kb.sb("crl_s", [1, H])
    kb.sb("fcw", [P, 32, 3])
    kb.sb("fcb_s", [P, 32])

    kb.ps("pA", [P, TK])
    kb.ps("pB", [P, TK])

    for c in range(DC):
        kb.dma(xr[:, c, :], dram["x_fm"][c], [("x_fm", "*")],
               [("xr", f"c{c}")])
    for nm, dst in [("ident_bf", "identity"), ("ones_bf", "ones_blk"),
                    ("aux", "aux_s"), ("mask_sl", "mask_sl_s"), ("mask_iu", "mask_iu_s"),
                    ("pfloor", "pfloor_s"),
                    ("b1c", "b1c_s"), ("rmsw", "rmsw_s"),
                    ("gate_bias", "gb_s"), ("log_temp", "ltm_s"),
                    ("crl", "crl_s"),
                    ("fcwin", "fcw"), ("fcbin", "fcb_s")]:
        h = kb.tensors[dst]
        kb.dma(h.ap(), dram[nm], [(nm, "*")], [(dst, "*")])
    for nm, dst in [("convq", "cq"), ("convk", "ck"), ("convv", "cv"),
                    ("fir_s", "firs")]:
        h = kb.tensors[dst]
        kb.dma(h.ap(), dram[nm], [(nm, "*")], [(dst, "*")])

    build_l0_attn(kb, stop_after)
    if stop_after is None or stop_after.startswith(("xq1", "wq1", "h1pre",
                                                    "hsilu", "ffn", "l1")):
        build_l0_ffn(kb, stop_after if stop_after in ("xq1", "wq1", "h1pre",
                                                      "hsilu") else None)
    if stop_after is None or (stop_after or "").startswith("l1"):
        build_l1_attn(kb, stop_after)
    if stop_after is None or (stop_after or "").startswith("e_"):
        if stop_after is not None:
            build_l1_attn(kb, None)
        build_l1_ffn(kb, stop_after)

    for c in range(DC):
        kb.dma(o_x[c], xr[:, c, :], [("xr", f"c{c}")], [("o_x", f"c{c}")],
               eng="scalar", tag="store_x")

    kb.prog.op("gpsimd", lambda e: e.engine_nop(),
               [("o_x", f"c{c}") for c in range(DC)] +
               [("dbg_" + t, "*") for t in kb.debug_taps
                if ("dbg_" + t) in kb.dram],
               [], tag="fin")

    kb.emit()
    return kb




def _bf16np():
    import ml_dtypes
    return ml_dtypes.bfloat16


def _blocks(w):
    din, dout = w.shape
    return np.ascontiguousarray(
        w.reshape(din // P, P, dout // P, P).transpose(2, 0, 1, 3)
    ).astype(np.float32)


def _prep_shared(inputs):
    bf = _bf16np()
    g = {}
    g["wq"] = _blocks(np.asarray(inputs["L0_attn_Wq"]))
    g["wk"] = _blocks(np.asarray(inputs["L0_attn_Wk"]))
    g["wv"] = _blocks(np.asarray(inputs["L0_attn_Wv"]))
    wb = np.asarray(inputs["L0_attn_Wb"])
    g["wb"] = np.ascontiguousarray(wb.reshape(DC, P, 4)).astype(np.float32)
    for nm, key in [("convq", "L0_attn_conv_q"), ("convk", "L0_attn_conv_k"),
                    ("convv", "L0_attn_conv_v")]:
        cw = np.asarray(inputs[key])
        g[nm] = np.ascontiguousarray(
            cw.reshape(DC, P, 4).transpose(1, 0, 2)).astype(np.float32)
    fir_s = np.asarray(inputs["L0_attn_fir_short"]).reshape(D, 5)
    fir_l = np.asarray(inputs["L0_attn_fir_long"]).reshape(D, 64)
    g["fir_s"] = np.ascontiguousarray(
        fir_s.reshape(DC, P, 5).transpose(1, 0, 2)).astype(np.float32)
    g["fir_l"] = np.ascontiguousarray(
        fir_l.reshape(DC, P, 64).transpose(1, 0, 2)).astype(np.float32)
    w1 = np.asarray(inputs["L0_attn_W1"])
    g["w1h"] = _blocks(w1[:D])
    g["w1s"] = np.ascontiguousarray(
        w1[D:].reshape(16, DC, P).transpose(1, 0, 2)).astype(np.float32)
    g["b1c"] = np.ascontiguousarray(
        np.asarray(inputs["L0_attn_b1"]).reshape(DC, P).T).astype(np.float32)
    w2 = np.asarray(inputs["L0_attn_W2"])
    g["w2"] = np.ascontiguousarray(w2.reshape(DC, P, 4)).astype(np.float32)
    g["gate_bias"] = (np.asarray(inputs["L0_attn_b2"])[None, :] +
                      np.asarray(inputs["L0_attn_base_bias"])
                      ).astype(np.float32).reshape(1, H * 4)
    g["log_temp"] = np.asarray(inputs["L0_attn_log_temp"]).reshape(1, H).astype(np.float32)
    g["crl"] = np.asarray(
        inputs["L0_attn_conv_residual_logit"]).reshape(1, H).astype(np.float32)
    rw = np.asarray(inputs["L0_attn_rms_w"])
    g["rmsw"] = np.ascontiguousarray(rw.reshape(2, P).T).astype(np.float32)
    g["wo"] = _blocks(np.asarray(inputs["L0_attn_Wo"]))
    g["mask_sl"] = np.tril(np.ones((P, P), np.float32), -1).astype(bf)
    g["mask_iu"] = np.triu(np.ones((P, P), np.float32), 0).astype(bf)
    pf = np.zeros((P, 4), np.float32)
    pf[:, 0] = 0.05
    pf[:, 1] = 0.05
    g["pfloor"] = pf.astype(bf)
    g["ident_bf"] = np.eye(P, dtype=np.float32).astype(bf)
    g["ones_bf"] = np.ones((P, P), np.float32).astype(bf)
    g["fw1"] = _blocks(np.ascontiguousarray(np.asarray(inputs["L0_ffn_w1"]).T))
    g["fw2"] = _blocks(np.ascontiguousarray(np.asarray(inputs["L0_ffn_w2"]).T))
    cw = np.asarray(inputs["L0_ffn_conv_w"]).reshape(4096, 3)
    g["fcwin"] = np.ascontiguousarray(
        cw.reshape(32, P, 3).transpose(1, 0, 2)).astype(np.float32)
    cb = np.asarray(inputs["L0_ffn_conv_b"]).reshape(32, P)
    g["fcbin"] = np.ascontiguousarray(cb.T).astype(np.float32)

    Wi = np.asarray(inputs["L1_attn_W_in"])
    Wx = np.asarray(inputs["L1_attn_W_x"])
    Wdt = np.asarray(inputs["L1_attn_W_dt"])
    Wo = np.asarray(inputs["L1_attn_W_out"])
    g["_mwi"] = [_blocks(np.ascontiguousarray(Wi[l].T)) for l in range(4)]
    wxp = np.zeros((4, P, 2048), np.float32)
    wxp[:, 0:64] = Wx[:, 0:64]
    wxp[:, 64:80] = Wx[:, 64:80]
    wxp[:, 96:112] = Wx[:, 80:96]
    g["_mwx"] = [
        np.ascontiguousarray(
            wxp[l].T.reshape(16, P, P)).astype(np.float32)
        for l in range(4)]
    g["_mwdt"] = [
        np.ascontiguousarray(
            Wdt[l].T.reshape(64, 16, P).transpose(1, 0, 2)).astype(np.float32)
        for l in range(4)]
    g["_mwo"] = [_blocks(np.ascontiguousarray(Wo[l].T)) for l in range(4)]
    g["_malog"] = np.asarray(inputs["L1_attn_A_log"]).reshape(4, 16, P, 16)
    g["_mcw"] = np.asarray(inputs["L1_attn_conv_w"]).reshape(4, 16, P, 4)
    g["_mcb"] = np.asarray(inputs["L1_attn_conv_b"]).reshape(4, 16, P)
    g["_mbdt"] = np.asarray(inputs["L1_attn_b_dt"]).reshape(4, 16, P)
    g["_mdp"] = np.asarray(inputs["L1_attn_Dp"]).reshape(4, 16, P)
    g["_mlnw"] = np.asarray(inputs["L1_attn_ln_w"]).reshape(4, DC, P)
    g["_mlnb"] = np.asarray(inputs["L1_attn_ln_b"]).reshape(4, DC, P)
    sel = np.zeros((16, 16 * P), np.float32)
    for s in range(16):
        sel[s, s * P:(s + 1) * P] = 1.0
    g["sel16"] = sel.astype(bf)

    g["erw"] = np.ascontiguousarray(
        np.asarray(inputs["L1_ffn_router_w"]).reshape(DC, P, 8)
    ).astype(np.float32)
    g["esgw"] = np.ascontiguousarray(
        np.asarray(inputs["L1_ffn_shared_gate_w"]).reshape(DC, P).T
    ).astype(np.float32)
    g["esgu"] = _blocks(np.asarray(inputs["L1_ffn_shared_gate_up"]))
    g["esd"] = _blocks(np.asarray(inputs["L1_ffn_shared_down"]))
    w1e = np.asarray(inputs["L1_ffn_w1"])
    w2e = np.asarray(inputs["L1_ffn_w2"])
    g["ew1"] = np.stack([_blocks(w1e[e]) for e in range(8)])
    g["ew2"] = np.stack([_blocks(w2e[e]) for e in range(8)])
    return g


def _prep_core(inputs, shared, core):
    b, half = core // 2, core % 2
    x = np.asarray(inputs["L0_attn_x"])
    shard = x[b, half * TK:(half + 1) * TK, :]
    x_fm = np.ascontiguousarray(shard.T).reshape(DC, P, TK).astype(np.float32)
    aux = np.zeros((P, 10), np.float32)
    aux[:, 0] = float(half)
    aux[:, 1] = 1.0 - float(half)
    aux[:, 3] = 1e-5
    aux[:, 4] = 1e-6
    lm = [0, 1, 2, 3, 3] if half == 0 else [0, 0, 1, 2, 3]
    msk = [1, 1, 1, 1, 0] if half == 0 else [0, 1, 1, 1, 1]
    for s in range(5):
        aux[:, 5 + s] = float(msk[s])
    im = dict(shared)
    im["x_fm"] = x_fm
    im["aux"] = aux
    im["mwi"] = np.stack([shared["_mwi"][l] for l in lm])
    im["mwx"] = np.stack([shared["_mwx"][l] for l in lm])
    im["mwdt"] = np.stack([shared["_mwdt"][l] for l in lm])
    im["mwo"] = np.stack([shared["_mwo"][l] for l in lm])
    im["malog"] = np.ascontiguousarray(
        np.stack([shared["_malog"][l] for l in lm])
        .reshape(5, 8, 2, P, 16)
        .transpose(0, 1, 3, 2, 4).reshape(5, 8, P, 32)).astype(np.float32)
    im["mcw"] = np.ascontiguousarray(
        np.stack([shared["_mcw"][l] for l in lm])
        .reshape(5, 2, 8, P, 4).transpose(0, 1, 3, 2, 4)).astype(np.float32)
    im["mcb"] = np.ascontiguousarray(
        np.stack([shared["_mcb"][l] for l in lm])
        .transpose(0, 2, 1)).astype(np.float32)
    im["mbdt"] = np.ascontiguousarray(
        np.stack([shared["_mbdt"][l] for l in lm])
        .transpose(0, 2, 1)).astype(np.float32)
    im["mdp"] = np.ascontiguousarray(
        np.stack([shared["_mdp"][l] for l in lm])
        .transpose(0, 2, 1)).astype(np.float32)
    im["mlnw"] = np.ascontiguousarray(
        np.stack([shared["_mlnw"][l] for l in lm])
        .transpose(0, 2, 1)).astype(np.float32)
    im["mlnb"] = np.ascontiguousarray(
        np.stack([shared["_mlnb"][l] for l in lm])
        .transpose(0, 2, 1)).astype(np.float32)
    for k in list(im):
        if k.startswith("_"):
            del im[k]
    return im


def kernel(**inputs):
    kb = build()
    shared = _prep_shared(inputs)
    in_maps = [_prep_core(inputs, shared, c) for c in range(8)]
    res = run_bass_kernel_spmd(kb.nc, in_maps, core_ids=list(range(8)))
    out = np.zeros((4, 4096, 1024), np.float32)
    for c in range(8):
        b, half = c // 2, c % 2
        xf = res.results[c]["o_x"].reshape(D, TK)
        out[b, half * TK:(half + 1) * TK, :] = xf.T
    return out




def build_l0_ffn(kb, stop_after=None):
    T = kb.tensors
    dram = kb.dram
    xr = T["xr"]
    xrb = xr.ap().bitcast(BF16)
    bq, bk, vf, bo = T["bq"], T["bk"], T["vf"], T["bo"]
    sc_a = T["sc_a"]
    dp = T["dpt"]
    chsc = T["chsc"]
    ones_blk = T["ones_blk"]
    aux = T["aux_s"]
    PAR, NPAR = aux[:, 0:1], aux[:, 1:2]
    EPS5 = aux[:, 3:4]
    AUX = ("aux_s", "*")
    wld2 = dp[:, 1152:1792].bitcast(F32)
    wcb = dp[:, 0:1152]
    DPW = ("dpt", "*")

    bo_flat = bo.ap().rearrange("p c t -> p (c t)").bitcast(F32)
    mu_ap = bo_flat[:, 0:TK]
    rb_ap = bo_flat[:, TK:2 * TK]
    am_ap = bo_flat[:, 2 * TK:3 * TK]
    iv_ap = bo_flat[:, 3 * TK:4 * TK]
    MU = [("bo", "c0"), ("bo", "c1")]
    RB = [("bo", "c2"), ("bo", "c3")]
    AM = [("bo", "c4"), ("bo", "c5")]
    IV = [("bo", "c6"), ("bo", "c7")]

    def xnb_ap(c):
        return (xrb[:, c // 2, (c % 2) * TK:(c % 2 + 1) * TK],
                ("xr", f"c{c // 2}"))

    for c in range(DC):
        kb.copy(sc_a[:, :], xr[:, c, :], [("xr", f"c{c}")], [("sc_a", "x")],
                eng="scalar", tag="f_ln_cast")
        for t in range(NT4):
            pap, pnm, pk = psl(kb, t)
            kb.mm(pap, ones_blk[:, :], sc_a[:, t * 512:(t + 1) * 512],
                  [("ones_blk", "*"), ("sc_a", "x")], [(pnm, pk)],
                  start=(c == 0), stop=(c == DC - 1), tag="f_ln_sum")
        kb.act(sc_a[:, :], xr[:, c, :], AF.Square, [("xr", f"c{c}")],
               [("sc_a", "x")], tag="f_ln_sq")
        for t in range(NT4):
            pap, pnm, pk = psl(kb, 4 + t)
            kb.mm(pap, ones_blk[:, :], sc_a[:, t * 512:(t + 1) * 512],
                  [("ones_blk", "*"), ("sc_a", "x")], [(pnm, pk)],
                  start=(c == 0), stop=(c == DC - 1), tag="f_ln_sqsum")
    for t in range(NT4):
        ts_ = slice(t * 512, (t + 1) * 512)
        pap, pnm, pk = psl(kb, t)
        pbq, pnm2, pk2 = psl(kb, 4 + t)
        kb.ts(mu_ap[:, ts_], pap, 1.0 / D, ALU.mult, [(pnm, pk)], MU,
              tag="f_ln_mu")
        kb.tt(rb_ap[:, ts_], mu_ap[:, ts_], mu_ap[:, ts_], ALU.mult, MU, RB,
              tag="f_ln_musq")
        kb.stt(rb_ap[:, ts_], pbq, 1.0 / D, rb_ap[:, ts_], ALU.mult,
               ALU.subtract, [(pnm2, pk2)] + RB, RB, tag="f_ln_var")
    kb.act(rb_ap[:, :], rb_ap[:, :], AF.Sqrt, RB + [AUX], RB, bias=EPS5,
           tag="f_ln_sqrt")
    kb.recip(rb_ap[:, :], rb_ap[:, :], RB, RB, tag="f_ln_recip")
    for c in range(DC):
        kb.tt(sc_a[:, :], xr[:, c, :], mu_ap[:, :], ALU.subtract,
              [("xr", f"c{c}")] + MU, [("sc_a", "x")], tag="f_ln_sub")
        kb.dma(dram["x_spill"][c], xr[:, c, :], [("xr", f"c{c}")],
               [("x_spill", f"c{c}")], eng="scalar", tag="f_spill")
        xnbc, xnbrd = xnb_ap(c)
        kb.tt(xnbc, sc_a[:, :], rb_ap[:, :], ALU.mult, [("sc_a", "x")] + RB,
              [xnbrd], tag="f_ln_scale")

    chi = chsc.ap().bitcast(I32)
    CHS = ("chsc", "x")

    def act_quant(src_ap_fn, nch, inv_dst, inv_keys, xq_dst_fn, tag):
        kb.memset(am_ap[:, :], 0.0, AM, tag=tag + "_amz")
        for c in range(nch):
            sap, srd = src_ap_fn(c)
            kb.stt(am_ap[:, :], sap, -1.0, am_ap[:, :], ALU.mult, ALU.max,
                   [srd] + AM, AM, tag=tag + "_amn")
            kb.stt(am_ap[:, :], sap, 1.0, am_ap[:, :], ALU.mult, ALU.max,
                   [srd] + AM, AM, tag=tag + "_amp")
        for tb in range(SC):
            pap, pnm, pk = psl(kb, 6 + tb % 2)
            kb.copy(chsc[:, 0:128], am_ap[:, tb * P:(tb + 1) * P], AM,
                    [CHS], tag=tag + "_bf")
            kb.transpose(pap.bitcast(BF16)[:, 0:128], chsc[:, 0:128],
                         [CHS], [(pnm, pk)], tag=tag + "_tr")
            kb.reduce(wld2[:, 300 + tb % 2:301 + tb % 2],
                      pap.bitcast(BF16)[:, 0:128],
                      ALU.max, [(pnm, pk)], [DPW], tag=tag + "_red")
            kb.dma(dram["rowtmp"][0:1, tb * P:(tb + 1) * P].transpose([1, 0]),
                   wld2[:, 300 + tb % 2:301 + tb % 2],
                   [DPW], [("rowtmp", "*")], eng="scalar",
                   tag=tag + "_rowst")
        kb.dma(mu_ap[0:1, :], dram["rowtmp"], [("rowtmp", "*")], MU,
               tag=tag + "_rowld")
        kb.ts(mu_ap[0:1, :], mu_ap[0:1, :], 1e-5, ALU.max, MU, MU,
              tag=tag + "_clip")
        kb.ts(rb_ap[0:1, 0:TK], mu_ap[0:1, :], 1.0 / 127.0, ALU.mult,
              MU, RB, tag=tag + "_inv")
        kb.memset(wld2[0:1, 128:256], 1.0, [DPW], tag=tag + "_ones")
        for t in range(NT4):
            pap, pnm, pk = psl(kb, 4 + t % 2)
            kb.mm(pap, wld2[0:1, 128:256], rb_ap[0:1, t * 512:(t + 1) * 512],
                  [DPW] + RB, [(pnm, pk)], tag=tag + "_invbc")
            kb.copy(inv_dst[:, t * 512:(t + 1) * 512], pap, [(pnm, pk)],
                    inv_keys, tag=tag + "_invev")
        kb.recip(rb_ap[0:1, 0:TK], rb_ap[0:1, 0:TK], RB, RB,
                 tag=tag + "_qrec")
        for t in range(NT4):
            pap, pnm, pk = psl(kb, 4 + t % 2)
            kb.mm(pap, wld2[0:1, 128:256], rb_ap[0:1, t * 512:(t + 1) * 512],
                  [DPW] + RB, [(pnm, pk)], tag=tag + "_qbc")
            kb.copy(am_ap[:, t * 512:(t + 1) * 512], pap, [(pnm, pk)], AM,
                    tag=tag + "_qev")
        mui = mu_ap.bitcast(I32)
        for c in range(nch):
            sap, srd = src_ap_fn(c)
            dap, drd = xq_dst_fn(c)
            kb.tt(mui[:, 0:TK], sap, am_ap[:, :], ALU.mult, [srd] + AM, MU,
                  tag=tag + "_qmul")
            kb.copy(dap, mui[:, 0:TK], MU, [drd], tag=tag + "_qcast")

    def xq_dst(c):
        return vf[:, c, 0:TK], ("vf", f"c{c}")

    act_quant(xnb_ap, DC, iv_ap, IV, xq_dst, "aq1")
    kb.tap("xq1", vf[:, :, 0:TK], [("vf", f"c{c}") for c in range(DC)])
    if stop_after == "xq1":
        return

    def wq_scale(wname, nmc, nkc, dst_col, tag):
        for m in range(nmc):
            for half in range((nkc + 1) // 2):
                kcs = list(range(2 * half, min(2 * half + 2, nkc)))
                w = len(kcs) * 128
                kb.dma(wld2[:, 0:w].rearrange("p (k q) -> p k q", q=P),
                       dram[wname][m, kcs[0]:kcs[0] + len(kcs)].transpose(
                           [1, 0, 2]),
                       [(wname, "*")], [DPW], tag=tag + "_ld")
                kb.prog.op("vector",
                           lambda e, o=wld2[:, 281:282], i=wld2[:, 0:w]:
                           e.tensor_reduce(o, i, mybir.AxisListType.X,
                                           ALU.add, True),
                           [DPW], [DPW], tag=tag + "_red")
                if m == 0 and half == 0:
                    kb.copy(wld2[:, 280:281], wld2[:, 281:282], [DPW], [DPW],
                            tag=tag + "_acc0")
                else:
                    kb.tt(wld2[:, 280:281], wld2[:, 280:281],
                          wld2[:, 281:282], ALU.add, [DPW], [DPW],
                          tag=tag + "_acc")
        kb.copy(sc_a[:, 0:1], wld2[:, 280:281], [DPW], [("sc_a", "x")],
                tag=tag + "_cp")
        pap, pnm, pk = psl(kb, 6)
        kb.mm(pap[0:1, 0:1], ones_blk[:, 0:1], sc_a[:, 0:1],
              [("ones_blk", "*"), ("sc_a", "x")], [(pnm, pk)],
              tag=tag + "_psum")
        nel = float(nmc * P * nkc * P)
        kb.ts(wld2[0:1, 285:286], pap[0:1, 0:1], 1.0 / nel,
              ALU.mult, [(pnm, pk)], [DPW], s2=1e-5, op1=ALU.max,
              tag=tag + "_mean")
        kb.recip(wld2[0:1, 284:285],
                 wld2[0:1, 285:286], [DPW], [DPW],
                 tag=tag + "_wrec")
        pap2, pnm2, pk2 = psl(kb, 7)
        kb.memset(wld2[0:1, 128:256], 1.0, [DPW], tag=tag + "_ones")
        kb.mm(pap2[:, 0:1], wld2[0:1, 128:256][0:1, 0:P],
              wld2[0:1, 284:285], [DPW], [(pnm2, pk2)],
              tag=tag + "_wbc")
        kb.copy(wld2[:, 286:287], pap2[:, 0:1],
                [(pnm2, pk2)], [DPW], tag=tag + "_wbcev")
        kb.mm(pap2[:, 1:2], wld2[0:1, 128:256][0:1, 0:P],
              wld2[0:1, 285:286], [DPW], [(pnm2, pk2)],
              tag=tag + "_wbci")
        kb.copy(wld2[:, 287:288], pap2[:, 1:2],
                [(pnm2, pk2)], [DPW], tag=tag + "_wbciev")

    wq_scale("fw1", 4 * DC, DC, 392, "wq1")
    if stop_after == "wq1":
        return

    kb.ts(iv_ap[:, :], iv_ap[:, :], wld2[:, 287:288], ALU.mult,
          IV + [DPW], IV, tag="iv1_w")

    def hchunk(j):
        if j < 8:
            return bq[:, j, 0:TK], ("bq", f"c{j}")
        if j < 16:
            return bk[:, j - 8, 0:TK], ("bk", f"c{j - 8}")
        jj = j - 16
        return (xrb[:, jj // 2, (jj % 2) * TK:(jj % 2 + 1) * TK],
                ("xr", f"c{jj // 2}"))

    def wq_stream_mm(wname, nmc, nkc, wscale_col, rhs_fn, out_cb, tag):
        KG = 4
        for m in range(nmc):
            for g in range(0, nkc, KG):
                kcs = list(range(g, min(g + KG, nkc)))
                for half in range(0, len(kcs), 2):
                    pk2_ = kcs[half:half + 2]
                    w = len(pk2_) * 128
                    kb.dma(wld2[:, 0:w].rearrange("p (k q) -> p k q", q=P),
                           dram[wname][m, pk2_[0]:pk2_[0] + len(pk2_)]
                           .transpose([1, 0, 2]),
                           [(wname, "*")], [DPW], tag=tag + "_ld")
                    kb.ts(chi[:, 0:w], wld2[:, 0:w], wld2[:, 286:287],
                          ALU.mult, [DPW], [CHS], tag=tag + "_qm")
                    kb.ts(chi[:, 0:w], chi[:, 0:w], -1, ALU.max, [CHS],
                          [CHS], s2=1, op1=ALU.min, tag=tag + "_clip")
                    kb.copy(wcb[:, half * 128:half * 128 + w], chi[:, 0:w],
                            [CHS], [DPW], eng="scalar", tag=tag + "_cast")
                for t in range(NT4):
                    pap, pnm, pk = psl(kb, (m % 2) * 4 + t)
                    for j, k in enumerate(kcs):
                        rhs, rrd = rhs_fn(k, t)
                        kb.mm(pap, wcb[:, j * 128:(j + 1) * 128], rhs,
                              [DPW, rrd], [(pnm, pk)], start=(k == 0),
                              stop=(k == nkc - 1), tag=tag + "_mm")
                    if kcs[-1] == nkc - 1:
                        out_cb(m, t, pap, (pnm, pk))

    def rhs_xq(k, t):
        return vf[:, k, t * 512:(t + 1) * 512], ("vf", f"c{k}")

    def ev1(m, t, pap, prd):
        hap, hrd = hchunk(m)
        kb.tt(hap[:, t * 512:(t + 1) * 512], pap,
              iv_ap[:, t * 512:(t + 1) * 512], ALU.mult, [prd] + IV,
              [hrd], tag="ev1")

    wq_stream_mm("fw1", 4 * DC, DC, 394, rhs_xq, ev1, "bm1")
    kb.tap("h1pre", bq[:, :, 0:TK], [("bq", f"c{c}") for c in range(DC)])
    if stop_after == "h1pre":
        return

    ccs2 = sc_a[:, 0:128]
    for j in range(32):
        hap, hrd = hchunk(j)
        kb.ts(sc_a[:, j:j + 1], hap[:, 0:1], PAR, ALU.mult, [hrd, AUX],
              [("sc_a", "x")], tag="cc4_stf")
        kb.ts(sc_a[:, 32 + j:33 + j], hap[:, TK - 1:TK], NPAR, ALU.mult,
              [hrd, AUX], [("sc_a", "x")], tag="cc4_stl")
    kb.dma(dram["cc4i"], sc_a[:, 0:64], [("sc_a", "x")], [("cc4i", "*")],
           eng="scalar")
    kb.cc(kb.dram_h["cc4i"], kb.dram_h["cc4o"], [("cc4i", "*")],
          [("cc4o", "*")])
    kb.dma(sc_a[:, 64:128], dram["cc4o"], [("cc4o", "*")], [("sc_a", "x")],
           eng="scalar")
    kb.ts(sc_a[:, 96:128], sc_a[:, 96:128], PAR, ALU.mult,
          [("sc_a", "x"), AUX], [("sc_a", "x")], tag="cc4_hl")
    kb.ts(sc_a[:, 64:96], sc_a[:, 64:96], NPAR, ALU.mult,
          [("sc_a", "x"), AUX], [("sc_a", "x")], tag="cc4_hr")

    fcw, fcb = T["fcw"], T["fcb_s"]
    for j in range(32):
        hap, hrd = hchunk(j)
        kb.ts(mu_ap[:, :], hap[:, :], fcw[:, j, 1:2], ALU.mult,
              [hrd, ("fcw", "*")], MU, tag="f_cv1")
        kb.stt(mu_ap[:, 1:TK], hap[:, 0:TK - 1], fcw[:, j, 0:1],
               mu_ap[:, 1:TK], ALU.mult, ALU.add, [hrd, ("fcw", "*")] + MU,
               MU, tag="f_cv0")
        kb.stt(mu_ap[:, 0:TK - 1], hap[:, 1:TK], fcw[:, j, 2:3],
               mu_ap[:, 0:TK - 1], ALU.mult, ALU.add,
               [hrd, ("fcw", "*")] + MU, MU, tag="f_cv2")
        kb.stt(mu_ap[:, 0:1], sc_a[:, 96 + j:97 + j], fcw[:, j, 0:1],
               mu_ap[:, 0:1], ALU.mult, ALU.add,
               [("sc_a", "x"), ("fcw", "*")] + MU, MU, tag="f_cve0")
        kb.stt(mu_ap[:, TK - 1:TK], sc_a[:, 64 + j:65 + j], fcw[:, j, 2:3],
               mu_ap[:, TK - 1:TK], ALU.mult, ALU.add,
               [("sc_a", "x"), ("fcw", "*")] + MU, MU, tag="f_cve1")
        kb.act(hap[:, :], mu_ap[:, :], AF.Silu, MU + [("fcb_s", "*")],
               [hrd], bias=fcb[:, j:j + 1], tag="f_silu")
    kb.tap("hsilu", bq[:, :, 0:TK], [("bq", f"c{c}") for c in range(DC)])
    if stop_after == "hsilu":
        return

    wq_scale("fw2", DC, 4 * DC, 392, "wq2")
    act_quant(hchunk, 32, iv_ap, IV, hchunk, "aq2")
    kb.ts(iv_ap[:, :], iv_ap[:, :], wld2[:, 287:288], ALU.mult,
          IV + [DPW], IV, tag="iv2_w")

    def rhs_h(k, t):
        hap, hrd = hchunk(k)
        return hap[:, t * 512:(t + 1) * 512], hrd

    def ev2(m, t, pap, prd):
        kb.tt(vf[:, m, t * 512:(t + 1) * 512], pap,
              iv_ap[:, t * 512:(t + 1) * 512], ALU.mult, [prd] + IV,
              [("vf", f"c{m}")], tag="ev2")

    wq_stream_mm("fw2", DC, 4 * DC, 394, rhs_h, ev2, "bm2")

    for c in range(DC):
        kb.dma(xr[:, c, :], dram["x_spill"][c], [("x_spill", f"c{c}")],
               [("xr", f"c{c}")], tag="f_reload")
        kb.tt(xr[:, c, :], xr[:, c, :], vf[:, c, 0:TK], ALU.add,
              [("xr", f"c{c}"), ("vf", f"c{c}")], [("xr", f"c{c}")],
              tag="f_res")
    kb.tap("l0ffn_out", xr[:, :, :], [("xr", f"c{c}") for c in range(DC)])




def build_l1_attn(kb, stop_after=None):
    T = kb.tensors
    dram = kb.dram
    xr = T["xr"]
    xrb = xr.ap().rearrange("p c k -> p (c k)").bitcast(BF16)
    bq, bk, vf, bo = T["bq"], T["bk"], T["vf"], T["bo"]
    sc_a, chsc = T["sc_a"], T["chsc"]
    scf = sc_a.ap().bitcast(F32)
    dp = T["dpt"]
    wld2 = dp[:, 1152:1792].bitcast(F32)
    wcb = dp[:, 0:1152]
    DPW = ("dpt", "*")
    aux = T["aux_s"]
    AUX = ("aux_s", "*")
    PAR, NPAR = aux[:, 0:1], aux[:, 1:2]
    apo = T["mapo_s"].ap()
    APO = ("mapo_s", "*")
    mcw = T["mcw_s"]
    msml = T["msml"].ap()
    MSM = ("msml", "*")
    mcb_ = msml[:, 0:16]
    nbdt = msml[:, 16:32]
    mdp = msml[:, 32:48]
    mlnw = msml[:, 48:56]
    mlnb = msml[:, 56:64]
    ident = T["identity"]
    pA, pB = T["pA"], T["pB"]
    pAf = pA.ap()
    pBf = pB.ap()
    PA_ALL = [("pA", f"t{j}") for j in range(4)]
    PB_ALL = [("pB", f"t{j}") for j in range(4)]
    SCA = ("sc_a", "x")
    CHS = ("chsc", "x")

    def uslab(j):
        if j < 8:
            return bq[:, j, :], ("bq", f"c{j}")
        return bk[:, j - 8, :], ("bk", f"c{j - 8}")

    def boslab(j):
        return bo[:, j, :], ("bo", f"c{j}")

    def Lslab(db):
        return (xrb[:, db * TK:(db + 1) * TK], ("xr", f"c{db // 2}"))

    eps_ap, eps_rd = kb.cconst(1e-5)

    def m_ln(src_fn, src_f32, dst_fn, with_wb, tag):
        ones_blk = T["ones_blk"]
        for c in range(DC):
            sap, srd = src_fn(c)
            if src_f32:
                kb.copy(sc_a[:, :], sap, [srd], [SCA], eng="scalar",
                        tag=tag + "_cast")
                rhs_ap, rhs_rd = sc_a[:, :], SCA
            else:
                rhs_ap, rhs_rd = sap, srd
            for t in range(NT4):
                pap, pnm, pk = psl(kb, t)
                kb.mm(pap, ones_blk[:, :], rhs_ap[:, t * 512:(t + 1) * 512],
                      [("ones_blk", "*"), rhs_rd], [(pnm, pk)],
                      start=(c == 0), stop=(c == DC - 1), tag=tag + "_sum")
            kb.act(chsc[:, :], sap, AF.Square, [srd], [CHS], tag=tag + "_sq")
            for t in range(NT4):
                pap, pnm, pk = psl(kb, 4 + t)
                kb.mm(pap, ones_blk[:, :], chsc[:, t * 512:(t + 1) * 512],
                      [("ones_blk", "*"), CHS], [(pnm, pk)],
                      start=(c == 0), stop=(c == DC - 1), tag=tag + "_sqs")
        chf = chsc.ap().bitcast(F32)
        for t in range(NT4):
            pap, pnm, pk = psl(kb, t)
            pbq, pnm2, pk2 = psl(kb, 4 + t)
            kb.ts(pap, pap, 1.0 / D, ALU.mult, [(pnm, pk)], [(pnm, pk)],
                  tag=tag + "_mu")
            kb.act(chf[:, 0:512], pap, AF.Square, [(pnm, pk)], [CHS],
                   tag=tag + "_musq")
            kb.stt(pbq, pbq, 1.0 / D, chf[:, 0:512], ALU.mult, ALU.subtract,
                   [(pnm2, pk2), CHS], [(pnm2, pk2)], tag=tag + "_var")
            kb.act(pbq, pbq, AF.Sqrt, [(pnm2, pk2), eps_rd], [(pnm2, pk2)],
                   bias=eps_ap, tag=tag + "_sqrt")
            kb.recip(pbq, pbq, [(pnm2, pk2)], [(pnm2, pk2)],
                     tag=tag + "_rcp")
        for c in range(DC):
            sap, srd = src_fn(c)
            dap, dwr = dst_fn(c)
            for t in range(NT4):
                ts_ = slice(t * 512, (t + 1) * 512)
                pap, pnm, pk = psl(kb, t)
                pbq, pnm2, pk2 = psl(kb, 4 + t)
                kb.tt(chsc[:, ts_], sap[:, ts_], pap, ALU.subtract,
                      [srd, (pnm, pk)], [CHS], tag=tag + "_sub")
                if not with_wb:
                    kb.tt(dap[:, ts_], chsc[:, ts_], pbq, ALU.mult,
                          [CHS, (pnm2, pk2)], [dwr], tag=tag + "_scl")
                else:
                    kb.tt(chsc[:, ts_], chsc[:, ts_], pbq, ALU.mult,
                          [CHS, (pnm2, pk2)], [CHS], tag=tag + "_scl")
                    kb.ts(dap[:, ts_], chsc[:, ts_], mlnw[:, c:c + 1],
                          ALU.mult, [CHS, MSM],
                          [dwr], s2=mlnb[:, c:c + 1], op1=ALU.add,
                          tag=tag + "_wb")

    m_ln(lambda c: (xr[:, c, :], ("xr", f"c{c}")), True,
         lambda c: (vf[:, c, 0:TK], ("vf", f"c{c}")), False, "m_ln0")
    for c in range(DC):
        kb.dma(dram["x_spill"][c], xr[:, c, :], [("xr", f"c{c}")],
               [("x_spill", f"c{c}")], eng="scalar", tag="m_spill")
    kb.tap("l1_ln0", vf[:, :, 0:TK], [("vf", f"c{c}") for c in range(DC)])
    if stop_after == "l1_ln0":
        return

    for st in range(5):
        stg = f"s{st}"
        mask_col = aux[:, 5 + st:6 + st]

        for nm, dsl in [("mlnw", msml[:, 48:56]), ("mlnb", msml[:, 56:64]),
                        ("mcb", msml[:, 0:16]), ("mdp", msml[:, 32:48]),
                        ("mbdt", msml[:, 16:32])]:
            kb.dma(dsl, dram[nm][st], [(nm, "*")], [MSM],
                   tag="m_prm" + stg)
        kb.ts(nbdt[:, :], nbdt[:, :], -1.0, ALU.mult, [MSM], [MSM],
              tag="m_nbdt" + stg)

        if st == 0:
            kb.memset(scf[:, 0:304], 0.0, [SCA], tag="m_hrecv0")
        else:
            kb.dma(scf[:, 0:304], dram["cc5o"], [("cc5o", "*")], [SCA],
                   tag="m_hrecv" + stg)

        m_ln(lambda c: (vf[:, c, 0:TK], ("vf", f"c{c}")), False,
             lambda c: (bo[:, c, :], ("bo", f"c{c}")), True, "m_ln" + stg)

        def wi_evac(m, t, pap, prd, st=st):
            if m < 16:
                uap, urd = uslab(m)
                kb.act(uap[:, 3 + t * 512:3 + (t + 1) * 512], pap, AF.Copy,
                       [prd], [urd], tag="m_xs_ev")
            else:
                kb.copy(chsc[:, t * 512:(t + 1) * 512], pap, [prd], [CHS],
                        eng="scalar", tag="m_z_ev")
                if t == NT4 - 1:
                    kb.dma(dram["zsp"][m - 16], chsc[:, 0:TK], [CHS],
                           [("zsp", f"c{m - 16}")], eng="scalar",
                           tag="m_z_sp")

        stream_proj(kb, "mwi", dram["mwi"][st], 8, 32,
                    lambda k, t: (bo[:, k, t * 512:(t + 1) * 512],
                                  ("bo", f"c{k}")),
                    wi_evac, tag="m_wi" + stg)

        for m in range(16):
            uap, urd = uslab(m)
            kb.copy(scf[:, 816 + m * 3:816 + (m + 1) * 3],
                    uap[:, 2048:2051], [urd], [SCA], tag="m_tailcap")
        for m in range(16):
            uap, urd = uslab(m)
            kb.ts(uap[:, 0:3], scf[:, 256 + m * 3:256 + (m + 1) * 3], PAR,
                  ALU.mult, [SCA, AUX], [urd], tag="m_halo")

        for c in range(16):
            uap, urd = uslab(c)
            if c % 8 == 0:
                kb.dma(T["mcw_s"].ap(), dram["mcw"][st, c // 8],
                       [("mcw", "*")], [("mcw_s", "*")], tag="m_cwld")
            for j in range(4):
                kb.ts(wcb[:, j * P:(j + 1) * P], ident[:, :],
                      mcw[:, c % 8, j:j + 1], ALU.mult,
                      [("identity", "*"), ("mcw_s", "*")], [DPW],
                      tag="m_cvd")
            for tt4 in range(NT4):
                pap, pnm, pk = psl(kb, (c % 2) * 4 + tt4)
                for j in range(4):
                    kb.mm(pap, wcb[:, j * P:(j + 1) * P],
                          uap[:, tt4 * 512 + j:tt4 * 512 + j + 512],
                          [DPW, urd], [(pnm, pk)], start=(j == 0),
                          stop=(j == 3), tag="m_cvmm")
            for tt4 in range(NT4):
                pap, pnm, pk = psl(kb, (c % 2) * 4 + tt4)
                kb.act(uap[:, 3 + tt4 * 512:3 + (tt4 + 1) * 512], pap,
                       AF.Silu, [(pnm, pk), MSM], [urd],
                       bias=mcb_[:, c:c + 1], tag="m_cvsilu")

        for g in range(2):
            kb.dma(wcb[:, 0:1024].rearrange("p (k q) -> p k q", q=P),
                   dram["mwx"][st, g * 8:g * 8 + 8].transpose([1, 0, 2]),
                   [("mwx", "*")], [DPW], eng="gpsimd", tag="m_wx_ld")
            for t in range(NT4):
                pap, pnm, pk = psl(kb, t)
                for j in range(8):
                    k = g * 8 + j
                    uap, urd = uslab(k)
                    kb.mm(pap, wcb[:, j * P:(j + 1) * P],
                          uap[:, 3 + t * 512:3 + (t + 1) * 512],
                          [DPW, urd], [(pnm, pk)], start=(k == 0),
                          stop=(k == 15), tag="m_wx_mm")
                if g == 1:
                    kb.act(bo[64:128, 4, t * 512:(t + 1) * 512],
                           pap[0:64, :], AF.Copy, [(pnm, pk)],
                           [("bo", "c4")], tag="m_dtin_ev")
                    kb.act(bo[64:80, 4, t * 512:(t + 1) * 512], pap[64:80, :],
                           AF.Copy, [(pnm, pk)], [("bo", "c4")],
                           tag="m_B_ev")
                    kb.act(bo[96:112, 7, t * 512:(t + 1) * 512],
                           pap[96:112, :], AF.Copy, [(pnm, pk)],
                           [("bo", "c7")], tag="m_C_ev")
        kb.dma(bo[0:16, 4, :], bo[64:80, 4, :], [("bo", "c4")],
               [("bo", "c4")], eng="scalar", tag="m_Bmv")
        kb.dma(bo[0:16, 7, :], bo[96:112, 7, :], [("bo", "c7")],
               [("bo", "c7")], eng="scalar", tag="m_Cmv")
        kb.dma(chsc[0:16, :], dram["sel16"], [("sel16", "*")],
               [CHS], tag="m_selld")

        kb.ts(scf[:, 304:560], scf[:, 0:256], PAR, ALU.mult, [SCA, AUX],
              [SCA], tag="m_hinit")

        dA_pair = [boslab(0), boslab(6)]
        d1_ap, d1_rd = boslab(1)
        g_ap, g_rd = boslab(2)
        ya_ap, ya_rd = boslab(3)
        du_ap, du_rd = boslab(5)
        for db in range(16):
            kb.dma(wcb[64:128, 0:P], dram["mwdt"][st, db],
                   [("mwdt", "*")], [DPW], eng="gpsimd", tag="m_wdt_ld")
            Lap, Lrd = Lslab(db)
            for t in range(NT4):
                pap, pnm, pk = psl(kb, (db % 2) * 4 + t)
                kb.mm(pap, wcb[64:128, 0:P],
                      bo[64:128, 4, t * 512:(t + 1) * 512],
                      [DPW, ("bo", "c4")], [(pnm, pk)], tag="m_wdt_mm")
            for t in range(NT4):
                pap, pnm, pk = psl(kb, (db % 2) * 4 + t)
                kb.act(pap, pap, AF.Sigmoid, [(pnm, pk), MSM],
                       [(pnm, pk)], bias=nbdt[:, db:db + 1],
                       scale=-1.0, tag="m_sig")
            for t in range(NT4):
                pap, pnm, pk = psl(kb, (db % 2) * 4 + t)
                kb.act(Lap[:, t * 512:(t + 1) * 512], pap, AF.Ln,
                       [(pnm, pk)], [Lrd], tag="m_lnact")
        for db in range(16):
            if db % 2 == 0:
                kb.dma(apo[:, :], dram["malog"][st, db // 2],
                       [("malog", "*")], [APO], tag="m_apold")
                kb.act(apo[:, :], apo[:, :], AF.Exp, [APO], [APO],
                       tag="m_apos")
            Lap, Lrd = Lslab(db)
            uap, urd = uslab(db)
            kb.stt(du_ap[:, 0:TK], uap[:, 3:3 + TK], -1.0, Lap[:, 0:TK],
                   ALU.mult, ALU.mult, [urd, Lrd], [du_rd], tag="m_du")
            pBb = pB.ap().bitcast(BF16)
            Bbf = pBb[:, 0:TK]
            Cbf = pBb[:, TK:2 * TK]
            PB01 = [("pB", "t0"), ("pB", "t1")]
            PB23 = [("pB", "t2"), ("pB", "t3")]
            for s in range(16):
                dA_ap, dA_rd = dA_pair[s % 2]
                scol = apo[:, (db % 2) * 16 + s:(db % 2) * 16 + s + 1]
                kb.act(dA_ap[:, 0:TK], Lap[:, 0:TK], AF.Exp, [Lrd, APO],
                       [dA_rd], scale=scol, tag="m_dA")
                for t in range(NT4):
                    pap, pnm, pk = psl(kb, t)
                    kb.mm(pap, chsc[0:16, s * P:(s + 1) * P],
                          bo[0:16, 4, t * 512:(t + 1) * 512],
                          [CHS, ("bo", "c4")], [(pnm, pk)], tag="m_Bbc")
                kb.tt(d1_ap[:, 0:TK], du_ap[:, 0:TK], pAf[:, 0:TK],
                      ALU.mult, [du_rd] + PA_ALL, [d1_rd], tag="m_d1")
                for t in range(NT4):
                    pap, pnm, pk = psl(kb, 4 + t)
                    kb.mm(pap, chsc[0:16, s * P:(s + 1) * P],
                          bo[0:16, 7, t * 512:(t + 1) * 512],
                          [CHS, ("bo", "c7")], [(pnm, pk)], tag="m_Cbc")
                kb.scan(g_ap[:, 0:TK], dA_ap[:, 0:TK], d1_ap[:, 0:TK],
                        scf[:, 304 + db * 16 + s:304 + db * 16 + s + 1],
                        [dA_rd, d1_rd, SCA], [g_rd], tag="m_scan")
                kb.copy(scf[:, 560 + db * 16 + s:560 + db * 16 + s + 1],
                        g_ap[:, TK - 1:TK], [g_rd], [SCA], tag="m_hfin")
                if s == 0:
                    kb.tt(ya_ap[:, 0:TK], g_ap[:, 0:TK], pBf[:, 0:TK],
                          ALU.mult, [g_rd] + PB_ALL, [ya_rd], tag="m_w0")
                else:
                    kb.tt(d1_ap[:, 0:TK], g_ap[:, 0:TK], pBf[:, 0:TK],
                          ALU.mult, [g_rd] + PB_ALL, [d1_rd], tag="m_w")
                    kb.tt(ya_ap[:, 0:TK], ya_ap[:, 0:TK], d1_ap[:, 0:TK],
                          ALU.add, [ya_rd, d1_rd], [ya_rd], tag="m_yadd")
            dA_ap, dA_rd = dA_pair[0]
            kb.dma(dA_ap[:, 0:TK], dram["zsp"][db], [("zsp", f"c{db}")],
                   [dA_rd], tag="m_zrel")
            kb.stt(d1_ap[:, 0:TK], uap[:, 3:3 + TK], mdp[:, db:db + 1],
                   ya_ap[:, 0:TK], ALU.mult, ALU.add,
                   [urd, MSM, ya_rd], [d1_rd], tag="m_udp")
            kb.act(g_ap[:, 0:TK], dA_ap[:, 0:TK], AF.Silu, [dA_rd], [g_rd],
                   tag="m_zsilu")
            kb.tt(uap[:, 3:3 + TK], d1_ap[:, 0:TK], g_ap[:, 0:TK], ALU.mult,
                  [d1_rd, g_rd], [urd], tag="m_ymix")

        for m in range(8):
            for g in range(2):
                kb.dma(wcb[:, 0:1024].rearrange("p (k q) -> p k q", q=P),
                       dram["mwo"][st, m, g * 8:g * 8 + 8]
                       .transpose([1, 0, 2]),
                       [("mwo", "*")], [DPW], eng="gpsimd", tag="m_wo_ld")
                for t in range(NT4):
                    pap, pnm, pk = psl(kb, (m % 2) * 4 + t)
                    for j in range(8):
                        k = g * 8 + j
                        uap, urd = uslab(k)
                        kb.mm(pap, wcb[:, j * P:(j + 1) * P],
                              uap[:, 3 + t * 512:3 + (t + 1) * 512],
                              [DPW, urd], [(pnm, pk)], start=(k == 0),
                              stop=(k == 15), tag="m_wo_mm")
                    if g == 1:
                        kb.stt(vf[:, m, t * 512:(t + 1) * 512], pap,
                               mask_col, vf[:, m, t * 512:(t + 1) * 512],
                               ALU.mult, ALU.add,
                               [(pnm, pk), AUX, ("vf", f"c{m}")],
                               [("vf", f"c{m}")], tag="m_resid")

        if st < 4:
            kb.ts(scf[:, 560:864], scf[:, 560:864], NPAR, ALU.mult,
                  [SCA, AUX], [SCA], tag="m_ccscale")
            kb.dma(dram["cc5i"], scf[:, 560:864], [SCA], [("cc5i", "*")],
                   eng="scalar", tag="m_ccst")
            kb.cc(kb.dram_h["cc5i"], kb.dram_h["cc5o"], [("cc5i", "*")],
                  [("cc5o", "*")], tag="m_cc")

        kb.tap(f"l1_y{st}", vf[:, :, 0:TK],
               [("vf", f"c{c}") for c in range(DC)])
        if stop_after == f"l1_y{st}":
            return

    for c in range(DC):
        kb.dma(xr[:, c, :], dram["x_spill"][c], [("x_spill", f"c{c}")],
               [("xr", f"c{c}")], tag="m_xrel")
        kb.tt(xr[:, c, :], xr[:, c, :], vf[:, c, 0:TK], ALU.add,
              [("xr", f"c{c}"), ("vf", f"c{c}")], [("xr", f"c{c}")],
              tag="m_xres")
    kb.tap("l1attn_out", xr[:, :, :], [("xr", f"c{c}") for c in range(DC)])



def build_l1_ffn(kb, stop_after=None):
    T = kb.tensors
    dram = kb.dram
    xr = T["xr"]
    bq, bk, vf, bo = T["bq"], T["bk"], T["vf"], T["bo"]
    sc_a, chsc = T["sc_a"], T["chsc"]
    ident = T["identity"]
    ones_blk = T["ones_blk"]
    SCA = ("sc_a", "x")
    CHS = ("chsc", "x")
    BK3 = ("bk", "c3")
    eps_ap, eps_rd = kb.cconst(1e-5)

    for c in range(DC):
        kb.copy(sc_a[:, :], xr[:, c, :], [("xr", f"c{c}")], [SCA],
                eng="scalar", tag="e_ln_cast")
        for t in range(NT4):
            pap, pnm, pk = psl(kb, t)
            kb.mm(pap, ones_blk[:, :], sc_a[:, t * 512:(t + 1) * 512],
                  [("ones_blk", "*"), SCA], [(pnm, pk)],
                  start=(c == 0), stop=(c == DC - 1), tag="e_ln_sum")
        kb.act(chsc[:, :], xr[:, c, :], AF.Square, [("xr", f"c{c}")], [CHS],
               tag="e_ln_sq")
        for t in range(NT4):
            pap, pnm, pk = psl(kb, 4 + t)
            kb.mm(pap, ones_blk[:, :], chsc[:, t * 512:(t + 1) * 512],
                  [("ones_blk", "*"), CHS], [(pnm, pk)],
                  start=(c == 0), stop=(c == DC - 1), tag="e_ln_sqs")
    chf = chsc.ap().bitcast(F32)
    for t in range(NT4):
        pap, pnm, pk = psl(kb, t)
        pbq, pnm2, pk2 = psl(kb, 4 + t)
        kb.ts(pap, pap, 1.0 / D, ALU.mult, [(pnm, pk)], [(pnm, pk)],
              tag="e_ln_mu")
        kb.act(chf[:, 0:512], pap, AF.Square, [(pnm, pk)], [CHS],
               tag="e_ln_musq")
        kb.stt(pbq, pbq, 1.0 / D, chf[:, 0:512], ALU.mult, ALU.subtract,
               [(pnm2, pk2), CHS], [(pnm2, pk2)], tag="e_ln_var")
        kb.act(pbq, pbq, AF.Sqrt, [(pnm2, pk2), eps_rd], [(pnm2, pk2)],
               bias=eps_ap, tag="e_ln_sqrt")
        kb.recip(pbq, pbq, [(pnm2, pk2)], [(pnm2, pk2)], tag="e_ln_rcp")
    for c in range(DC):
        for t in range(NT4):
            ts_ = slice(t * 512, (t + 1) * 512)
            pap, pnm, pk = psl(kb, t)
            pbq, pnm2, pk2 = psl(kb, 4 + t)
            kb.tt(chsc[:, ts_], xr[:, c, ts_], pap, ALU.subtract,
                  [("xr", f"c{c}"), (pnm, pk)], [CHS], tag="e_ln_sub")
            kb.tt(bo[:, c, ts_], chsc[:, ts_], pbq, ALU.mult,
                  [CHS, (pnm2, pk2)], [("bo", f"c{c}")], tag="e_ln_scl")
    kb.tap("e_ln", bo[:, :, :], [("bo", f"c{c}") for c in range(DC)])
    if stop_after == "e_ln":
        return

    dpv = T["dpt"]
    wld2 = dpv[:, 1152:1792].bitcast(F32)
    wcb = dpv[:, 0:1152]
    DPW = ("dpt", "*")
    kb.dma(wcb[:, 0:64].rearrange("p (k q) -> p k q", q=8),
           dram["erw"].transpose([1, 0, 2]), [("erw", "*")], [DPW],
           eng="gpsimd", tag="e_rw_ld")
    kb.dma(wcb[:, 64:72], dram["esgw"], [("esgw", "*")], [DPW],
           eng="gpsimd", tag="e_sgw_ld")
    for t in range(NT4):
        pap, pnm, pk = psl(kb, t)
        for k in range(DC):
            kb.mm(pap[0:8, :], wcb[:, k * 8:(k + 1) * 8],
                  bo[:, k, t * 512:(t + 1) * 512],
                  [DPW, ("bo", f"c{k}")], [(pnm, pk)],
                  start=(k == 0), stop=(k == DC - 1), tag="e_rt_mm")
        kb.act(bk[0:8, 1, t * 512:(t + 1) * 512], pap[0:8, :], AF.Copy,
               [(pnm, pk)], [("bk", "c1")], tag="e_rt_ev")
    bkf = chsc.ap().bitcast(F32)
    q0 = 0
    q_ = bkf[:, q0:q0 + 8]
    q2 = bkf[:, q0 + 8:q0 + 16]
    cnd = bkf[:, q0 + 16:q0 + 24]
    mk1 = bkf[:, q0 + 24:q0 + 32]
    mk2 = bkf[:, q0 + 32:q0 + 40]
    mcol = bkf[:, q0 + 40:q0 + 41]
    icol = bkf[:, q0 + 41:q0 + 42]
    it8 = bkf[:, q0 + 42:q0 + 50]
    IT8 = CHS
    BK3 = CHS
    combb = chsc[:, 200:208]
    for j in range(8):
        kb.memset(it8[:, j:j + 1], float(j), [BK3], tag="e_iota")
    for ti in range(16):
        cs = slice(ti * P, (ti + 1) * P)
        pap, pnm, pk = psl(kb, 4 + ti % 2)
        kb.prog.op("tensor",
                   lambda e, o=pap.bitcast(BF16)[:, 0:8], i=bk[0:8, 1, cs],
                   idn=ident[0:8, 0:8]: e.transpose(o, i, idn),
                   [("bk", "c1"), ("identity", "*")], [(pnm, pk)],
                   tag="e_rt_tr")
        kb.reduce(mcol, pap.bitcast(BF16)[:, 0:8], ALU.max, [(pnm, pk)],
                  [BK3], tag="e_sm_max")
        kb.ts(icol, mcol, -1.0, ALU.mult, [BK3], [BK3], tag="e_sm_neg")
        kb.act(q_, pap.bitcast(BF16)[:, 0:8], AF.Exp, [(pnm, pk), BK3],
               [BK3], bias=icol, tag="e_sm_exp")
        kb.reduce(mcol, q_, ALU.add, [BK3], [BK3], tag="e_sm_sum")
        kb.recip(mcol, mcol, [BK3], [BK3], tag="e_sm_rcp")
        kb.ts(q_, q_, mcol, ALU.mult, [BK3], [BK3], tag="e_sm_div")
        kb.reduce(mcol, q_, ALU.max, [BK3], [BK3], tag="e_t1_max")
        kb.ts(cnd, q_, mcol, ALU.is_equal, [BK3], [BK3], tag="e_t1_eq")
        kb.ts(cnd, cnd, -99.0, ALU.mult, [BK3], [BK3], s2=99.0, op1=ALU.add,
              tag="e_t1_pen")
        kb.tt(cnd, cnd, it8, ALU.add, [BK3, IT8], [BK3], tag="e_t1_sel")
        kb.reduce(icol, cnd, ALU.min, [BK3], [BK3], tag="e_t1_idx")
        kb.ts(mk1, it8, icol, ALU.is_equal, [IT8, BK3], [BK3],
              tag="e_t1_msk")
        kb.stt(q2, mk1, -1e9, q_, ALU.mult, ALU.add, [BK3], [BK3],
               tag="e_t2_q")
        kb.reduce(mcol, q2, ALU.max, [BK3], [BK3], tag="e_t2_max")
        kb.ts(cnd, q2, mcol, ALU.is_equal, [BK3], [BK3], tag="e_t2_eq")
        kb.ts(cnd, cnd, -99.0, ALU.mult, [BK3], [BK3], s2=99.0, op1=ALU.add,
              tag="e_t2_pen")
        kb.tt(cnd, cnd, it8, ALU.add, [BK3, IT8], [BK3], tag="e_t2_sel")
        kb.reduce(icol, cnd, ALU.min, [BK3], [BK3], tag="e_t2_idx")
        kb.ts(mk2, it8, icol, ALU.is_equal, [IT8, BK3], [BK3],
              tag="e_t2_msk")
        kb.tt(mk1, mk1, mk2, ALU.add, [BK3], [BK3], tag="e_cb_m")
        kb.tt(combb, q_, mk1, ALU.mult, [BK3], [BK3], tag="e_cb")
        pap2, pnm2, pk2 = psl(kb, 6 + ti % 2)
        kb.transpose(pap2.bitcast(BF16)[0:8, 0:128], combb, [BK3],
                     [(pnm2, pk2)], tag="e_cb_tr")
        kb.act(sc_a[0:8, cs], pap2.bitcast(BF16)[0:8, 0:128], AF.Copy,
               [(pnm2, pk2)], [SCA], tag="e_cb_ev")
    kb.tap("e_comb", sc_a[:, :], [SCA])
    if stop_after == "e_comb":
        return

    for t in range(NT4):
        pap, pnm, pk = psl(kb, t)
        for k in range(DC):
            kb.mm(pap[0:1, :], wcb[:, 64 + k:65 + k],
                  bo[:, k, t * 512:(t + 1) * 512],
                  [DPW, ("bo", f"c{k}")], [(pnm, pk)],
                  start=(k == 0), stop=(k == DC - 1), tag="e_sg_mm")
        kb.act(bk[0:1, 1, t * 512:(t + 1) * 512], pap[0:1, :], AF.Sigmoid,
               [(pnm, pk)], [("bk", "c1")], tag="e_sg_ev")
    for t in range(NT4):
        pap, pnm, pk = psl(kb, 4 + t)
        kb.mm(pap, ones_blk[0:1, :], bk[0:1, 1, t * 512:(t + 1) * 512],
              [("ones_blk", "*"), ("bk", "c1")], [(pnm, pk)], tag="e_sg_bc")
        kb.act(bk[:, 2, t * 512:(t + 1) * 512], pap, AF.Copy, [(pnm, pk)],
               [("bk", "c2")], tag="e_sg_bcev")

    def sgu_evac2(m, t, pap, prd):
        if m < 8:
            kb.act(bq[:, m, t * 512:(t + 1) * 512], pap, AF.Silu, [prd],
                   [("bq", f"c{m}")], tag="e_gate_ev")
        elif m < 12:
            kb.act(bk[:, 4 + (m - 8), t * 512:(t + 1) * 512], pap, AF.Copy,
                   [prd], [("bk", f"c{4 + (m - 8)}")], tag="e_up_ev")
        else:
            kb.act(vf[:, m - 12, t * 512:(t + 1) * 512], pap, AF.Copy,
                   [prd], [("vf", f"c{m - 12}")], tag="e_up_ev2")

    stream_proj(kb, "esgu", dram["esgu"], DC, 16,
                lambda k, t: (bo[:, k, t * 512:(t + 1) * 512],
                              ("bo", f"c{k}")),
                sgu_evac2, tag="e_sgu")
    for m in range(8):
        if m < 4:
            uap, urd = bk[:, 4 + m, 0:TK], ("bk", f"c{4 + m}")
        else:
            uap, urd = vf[:, m - 4, 0:TK], ("vf", f"c{m - 4}")
        kb.tt(bq[:, m, 0:TK], bq[:, m, 0:TK], uap, ALU.mult,
              [("bq", f"c{m}"), urd], [("bq", f"c{m}")], tag="e_sh")
    def sd_evac(m, t, pap, prd):
        kb.act(chsc[:, t * 512:(t + 1) * 512], pap, AF.Copy, [prd], [CHS],
               tag="e_sd_ev")
        kb.tt(vf[:, m, t * 512:(t + 1) * 512],
              chsc[:, t * 512:(t + 1) * 512],
              bk[:, 2, t * 512:(t + 1) * 512], ALU.mult,
              [CHS, ("bk", "c2")], [("vf", f"c{m}")], tag="e_sd_scl")

    stream_proj(kb, "esd", dram["esd"], DC, DC,
                lambda k, t: (bq[:, k, t * 512:(t + 1) * 512],
                              ("bq", f"c{k}")),
                sd_evac, tag="e_sd")
    kb.tap("e_shared", vf[:, :, 0:TK], [("vf", f"c{c}") for c in range(DC)])
    if stop_after == "e_shared":
        return

    kb.dma(chsc[0:16, :], dram["sel16"], [("sel16", "*")], [CHS],
           tag="e_selld")
    for e in range(8):
        for t in range(NT4):
            pap, pnm, pk = psl(kb, 4 + t)
            kb.mm(pap, chsc[0:8, e * P:(e + 1) * P],
                  sc_a[0:8, t * 512:(t + 1) * 512],
                  [CHS, SCA], [(pnm, pk)], tag="e_cbc_mm")
            kb.act(bk[:, 0, t * 512:(t + 1) * 512], pap, AF.Copy,
                   [(pnm, pk)], [("bk", "c0")], tag="e_cbc_ev")

        def w1_evac(m, t, pap, prd, e=e):
            kb.act(bq[:, m, t * 512:(t + 1) * 512], pap, AF.Silu, [prd],
                   [("bq", f"c{m}")], tag="e_h_ev")

        stream_proj(kb, "ew1", dram["ew1"][e], DC, DC,
                    lambda k, t: (bo[:, k, t * 512:(t + 1) * 512],
                                  ("bo", f"c{k}")),
                    w1_evac, tag=f"e_w1_{e}")
        for m in range(8):
            kb.tt(bq[:, m, 0:TK], bq[:, m, 0:TK], bk[:, 0, 0:TK], ALU.mult,
                  [("bq", f"c{m}"), ("bk", "c0")], [("bq", f"c{m}")],
                  tag="e_hscl")

        def w2_evac(m, t, pap, prd, e=e):
            kb.tt(vf[:, m, t * 512:(t + 1) * 512],
                  vf[:, m, t * 512:(t + 1) * 512], pap, ALU.add,
                  [("vf", f"c{m}"), prd], [("vf", f"c{m}")], tag="e_racc")

        stream_proj(kb, "ew2", dram["ew2"][e], DC, DC,
                    lambda k, t: (bq[:, k, t * 512:(t + 1) * 512],
                                  ("bq", f"c{k}")),
                    w2_evac, tag=f"e_w2_{e}")
    kb.tap("e_racc", vf[:, :, 0:TK], [("vf", f"c{c}") for c in range(DC)])

    for c in range(DC):
        kb.tt(xr[:, c, :], xr[:, c, :], vf[:, c, 0:TK], ALU.add,
              [("xr", f"c{c}"), ("vf", f"c{c}")], [("xr", f"c{c}")],
              tag="e_xres")
